# version 05
# baseline (speedup 1.0000x reference)
import numpy as np
import ml_dtypes
from contextlib import ExitStack

import concourse.bass as bass
import concourse.mybir as mybir
from concourse.bass import IndirectOffsetOnAxis
from concourse.bass_utils import run_bass_kernel_spmd

F32 = mybir.dt.float32
BF16 = mybir.dt.bfloat16
I32 = mybir.dt.int32
U8 = mybir.dt.uint8
AF = mybir.ActivationFunctionType
ALU = mybir.AluOpType
AX = mybir.AxisListType
BF = ml_dtypes.bfloat16

D = 1024
H = 16
HD = 64
DI = 2048
DS = 16
NE = 8
FF = 2048
KD = D // 128
NEG = -1.0e30
DBG = {}
DEBUG_DUMPS = False
BIG = float(1 << 19)
POLY = (-0.037, 0.481, -2.3, 4.917, -1.791)
LN_EPS = 1e-5
GROUPS = [[0, 1, 2, 3], [4, 5, 6, 7]]


def TS(i, n=128):
    return slice(i * n, (i + 1) * n)


class Plan:

    def __init__(self, nc):
        self.nc = nc
        self.streams = {k: [] for k in "tvsgy"}
        self.es = ExitStack()
        self.sems = {}
        self.cnt = {}
        self.n_coll = 0
        self.nbar = 0
        self.bregs = {}

    def sem(self, name):
        if name not in self.sems:
            self.sems[name] = self.es.enter_context(self.nc.semaphore(name))
            self.cnt[name] = 0
        return self.sems[name]

    def op(self, eng, fn, inc=None, k=1, wait=()):
        if eng == "v":
            wait = tuple(wait) + (self.w("vv"),)
            if inc is None:
                inc = "vv"
        elif eng == "s":
            wait = tuple(wait) + (self.w("ss"),)
            if inc is None:
                inc = "ss"
        sem_h = self.sem(inc) if inc else None
        waits = [(self.sem(s), v) for (s, v) in wait if v > 0]
        import traceback
        stk = traceback.extract_stack()
        site = stk[-3] if len(stk) >= 3 else stk[0]

        def thunk(e, fn=fn, sem_h=sem_h, k=k, waits=waits, site=site):
            for s, v in waits:
                e.wait_ge(s, v)
            try:
                ins = fn(e)
            except Exception:
                print(f"PLAN SITE: {site.filename}:{site.lineno} {site.line}")
                raise
            if sem_h is not None:
                ins.then_inc(sem_h, k)
        self.streams[eng].append(thunk)
        if inc:
            self.cnt[inc] += k
        return self.cnt[inc] if inc else None

    def val(self, name):
        return self.cnt.get(name, 0)

    def w(self, name):
        return (name, self.cnt.get(name, 0))

    def dma(self, eng, out, in_, inc, wait=()):
        return self.op(eng, lambda e, o=out, i=in_: e.dma_start(out=o, in_=i),
                       inc=inc, k=16, wait=wait)

    def idma(self, out, idx_ap, in_, inc, wait=(), scatter=False, add=False,
             bound=None):
        assert bound is not None
        if bound not in self.bregs:
            holder = {}
            self.bregs[bound] = holder

            def mkreg(g, holder=holder, val=bound):
                r = g.alloc_register(f"bnd{val}")
                g.reg_mov(r, val)
                holder["r"] = r
                return None
            self.streams["g"].append(lambda e, f=mkreg: f(e))
        holder = self.bregs[bound]

        def fn(g, out=out, idx_ap=idx_ap, in_=in_, holder=holder):
            bound_r = holder["r"]
            if scatter:
                return g.indirect_dma_start(
                    out=out, out_offset=IndirectOffsetOnAxis(idx_ap, 0),
                    in_=in_, in_offset=None, bounds_check=bound_r,
                    oob_is_err=False,
                    compute_op=(ALU.add if add else ALU.bypass))
            return g.indirect_dma_start(
                out=out, out_offset=None, in_=in_,
                in_offset=IndirectOffsetOnAxis(idx_ap, 0), bounds_check=bound_r,
                oob_is_err=False,
                compute_op=(ALU.add if add else ALU.bypass))
        return self.op("g", fn, inc=inc, k=16, wait=wait)

    def coll(self, ins_, outs_, wait=()):
        name = f"cc{self.n_coll}"
        self.n_coll += 1
        self.op("g", lambda g, i=ins_, o=outs_: g.collective_compute(
            "AllGather", ALU.bypass, replica_groups=GROUPS, ins=[i.opt()],
            outs=[o.opt()]), inc=name, k=1, wait=wait)
        return (name, 1)

    def barrier(self, engines="tvsg"):
        self.nbar += 1
        vals = {}
        for e in engines:
            vals[e] = self.op(e, lambda x: x.nop(), inc=f"bar_{e}", k=1)
        for e in engines:
            ww = tuple((f"bar_{o}", vals[o]) for o in engines if o != e)
            self.op(e, lambda x: x.nop(), wait=ww)
        if not hasattr(self, "cuts"):
            self.cuts = []
        self.cuts.append({k: len(v) for k, v in self.streams.items()})

    def emit(self, block):
        import os
        lim = int(os.environ.get("KLIM", "0"))
        kcut = int(os.environ.get("KCUT", "-1"))
        eng_map = {"t": block.tensor, "v": block.vector, "s": block.scalar,
                   "g": block.gpsimd, "y": block.sync}
        if kcut >= 0:
            print(f"KCUT={kcut} of {len(self.cuts)} barriers")
            cut = self.cuts[min(kcut, len(self.cuts) - 1)]
        for k in "tvsgy":
            thunks = self.streams[k]
            if kcut >= 0:
                thunks = thunks[:cut[k]]
            if lim and k == "g":
                thunks = thunks[:lim]

            def body(e, thunks=thunks):
                for th in thunks:
                    th(e)
            eng_map[k](body)


def build_program(T):
    DBG.clear()
    Q = T // 4
    NT = Q // 128
    QT = T // 128
    NC5 = Q // 512
    QC5 = T // 512
    CAP = T * 2 // NE
    CT = CAP // 128
    CC5 = CAP // 512
    DSL = DI // 4
    NPAN = (DSL * DS) // 128
    NDT = DSL // 128
    GR = D + NE
    NROW = 4 * 3 * H * HD

    nc = bass.Bass("TRN2", target_bir_lowering=False)
    dram = {}

    def din(name, shape, dt=F32):
        dram[name] = nc.dram_tensor(name, list(shape), dt, kind="ExternalInput").ap()
        return dram[name]

    def dint(name, shape, dt=F32):
        dram[name] = nc.dram_tensor(name, list(shape), dt).ap()
        return dram[name]

    xin = din("xin", [Q, D])
    wqT = din("wqT", [D, D]); wkT = din("wkT", [D, D]); wvT = din("wvT", [D, D])
    woT = din("woT", [D, D]); bo_d = din("bo", [1, D])
    choiceT = din("choiceT", [D, NE])
    w1my = din("w1my", [2, D, FF])
    w2my = din("w2my", [2, FF, D])
    inpw = din("inpw", [D, 2 * DI])
    convw = din("convw", [128, 4 * (DI // 128)])
    convb = din("convb", [128, DI // 128])
    xpw = din("xpw", [DI, 2 * DS + 1])
    wdtT = din("wdtT", [1, DSL])
    bdt = din("bdt", [128, NDT])
    alog = din("alog", [128, NPAN])
    dpar = din("dpar", [128, DI // 128])
    cpw = din("cpw", [DI, D])
    identb = din("identb", [128, 128], BF16)
    identf = din("identf", [128, 128], F32)
    onesb = din("onesb", [128, 128], BF16)
    ltrib = din("ltrib", [128, 128], BF16)
    masku = din("masku", [128, 128], U8)
    maskl = din("maskl", [128, 128], U8)
    sel16 = din("sel16", [48, 128], BF16)
    sel8 = din("sel8", [128, 16 * 128], BF16)
    selY = din("selY", [128, 16 * 128], BF16)
    eqsB = din("eqsB", [32, 8])
    eqbc = din("eqbc", [8, 32])
    eqtri = din("eqtri", [32, 32])
    selmy = din("selmy", [32, 8])
    mygate = din("mygate", [2, 128, NE])
    qkvgidx = din("qkvgidx", [64, 48], I32)
    aogidx = din("aogidx", [128, 8], I32)
    pgidx = din("pgidx", [32, 1], I32)
    bcg = din("bcg", [16, 12], I32)
    xbgidx = din("xbgidx", [128, 16], I32)
    ygidx = din("ygidx", [128, 16], I32)
    halog = din("halog", [128, 16], I32)
    ebase = din("ebase", [1, 8])

    xout = nc.dram_tensor("xout", [Q, D], F32, kind="ExternalOutput").ap()

    qkv_send = dint("qkv_send", [3 * H * HD, Q], BF16)
    qkv_ag = dint("qkv_ag", [NROW, Q], BF16)
    ao_send = dint("ao_send", [4 * HD, T], BF16)
    ao_ag = dint("ao_ag", [16 * HD, T], BF16)
    pr_send = dint("pr_send", [NE, Q], F32)
    pr_ag = dint("pr_ag", [4 * NE, Q], F32)
    xl_send = dint("xl_send", [Q, GR], BF16)
    xl_ag = dint("xl_ag", [4 * Q, GR], BF16)
    comp = dint("comp", [2 * CAP, GR], BF16)
    y_send = dint("y_send", [2 * CAP, D], F32)
    y_ag = dint("y_ag", [8 * CAP, D], F32)
    xbpre_sp = dint("xbpre_sp", [DI, Q], F32)
    halo_send = dint("halo_send", [DI, 4], F32)
    halo_ag = dint("halo_ag", [4 * DI, 4], F32)
    xs_send = dint("xs_send", [2 * DS + 1, Q], F32)
    xs_ag = dint("xs_ag", [4 * (2 * DS + 1), Q], F32)
    xb_send = dint("xb_send", [DI, Q], F32)
    xb_ag = dint("xb_ag", [4 * DI, Q], F32)
    ys_send = dint("ys_send", [DSL, T], F32)
    ys_ag = dint("ys_ag", [4 * DSL, T], F32)
    z_spill = dint("z_spill", [DI, Q], F32)

    p = Plan(nc)
    es = p.es

    _nm = [0]

    def sbs(stack, shape, dt=F32, tag=None):
        _nm[0] += 1
        h = stack.enter_context(nc.sbuf_tensor(f"sb{_nm[0]}", list(shape), dt))
        if tag is not None:
            DBG.setdefault(tag, []).append(f"sb{_nm[0]}")
        return h

    def sb(name, shape, dt=F32):
        return es.enter_context(nc.sbuf_tensor(name, list(shape), dt))

    x_sb = sb("x_sb", [128, NT, D])
    cid = sb("cid", [128, 128], BF16)
    cidf = sb("cidf", [128, 128], F32)
    cones = sb("cones", [128, 128], BF16)
    conesf = sb("conesf", [1, 128])
    cltri = sb("cltri", [128, 128], BF16)
    cmu = sb("cmu", [128, 128], U8)
    cml = sb("cml", [128, 128], U8)
    cneg = sb("cneg", [128, 128], F32)
    ln_st = sb("ln_st", [128, 8])
    ceps = sb("ceps", [128, 2])
    ln_bt = sb("ln_bt", [128, D], BF16)
    w_a = sb("w_a", [128, KD, 128], BF16)
    w_b = sb("w_b", [128, KD, 128], BF16)
    bo_sb = sb("bo_sb", [1, D], BF16)
    idxg = sb("idxg", [64, 48], I32)
    idxa = sb("idxa", [128, 16], I32)
    miscf = sb("miscf", [128, 2048])
    miscb = sb("miscb", [128, 2048], BF16)
    misc2 = sb("misc2", [128, 1024])
    rowa = sb("rowa", [1, 64])
    rowb = sb("rowb", [1, 64])
    s16 = sb("s16", [48, 128], BF16)
    eqs_sb = sb("eqs_sb", [32, 8])
    eqb_sb = sb("eqb_sb", [8, 32])
    eqt_sb = sb("eqt_sb", [32, 32])
    selmy_sb = sb("selmy_sb", [32, 8])
    mygate_sb = sb("mygate_sb", [128, 2, NE])
    ebase_sb = sb("ebase_sb", [1, 8])
    smallw = sb("smallw", [128, 80])
    pA = es.enter_context(nc.psum_tensor("pA", [128, 2048], F32))
    pB = es.enter_context(nc.psum_tensor("pB", [128, 1024], F32))
    pT = es.enter_context(nc.psum_tensor("pT", [128, 1024], BF16))
    pTf = es.enter_context(nc.psum_tensor("pTf", [128, 512], F32))

    LD, YD, TT, VV, SS = "ld", "yd", "tt", "vv", "ss"

    for dst, srcd in ((cid, identb), (cidf, identf), (cones, onesb),
                      (cltri, ltrib), (cmu, masku), (cml, maskl)):
        p.dma("g", dst[:, :], srcd, LD)
    p.dma("g", s16[:, :], sel16, LD)
    p.dma("g", eqs_sb[:, :], eqsB, LD)
    p.dma("g", eqb_sb[:, :], eqbc, LD)
    p.dma("g", eqt_sb[:, :], eqtri, LD)
    p.dma("g", selmy_sb[:, :], selmy, LD)
    p.dma("g", mygate_sb[:, :, :], mygate.rearrange("a p c -> p a c"), LD)
    p.dma("g", idxg[:, :], qkvgidx, LD)
    p.dma("g", bo_sb[:, :], bo_d, LD)
    p.dma("g", ebase_sb[:, :], ebase, LD)
    for i in range(NT):
        p.dma("y", x_sb[:, i, :], xin[TS(i), :], YD)
    p.op("v", lambda v: v.memset(cneg[:, :], NEG), inc=VV)
    p.op("v", lambda v: v.memset(conesf[0:1, :], 1.0), inc=VV)
    p.op("v", lambda v: v.memset(ceps[:, 0:1], LN_EPS))
    p.op("v", lambda v: v.memset(ceps[:, 1:2], 1.0), inc=VV)
    p.op("g", lambda g: g.nop(), wait=(p.w(LD),))
    p.op("y", lambda y: y.nop(), wait=(p.w(YD),))
    p.barrier("tvsgy")

    if DEBUG_DUMPS:
        xd0 = nc.dram_tensor("xdump0", [Q, D], F32, kind="ExternalOutput").ap()
        for i in range(NT):
            p.dma("y", xd0[TS(i), :], x_sb[:, i, :], YD)
        p.op("y", lambda y: y.nop(), wait=(p.w(YD),))
        p.barrier("tvsgy")

    def emit_ln(xl, send=False):
        for i in range(NT):
            ydp = p.val(YD)
            p.op("v", lambda v, i=i: v.reduce_sum(ln_st[:, 0:1], x_sb[:, i, :], axis=AX.X), inc=VV)
            p.op("s", lambda s, i=i: s.activation(miscf[:, 0:D], x_sb[:, i, :], AF.Square,
                                                  accum_out=ln_st[:, 1:2]), inc=SS)
            p.op("v", lambda v: v.tensor_scalar(ln_st[:, 2:3], ln_st[:, 0:1], 1.0 / D, None, ALU.mult),
                 wait=(p.w(SS), p.w(VV)))
            p.op("v", lambda v: v.tensor_tensor(ln_st[:, 3:4], ln_st[:, 2:3], ln_st[:, 2:3], ALU.mult))
            p.op("v", lambda v: v.scalar_tensor_tensor(out=ln_st[:, 3:4], in0=ln_st[:, 1:2],
                                                       scalar=1.0 / D, in1=ln_st[:, 3:4],
                                                       op0=ALU.mult, op1=ALU.subtract), inc=VV)
            p.op("s", lambda s: s.activation(ln_st[:, 4:5], ln_st[:, 3:4], AF.Ln, bias=ceps[:, 0:1]),
                 wait=(p.w(VV),))
            p.op("s", lambda s: s.activation(ln_st[:, 4:5], ln_st[:, 4:5], AF.Exp, scale=-0.5), inc=SS)
            p.op("v", lambda v: v.tensor_tensor(ln_st[:, 5:6], ln_st[:, 2:3], ln_st[:, 4:5], ALU.mult),
                 wait=(p.w(SS),))
            p.op("v", lambda v: v.tensor_scalar(ln_st[:, 5:6], ln_st[:, 5:6], -1.0, None, ALU.mult), inc=VV)
            p.op("v", lambda v, i=i, ydp=ydp: v.tensor_scalar(
                ln_bt[:, :], x_sb[:, i, :], ln_st[:, 4:5], ln_st[:, 5:6],
                ALU.mult, ALU.add), inc=VV, wait=(p.w(VV), p.w(TT), (YD, ydp)))
            sv = p.val(VV)
            if send:
                p.dma("y", xl_send[TS(i), 0:D], ln_bt[:, :], YD, wait=((VV, sv),))
            for c in range(KD):
                p.op("t", lambda t, c=c: t.transpose(pT[:, TS(c % 8)], ln_bt[:, TS(c)],
                                                     cid[:, :]),
                     inc=TT, wait=((VV, sv), p.w(VV)))
                p.op("v", lambda v, dst=xl[:, c, TS(i)], c=c: v.tensor_copy(
                    dst, pT[:, TS(c % 8)]), inc=VV, wait=(p.w(TT),))
        p.barrier("tvsy" if send else "tvs")

    def emit_projT(wsrc, OD, out_sb, xl):
        for oc in range(OD // 128):
            buf = w_a if oc % 2 == 0 else w_b
            lv = p.dma("g", buf[:, :, :],
                       wsrc[:, TS(oc)].rearrange("(a p) m -> p a m", p=128), LD,
                       wait=(p.w(TT),))
            for tc in range(NC5):
                for kc in range(KD):
                    p.op("t", lambda t, buf=buf, kc=kc, tc=tc,
                         rr=xl[:, kc, TS(tc, 512)]: t.matmul(
                        pA[:, TS(tc % 2, 512)], buf[:, kc, :], rr,
                        start=(kc == 0), stop=(kc == KD - 1)),
                        inc=(TT if kc == KD - 1 else None),
                        wait=(((LD, lv), p.w(SS)) if kc == 0 else ()))
                p.op("s", lambda s, oc=oc, tc=tc: s.activation(
                    out_sb[:, oc, TS(tc, 512)], pA[:, TS(tc % 2, 512)], AF.Copy),
                    inc=SS, wait=(p.w(TT),))
        p.barrier("tvsg")

    sc0 = ExitStack()
    xlnT0 = sbs(sc0, [128, KD, Q], BF16)
    projs = sbs(sc0, [128, KD, Q], BF16)
    emit_ln(xlnT0)
    if DEBUG_DUMPS:
        dbg9 = nc.dram_tensor("attdbg9", [128, KD * Q], BF16,
                              kind="ExternalOutput").ap()
        p.dma("y", dbg9.rearrange("p (a m) -> p a m", a=KD), xlnT0[:, :, :], YD,
              wait=(p.w(VV),))
        p.op("y", lambda y: y.nop(), wait=(p.w(YD),))
        p.barrier("tvsgy")
    for si, wsrc in enumerate((wqT, wkT, wvT)):
        emit_projT(wsrc, D, projs, xlnT0)
        p.dma("y", qkv_send[TS(si, D), :].rearrange("(a p) m -> p a m", p=128),
              projs[:, :, :], YD, wait=(p.w(SS),))
        p.op("y", lambda y: y.nop(), wait=(p.w(YD),))
        p.barrier("tvsy")
    sc0.close()
    p.op("g", lambda g: g.nop(), wait=(p.w(YD),))
    w_qkv = p.coll(qkv_send[:, :], qkv_ag[:, :])

    if DEBUG_DUMPS:
        dbg7 = nc.dram_tensor("attdbg7", [3 * H * HD, Q], BF16,
                              kind="ExternalOutput").ap()
        dbg8 = nc.dram_tensor("attdbg8", [NROW, Q], BF16,
                              kind="ExternalOutput").ap()
        p.dma("g", dbg7[:, :], qkv_send[:, :], LD, wait=(w_qkv,))
        p.dma("g", dbg8[:, :], qkv_ag[:, :], LD)
        p.op("g", lambda g: g.nop(), wait=(p.w(LD),))
        p.barrier("tvsgy")

    sc1 = ExitStack()
    kT_h = sbs(sc1, [64, T], BF16)
    qT_h = sbs(sc1, [64, T], BF16)
    vqT_h = sbs(sc1, [64, T], BF16)
    v_to = sbs(sc1, [128, T // 128, 66], BF16)
    e1sc = sbs(sc1, [128, 2048])
    z1p = sbs(sc1, [128, 8 * QT])
    t1p = sbs(sc1, [128, 8 * QT])
    statA = sbs(sc1, [128, QT])
    statB = sbs(sc1, [128, QT])
    statC = sbs(sc1, [128, QT])
    beta = sbs(sc1, [128, QT])
    qcountf = sbs(sc1, [128, QT])
    maskq = sbs(sc1, [128, QT], U8)
    e2t = sbs(sc1, [128, 1024], BF16)
    av_f = sbs(sc1, [65, 512])
    rz2 = sbs(sc1, [128, 4])
    aost = sbs(sc1, [64, 512], BF16)

    p.op("g", lambda g: g.iota(idxa[:, 0:1], pattern=[[1, 1]], base=1,
                               channel_multiplier=1), inc="gg", k=1)
    p.op("v", lambda v: v.tensor_copy(statA[:, 0:1], idxa[:, 0:1]), inc=VV,
         wait=(p.w("gg"),))
    for qi in range(QT):
        p.op("v", lambda v, qi=qi: v.tensor_scalar(qcountf[:, qi:qi + 1], statA[:, 0:1],
                                                   float(128 * qi), None, ALU.add),
             wait=((p.w(VV),) if qi == 0 else ()))
    p.op("v", lambda v: v.memset(v_to[:, :, 64:66], 0.0))
    p.op("v", lambda v: v.memset(v_to[:, :, 64:65], 1.0), inc=VV)

    for hh in range(4):
        gsem = LD
        for si, dst in ((0, qT_h), (1, kT_h), (2, vqT_h)):
            for q4 in range(4):
                col = hh * 12 + si * 4 + q4
                p.idma(dst[0:64, TS(q4, Q)], idxg[0:64, col:col + 1],
                       qkv_ag[:, :], gsem,
                       wait=((w_qkv, p.w(VV), p.w(TT)) if (si == 0 and q4 == 0) else ()),
                       bound=NROW - 1)
        gv = p.val(gsem)
        for jt in range(T // 128):
            p.op("t", lambda t, jt=jt: t.transpose(pT[:, TS(jt % 8)][:, 0:64],
                                                   vqT_h[:, TS(jt)],
                                                   cid[0:64, 0:64]), inc=TT,
                 wait=((gsem, gv), p.w(VV)))
            p.op("v", lambda v, jt=jt: v.tensor_copy(v_to[:, jt, 0:64],
                                                     pT[:, TS(jt % 8)][:, 0:64]),
                 inc=VV, wait=(p.w(TT),))
        p.op("v", lambda v: v.memset(z1p[:, :], 0.0))
        p.op("v", lambda v: v.memset(t1p[:, :], 0.0), inc=VV)
        half_done = [p.val(VV), p.val(VV)]
        half_sdone = [p.val(SS), p.val(SS)]
        hidx = 0
        for qi in range(QT):
            jend = 128 * (qi + 1)
            for js in range(0, jend, 1024):
                wid = min(1024, jend - js)
                hh2 = hidx % 2
                hidx += 1
                base = hh2 * 1024
                for mm in range(0, wid, 512):
                    nn = min(512, wid - mm)
                    p.op("t", lambda t, qi=qi, js=js, mm=mm, nn=nn, base=base: t.matmul(
                        pA[:, base + mm: base + mm + nn], qT_h[:, TS(qi)],
                        kT_h[:, js + mm: js + mm + nn], start=True, stop=True),
                        inc=(TT if mm + 512 >= wid else None),
                        wait=(((VV, half_done[hh2]), (SS, half_sdone[hh2]))
                              if mm == 0 else ()))
                if js <= jend - 128 < js + 1024:
                    off = jend - 128 - js
                    p.op("v", lambda v, off=off, base=base: v.copy_predicated(
                        pA[:, base + off: base + off + 128], cmu[:, :], cneg[:, :]),
                        inc=VV, wait=(p.w(TT),))
                pcol = qi * 8 + js // 1024
                p.op("s", lambda s, wid=wid, pcol=pcol, base=base: s.activation(
                    e1sc[:, base:base + wid], pA[:, base:base + wid], AF.Exp,
                    scale=0.125, accum_out=z1p[:, pcol:pcol + 1]), inc=SS,
                    wait=(p.w(TT), p.w(VV)))
                half_sdone[hh2] = p.val(SS)
                p.op("v", lambda v, wid=wid, pcol=pcol, base=base: v.scalar_tensor_tensor(
                    out=miscf[:, base:base + wid], in0=pA[:, base:base + wid],
                    scalar=0.125, in1=e1sc[:, base:base + wid], op0=ALU.mult,
                    op1=ALU.mult, accum_out=t1p[:, pcol:pcol + 1]), inc=VV,
                    wait=(p.w(SS),))
                half_done[hh2] = p.val(VV)
        p.op("v", lambda v: v.reduce_sum(statA[:, :],
                                         z1p.ap().rearrange("p (a b) -> p a b", b=8), axis=AX.X),
             inc=VV, wait=(p.w(VV),))
        p.op("v", lambda v: v.reduce_sum(statB[:, :],
                                         t1p.ap().rearrange("p (a b) -> p a b", b=8), axis=AX.X), inc=VV)
        p.op("v", lambda v: v.reciprocal(statC[:, :], statA[:, :]), wait=(p.w(VV),))
        p.op("v", lambda v: v.tensor_tensor(statB[:, :], statB[:, :], statC[:, :], ALU.mult), inc=VV)
        p.op("s", lambda s: s.activation(statA[:, :], statA[:, :], AF.Ln), inc=SS, wait=(p.w(VV),))
        p.op("v", lambda v: v.tensor_sub(statA[:, :], statA[:, :], statB[:, :]), wait=(p.w(SS),))
        p.op("v", lambda v: v.tensor_scalar(statB[:, :], qcountf[:, :], 1e-9, None, ALU.mult))
        p.op("v", lambda v: v.tensor_sub(statA[:, :], statA[:, :], statB[:, :]))
        p.op("v", lambda v: v.tensor_scalar(statB[:, :], statA[:, :], POLY[0], None, ALU.mult))
        for ci, c in enumerate(POLY[1:]):
            p.op("v", lambda v, c=c: v.tensor_scalar(statB[:, :], statB[:, :], c, None, ALU.add))
            if ci < 3:
                p.op("v", lambda v: v.tensor_tensor(statB[:, :], statB[:, :], statA[:, :], ALU.mult))
        p.op("v", lambda v: v.tensor_scalar(statB[:, :], statB[:, :], 1.0, None, ALU.max))
        p.op("v", lambda v: v.tensor_scalar(maskq[:, :], statA[:, :], 0.5, None, ALU.is_gt))
        p.op("v", lambda v: v.memset(statC[:, :], 1.0))
        p.op("v", lambda v: v.select(beta[:, :], maskq[:, :], statB[:, :], statC[:, :]))
        p.op("v", lambda v: v.tensor_scalar(beta[:, :], beta[:, :], 0.125, None, ALU.mult), inc=VV)
        for qi in range(QT):
            p.op("t", lambda t, qi=qi: t.transpose(pT[:, 0:64], qT_h[:, TS(qi)],
                                                   cid[0:64, 0:64]), inc=TT,
                 wait=(p.w(VV),))
            p.op("v", lambda v, qi=qi: v.tensor_scalar(miscb[:, 0:64], pT[:, 0:64],
                                                       beta[:, qi:qi + 1], None, ALU.mult),
                 inc=VV, wait=(p.w(TT),))
            p.op("t", lambda t: t.transpose(pT[0:64, TS(1)], miscb[0:128, 0:64],
                                            cid[:, :]), inc=TT, wait=(p.w(VV),))
            p.op("v", lambda v, qi=qi: v.tensor_copy(vqT_h[0:64, TS(qi)], pT[0:64, TS(1)]),
                 inc=VV, wait=(p.w(TT),))
        p.barrier("tvs")
        for qc in range(QC5):
            npair = 4 * qc + 4
            exp_done = [p.val(SS), p.val(SS)]
            av_done = [p.val(TT), p.val(TT)]
            for ji in range(npair):
                off = max(0, 128 * ji - 512 * qc)
                nn = 512 - off
                slot = ji % 2
                p.op("t", lambda t, ji=ji, qc=qc, off=off, nn=nn, slot=slot: t.matmul(
                    pA[:, slot * 512 + off: slot * 512 + off + nn], kT_h[:, TS(ji)],
                    vqT_h[:, 512 * qc + off: 512 * qc + off + nn], start=True, stop=True),
                    inc=TT, wait=((SS, exp_done[slot]),))
                mv = None
                if ji >= 4 * qc:
                    p.op("v", lambda v, off=off, slot=slot: v.copy_predicated(
                        pA[:, slot * 512 + off: slot * 512 + off + 128], cml[:, :],
                        cneg[:, :]), inc=VV, wait=(p.w(TT),))
                    mv = p.val(VV)
                p.op("s", lambda s, off=off, nn=nn, slot=slot: s.activation(
                    e2t.ap().rearrange("p (a b) -> p a b", a=2)[:, slot, off:off + nn],
                    pA[:, slot * 512 + off: slot * 512 + off + nn], AF.Exp), inc=SS,
                    wait=(p.w(TT), (TT, av_done[slot])) + (((VV, mv),) if mv else ()))
                exp_done[slot] = p.val(SS)
                p.op("t", lambda t, ji=ji, off=off, nn=nn, npair=npair, slot=slot: t.matmul(
                    pB[0:65, off:off + nn], v_to[:, ji, 0:65],
                    e2t.ap().rearrange("p (a b) -> p a b", a=2)[:, slot, off:off + nn],
                    start=(ji == 0), stop=(ji == npair - 1)),
                    inc=TT, wait=(p.w(SS),))
                av_done[slot] = p.val(TT)
            ydao = p.val(YD)
            p.op("s", lambda s: s.activation(av_f[:, :], pB[0:65, 0:512], AF.Copy),
                 inc=SS, wait=(p.w(TT),))
            p.op("v", lambda v: v.nop(), wait=((YD, ydao),))
            for a in range(4):
                p.op("t", lambda t, a=a: t.transpose(pTf[:, TS(a)][:, 0:65],
                                                     av_f[:, TS(a)], cidf[0:65, 0:65]),
                     inc=TT, wait=(p.w(SS), p.w(VV)))
                tv = p.val(TT)
                p.op("v", lambda v, a=a: v.reciprocal(rz2[:, 0:1], pTf[:, TS(a)][:, 64:65]),
                     inc=VV, wait=((TT, tv),))
                p.op("v", lambda v, a=a: v.tensor_scalar(miscb[:, 0:64], pTf[:, TS(a)][:, 0:64],
                                                         rz2[:, 0:1], None, ALU.mult),
                     inc=VV, wait=(p.w(VV),))
                p.op("t", lambda t: t.transpose(pT[0:64, TS(2)], miscb[0:128, 0:64],
                                                cid[:, :]), inc=TT, wait=(p.w(VV),))
                p.op("v", lambda v, a=a: v.tensor_copy(
                    aost[0:64, TS(a)], pT[0:64, TS(2)]),
                    inc=VV, wait=(p.w(TT),))
            p.dma("y", ao_send[TS(hh, HD), qc * 512:(qc + 1) * 512], aost[0:64, :],
                  YD, wait=(p.w(VV),))
    if DEBUG_DUMPS:
        add0 = nc.dram_tensor("attdbg0", [64, T], F32, kind="ExternalOutput").ap()
        add1 = nc.dram_tensor("attdbg1", [64, T], F32, kind="ExternalOutput").ap()
        add2 = nc.dram_tensor("attdbg2", [128, 8 * QT], F32, kind="ExternalOutput").ap()
        add3 = nc.dram_tensor("attdbg3", [128, 8 * QT], F32, kind="ExternalOutput").ap()
        add4 = nc.dram_tensor("attdbg4", [128, QT], F32, kind="ExternalOutput").ap()
        add5 = nc.dram_tensor("attdbg5", [128, QT], F32, kind="ExternalOutput").ap()
        p.barrier("tvsgy")
        for blk in range(T // 2048):
            p.op("v", lambda v, blk=blk: v.tensor_copy(miscf[0:64, 0:2048],
                                                       qT_h[0:64, TS(blk, 2048)]), inc=VV)
            p.dma("y", add0[:, TS(blk, 2048)], miscf[0:64, 0:2048], YD, wait=(p.w(VV),))
            p.op("v", lambda v, blk=blk: v.tensor_copy(miscf[64:128, 0:2048],
                                                       kT_h[0:64, TS(blk, 2048)]), inc=VV)
            p.dma("y", add1[:, TS(blk, 2048)], miscf[64:128, 0:2048], YD, wait=(p.w(VV),))
            p.op("v", lambda v: v.nop(), inc=VV, wait=(p.w(YD),))
        p.dma("y", add2[:, :], z1p[:, :], YD)
        p.dma("y", add3[:, :], t1p[:, :], YD)
        p.dma("y", add4[:, :], statA[:, :], YD)
        p.dma("y", add5[:, :], beta[:, :], YD)
        p.op("y", lambda y: y.nop(), wait=(p.w(YD),))
        p.barrier("tvsgy")

    p.op("g", lambda g: g.nop(), wait=(p.w(YD),))
    p.barrier("tvsg")
    sc1.close()
    w_ao = p.coll(ao_send[:, :], ao_ag[:, :])

    if DEBUG_DUMPS:
        add6 = nc.dram_tensor("attdbg6", [4 * HD, T], F32, kind="ExternalOutput").ap()
        p.barrier("tvsgy")
        for blk in range(4 * HD // 128):
            for cc in range(T // 2048):
                lx = p.dma("g", miscb[:, 0:2048], ao_send[TS(blk), TS(cc, 2048)], LD,
                           wait=(p.w(VV),))
                p.op("v", lambda v: v.tensor_copy(miscf[:, 0:2048], miscb[:, 0:2048]),
                     inc=VV, wait=((LD, lx),))
                p.dma("y", add6[TS(blk), TS(cc, 2048)], miscf[:, 0:2048], YD,
                      wait=(p.w(VV),))
                p.op("v", lambda v: v.nop(), inc=VV, wait=(p.w(YD),))
        p.op("y", lambda y: y.nop(), wait=(p.w(YD),))
        p.barrier("tvsgy")

    sc1b = ExitStack()
    aoT = sbs(sc1b, [128, KD, Q], BF16)
    wop = sbs(sc1b, [128, KD, 512], BF16)
    p.dma("g", idxa[:, 0:8], aogidx, LD)
    lv = p.val(LD)
    gsem = LD
    for r in range(8):
        p.idma(aoT[:, r, :], idxa[:, r:r + 1],
               ao_ag.rearrange("r (m one) -> (r m) one", one=1)[:, :], gsem,
               wait=(w_ao, (LD, lv)), bound=16 * HD * T - 1)
    gv = p.val(gsem)
    for oc in range(2):
        lw = p.dma("g", wop[:, :, :],
                   woT[:, TS(oc, 512)].rearrange("(a p) m -> p a m", p=128), LD,
                   wait=(p.w(TT),))
        for i in range(NT):
            for kc in range(KD):
                p.op("t", lambda t, i=i, kc=kc: t.matmul(
                    pA[:, TS(i % 2, 512)], aoT[:, kc, TS(i)], wop[:, kc, :],
                    start=(kc == 0), stop=False),
                    wait=(((LD, lw), (gsem, gv), p.w(VV)) if kc == 0 else ()))
            p.op("t", lambda t, i=i, oc=oc: t.matmul(
                pA[:, TS(i % 2, 512)], cones[0:1, :], bo_sb[0:1, TS(oc, 512)],
                start=False, stop=True), inc=TT)
            p.op("v", lambda v, i=i, oc=oc: v.tensor_tensor(
                x_sb[:, i, TS(oc, 512)], x_sb[:, i, TS(oc, 512)], pA[:, TS(i % 2, 512)],
                ALU.add), inc=VV, wait=(p.w(TT),))
    p.barrier("tvsg")
    sc1b.close()

    if DEBUG_DUMPS:
        xd1 = nc.dram_tensor("xdump1", [Q, D], F32, kind="ExternalOutput").ap()
        p.barrier("tvsgy")
        for i in range(NT):
            p.dma("y", xd1[TS(i), :], x_sb[:, i, :], YD)
        p.op("y", lambda y: y.nop(), wait=(p.w(YD),))
        p.barrier("tvsgy")

    def emit_ffn():
        scf = ExitStack()
        oidx = sbs(scf, [128, NT * 8], I32, tag="oidx")
        rz2f = sbs(scf, [128, 4])
        mq8 = sbs(scf, [8, 2], U8)
        scfA = ExitStack()
        xlnT = sbs(scfA, [128, KD, Q], BF16)
        emit_ln(xlnT, send=True)
        probs = sbs(scfA, [128, NT, NE])
        gatecol = sbs(scfA, [128, NT, NE], BF16)
        prf = sbs(scfA, [32, Q], tag="prf")
        prfmy = sbs(scfA, [8, Q], tag="prfmy")
        bisv = sbs(scfA, [32, 16], tag="bisv")
        mfm = sbs(scfA, [32, Q], BF16, tag="mfm")
        mfm8 = sbs(scfA, [8, Q], BF16, tag="mfm8")
        mtok = sbs(scfA, [128, QT * 32], BF16, tag="mtok")
        ptok = sbs(scfA, [128, QT * 32], tag="ptok")
        mown = sbs(scfA, [128, NT * 8], BF16, tag="mown")
        pown = sbs(scfA, [128, NT * 8], tag="pown")
        rowz = sbs(scfA, [1, 2048], tag="rowz")
        vsc = sbs(scfA, [128, 2048])
        lw = p.dma("g", w_a[:, :, 0:8], choiceT.rearrange("(a p) m -> p a m", p=128), LD)
        for i in range(NT):
            for kc in range(KD):
                p.op("t", lambda t, i=i, kc=kc, ll=xlnT[:, kc, TS(i)]: t.matmul(
                    pTf[:, 0:8], ll, w_a[:, kc, 0:8],
                    start=(kc == 0), stop=(kc == KD - 1)),
                    inc=(TT if kc == KD - 1 else None),
                    wait=(((LD, lw), p.w(VV)) if kc == 0 else ()))
            p.op("v", lambda v: v.reduce_max(rz2f[:, 0:1], pTf[:, 0:8], axis=AX.X),
                 inc=VV, wait=(p.w(TT),))
            p.op("v", lambda v: v.tensor_scalar(rz2f[:, 1:2], rz2f[:, 0:1], -1.0, None, ALU.mult),
                 inc=VV, wait=(p.w(VV),))
            p.op("s", lambda s, i=i: s.activation(probs[:, i, :], pTf[:, 0:8], AF.Exp,
                                                  bias=rz2f[:, 1:2]), inc=SS, wait=(p.w(VV),))
            p.op("v", lambda v, i=i: v.reduce_sum(rz2f[:, 2:3], probs[:, i, :], axis=AX.X),
                 inc=VV, wait=(p.w(SS),))
            p.op("v", lambda v: v.reciprocal(rz2f[:, 3:4], rz2f[:, 2:3]), inc=VV,
                 wait=(p.w(VV),))
            p.op("v", lambda v, i=i: v.tensor_scalar(probs[:, i, :], probs[:, i, :],
                                                     rz2f[:, 3:4], None, ALU.mult), inc=VV,
                 wait=(p.w(VV),))
            p.op("v", lambda v, i=i: v.tensor_copy(gatecol[:, i, :], probs[:, i, :]), inc=VV)
        for i in range(NT):
            p.op("t", lambda t, i=i: t.transpose(pTf[0:8, 0:128], probs[:, i, :],
                                                 cidf[:, :]), inc=TT, wait=(p.w(VV),))
            p.op("v", lambda v, i=i: v.tensor_copy(prf[0:8, TS(i)], pTf[0:8, 0:128]),
                 inc=VV, wait=(p.w(TT),))
            p.dma("y", xl_send[TS(i), D:GR], gatecol[:, i, :], YD, wait=(p.w(VV),))
        p.dma("y", pr_send[:, :], prf[0:8, :], YD, wait=(p.w(VV),))
        p.op("g", lambda g: g.nop(), wait=(p.w(YD),))
        w_pr = p.coll(pr_send[:, :], pr_ag[:, :])
        w_xl = p.coll(xl_send[:, :], xl_ag[:, :])
        p.dma("g", idxa[0:32, 8:9], pgidx, LD)
        wgpr = p.idma(prf[0:32, :], idxa[0:32, 8:9], pr_ag[:, :], LD,
                      wait=(w_pr, p.w(LD), p.w(VV), p.w(TT)), bound=4 * NE - 1)
        for tc in range(NC5):
            p.op("t", lambda t, tc=tc: t.matmul(
                pTf[0:8, 0:512], selmy_sb[0:32, 0:8], prf[0:32, TS(tc, 512)],
                start=True, stop=True), inc=TT, wait=((LD, wgpr), p.w(VV)))
            p.op("v", lambda v, tc=tc: v.tensor_copy(prfmy[0:8, TS(tc, 512)], pTf[0:8, 0:512]),
                 inc=VV, wait=(p.w(TT),))
        p.op("v", lambda v: v.memset(bisv[0:8, 0:1], 0.0))
        p.op("v", lambda v: v.memset(bisv[0:8, 1:2], 1.0), inc=VV)
        for it in range(40):
            p.op("v", lambda v: v.tensor_tensor(bisv[0:8, 2:3], bisv[0:8, 0:1],
                                                bisv[0:8, 1:2], ALU.add))
            p.op("v", lambda v: v.tensor_scalar(bisv[0:8, 2:3], bisv[0:8, 2:3], 0.5,
                                                None, ALU.mult), inc=VV)
            p.op("t", lambda t: t.matmul(pTf[0:32, 0:1], eqb_sb[0:8, :], bisv[0:8, 2:3],
                                         start=True, stop=True), inc=TT, wait=(p.w(VV),))
            p.op("v", lambda v: v.tensor_copy(bisv[0:32, 3:4], pTf[0:32, 0:1]), inc=VV,
                 wait=(p.w(TT),))
            p.op("v", lambda v: v.tensor_scalar(miscf[0:32, 0:Q], prf[0:32, :],
                                                bisv[0:32, 3:4], 0.0, ALU.is_gt, ALU.add,
                                                accum_out=bisv[0:32, 4:5]), inc=VV,
                 wait=(p.w(VV),))
            p.op("t", lambda t: t.matmul(pTf[0:8, 1:2], eqs_sb[0:32, :], bisv[0:32, 4:5],
                                         start=True, stop=True), inc=TT, wait=(p.w(VV),))
            p.op("v", lambda v: v.tensor_scalar(mq8[0:8, 0:1], pTf[0:8, 1:2],
                                                float(CAP) - 0.5, None, ALU.is_ge),
                 inc=VV, wait=(p.w(TT),))
            p.op("v", lambda v: v.tensor_scalar(mq8[0:8, 1:2], pTf[0:8, 1:2],
                                                float(CAP) - 0.5, None, ALU.is_lt))
            p.op("v", lambda v: v.select(bisv[0:8, 0:1], mq8[0:8, 0:1], bisv[0:8, 2:3],
                                         bisv[0:8, 0:1]))
            p.op("v", lambda v: v.select(bisv[0:8, 1:2], mq8[0:8, 1:2], bisv[0:8, 2:3],
                                         bisv[0:8, 1:2]), inc=VV)
        p.op("t", lambda t: t.matmul(pTf[0:32, 0:1], eqb_sb[0:8, :], bisv[0:8, 0:1],
                                     start=True, stop=True), inc=TT, wait=(p.w(VV),))
        p.op("v", lambda v: v.tensor_copy(bisv[0:32, 3:4], pTf[0:32, 0:1]), inc=VV,
             wait=(p.w(TT),))
        p.op("v", lambda v: v.tensor_scalar(mfm[0:32, :], prf[0:32, :], bisv[0:32, 3:4],
                                            0.0, ALU.is_gt, ALU.add,
                                            accum_out=bisv[0:32, 4:5]), inc=VV,
             wait=(p.w(VV),))
        p.op("t", lambda t: t.matmul(pTf[0:32, 2:3], eqt_sb[0:32, :], bisv[0:32, 4:5],
                                     start=True, stop=True), inc=TT, wait=(p.w(VV),))
        p.op("v", lambda v: v.tensor_copy(bisv[0:32, 5:6], pTf[0:32, 2:3]), inc=VV,
             wait=(p.w(TT),))
        p.op("t", lambda t: t.matmul(pTf[0:8, 3:4], selmy_sb[0:32, :], bisv[0:32, 3:4],
                                     start=True, stop=True), wait=(p.w(VV),))
        p.op("t", lambda t: t.matmul(pTf[0:8, 4:5], selmy_sb[0:32, :], bisv[0:32, 5:6],
                                     start=True, stop=True), inc=TT)
        p.op("v", lambda v: v.tensor_copy(bisv[0:8, 6:7], pTf[0:8, 3:4]), wait=(p.w(TT),))
        p.op("v", lambda v: v.tensor_copy(bisv[0:8, 7:8], pTf[0:8, 4:5]), inc=VV)
        p.op("v", lambda v: v.tensor_scalar(mfm8[0:8, :], prfmy[0:8, :], bisv[0:8, 6:7],
                                            0.0, ALU.is_gt, ALU.add,
                                            accum_out=bisv[0:8, 8:9]), inc=VV,
             wait=(p.w(VV),))
        p.dma("g", rowa[0:1, 0:32], bisv[0:32, 5:6], LD, wait=(p.w(VV),))
        p.dma("g", rowb[0:1, 0:8], bisv[0:8, 7:8], LD)
        lqoff = p.val(LD)

        for tb in range(QT):
            p.op("t", lambda t, tb=tb: t.transpose(pT[:, TS(tb % 8)][:, 0:32],
                                                   mfm[0:32, TS(tb % NT)],
                                                   cid[0:32, 0:32]), inc=TT,
                 wait=(p.w(VV),))
            p.op("v", lambda v, tb=tb: v.tensor_copy(mtok[:, TS(tb, 32)],
                                                     pT[:, TS(tb % 8)][:, 0:32]),
                 inc=VV, wait=(p.w(TT),))
        for mc in range(QT * 32 // 512):
            p.op("t", lambda t, mc=mc: t.matmul(pTf[:, 0:512], cltri[:, :],
                                                mtok[:, TS(mc, 512)], start=True,
                                                stop=True), inc=TT, wait=(p.w(VV),))
            p.op("v", lambda v, mc=mc: v.tensor_copy(ptok[:, TS(mc, 512)], pTf[:, 0:512]),
                 inc=VV, wait=(p.w(TT),))
        p.dma("g", rowz[0:1, 0:QT * 32], ptok[127:128, :], LD, wait=(p.w(VV),))
        lbs = p.val(LD)
        p.op("v", lambda v: v.memset(rowa[0:1, 32:64], 0.0), wait=((LD, lbs),))
        p.op("v", lambda v: v.tensor_copy(miscf[0:1, 0:QT * 32], rowz[0:1, 0:QT * 32]))
        rz3 = rowz.ap().rearrange("p (a b) -> p a b", b=32)
        nqb = QT // 4
        for j in range(2):
            for q in range(4):
                sl = rz3[0:1, q * nqb:(q + 1) * nqb, 4 * j + q:4 * j + q + 1]
                p.op("v", lambda v, sl=sl: v.tensor_copy(rowa[0:1, 48:48 + nqb], sl))
                p.op("v", lambda v: v.tensor_tensor_scan(
                    rowa[0:1, 48:48 + nqb], rowa[0:1, 48:48 + nqb],
                    rowa[0:1, 32:32 + nqb], 0.0, ALU.add, ALU.add), inc=VV)
                p.op("v", lambda v, sl=sl: v.tensor_copy(sl, rowa[0:1, 48:48 + nqb]),
                     wait=(p.w(VV),))
        p.op("v", lambda v: v.tensor_sub(rowz[0:1, 0:QT * 32], rowz[0:1, 0:QT * 32],
                                         miscf[0:1, 0:QT * 32]), inc=VV)
        p.op("v", lambda v: v.nop(), wait=((LD, lqoff),))
        for tb in range(QT):
            p.op("v", lambda v, tb=tb: v.tensor_tensor(rowz[0:1, TS(tb, 32)],
                                                       rowz[0:1, TS(tb, 32)],
                                                       rowa[0:1, 0:32], ALU.add))
        p.op("v", lambda v: v.tensor_scalar(rowz[0:1, 0:QT * 32], rowz[0:1, 0:QT * 32],
                                            -1.0, None, ALU.add), inc=VV)
        for mc in range(QT * 32 // 512):
            p.op("t", lambda t, mc=mc: t.matmul(pTf[:, 0:512], conesf[0:1, :],
                                                rowz[0:1, TS(mc, 512)],
                                                start=True, stop=True), inc=TT,
                 wait=(p.w(VV),))
            p.op("v", lambda v, mc=mc: v.tensor_tensor(ptok[:, TS(mc, 512)],
                                                       ptok[:, TS(mc, 512)],
                                                       pTf[:, 0:512], ALU.add), inc=VV,
                 wait=(p.w(TT),))
        CHW = min(2048, QT * 32)
        for mc in range(QT * 32 // CHW):
            sl = TS(mc, CHW)
            p.op("v", lambda v, sl=sl: v.tensor_scalar(miscf[:, 0:CHW], mtok[:, sl],
                                                       -BIG, None, ALU.mult))
            p.op("v", lambda v, sl=sl: v.tensor_scalar(miscf[:, 0:CHW], miscf[:, 0:CHW],
                                                       BIG, None, ALU.add))
            p.op("v", lambda v, sl=sl: v.tensor_scalar(vsc[:, 0:CHW], ptok[:, sl],
                                                       float(CAP) - 0.5, None, ALU.is_ge))
            p.op("v", lambda v, sl=sl: v.scalar_tensor_tensor(out=miscf[:, 0:CHW],
                                                              in0=vsc[:, 0:CHW],
                                                              scalar=BIG, in1=miscf[:, 0:CHW],
                                                              op0=ALU.mult, op1=ALU.add))
            p.op("v", lambda v, sl=sl: v.tensor_tensor(ptok[:, sl], ptok[:, sl],
                                                       miscf[:, 0:CHW], ALU.add), inc=VV)
        for i in range(NT):
            p.op("t", lambda t, i=i: t.transpose(pT[:, TS(i % 8)][:, 0:8],
                                                 mfm8[0:8, TS(i)],
                                                 cid[0:8, 0:8]), inc=TT, wait=(p.w(VV),))
            p.op("v", lambda v, i=i: v.tensor_copy(mown[:, TS(i, 8)],
                                                   pT[:, TS(i % 8)][:, 0:8]), inc=VV,
                 wait=(p.w(TT),))
        p.op("t", lambda t: t.matmul(pTf[:, 0:NT * 8], cltri[:, :], mown[:, :],
                                     start=True, stop=True), inc=TT, wait=(p.w(VV),))
        p.op("v", lambda v: v.tensor_copy(pown[:, :], pTf[:, 0:NT * 8]), inc=VV,
             wait=(p.w(TT),))
        p.dma("g", rowz[0:1, 0:NT * 8], pown[127:128, :], LD, wait=(p.w(VV),))
        lbs = p.val(LD)
        p.op("v", lambda v: v.tensor_copy(miscf[0:1, 0:NT * 8], rowz[0:1, 0:NT * 8]),
             wait=((LD, lbs),))
        rz8 = rowz.ap().rearrange("p (a b) -> p a b", b=8)
        for j in range(8):
            sl = rz8[0:1, 0:NT, j:j + 1]
            p.op("v", lambda v, sl=sl: v.tensor_copy(rowa[0:1, 48:48 + NT], sl))
            p.op("v", lambda v: v.tensor_tensor_scan(
                rowa[0:1, 48:48 + NT], rowa[0:1, 48:48 + NT],
                rowa[0:1, 32:32 + NT], 0.0, ALU.add, ALU.add), inc=VV)
            p.op("v", lambda v, sl=sl: v.tensor_copy(sl, rowa[0:1, 48:48 + NT]),
                 wait=(p.w(VV),))
        p.op("v", lambda v: v.tensor_sub(rowz[0:1, 0:NT * 8], rowz[0:1, 0:NT * 8],
                                         miscf[0:1, 0:NT * 8]))
        p.op("v", lambda v: v.tensor_scalar(rowb[0:1, 0:8], rowb[0:1, 0:8], -1.0,
                                            None, ALU.add), inc=VV)
        for i in range(NT):
            p.op("v", lambda v, i=i: v.tensor_tensor(rowz[0:1, TS(i, 8)],
                                                     rowz[0:1, TS(i, 8)],
                                                     rowb[0:1, 0:8], ALU.add))
        p.op("v", lambda v: v.nop(), inc=VV)
        p.op("t", lambda t: t.matmul(pTf[:, 0:NT * 8], conesf[0:1, :], rowz[0:1, 0:NT * 8],
                                     start=True, stop=True), inc=TT, wait=(p.w(VV),))
        p.op("v", lambda v: v.tensor_tensor(pown[:, :], pown[:, :], pTf[:, 0:NT * 8],
                                            ALU.add), inc=VV, wait=(p.w(TT),))
        p.op("v", lambda v: v.tensor_scalar(miscf[:, 0:NT * 8], mown[:, :], -BIG, None,
                                            ALU.mult))
        p.op("v", lambda v: v.tensor_scalar(miscf[:, 0:NT * 8], miscf[:, 0:NT * 8], BIG,
                                            None, ALU.add))
        p.op("v", lambda v: v.tensor_tensor(pown[:, :], pown[:, :], miscf[:, 0:NT * 8],
                                            ALU.add))
        p.op("v", lambda v: v.tensor_scalar(miscf[:, 0:NT * 8], pown[:, :],
                                            float(CAP) - 0.5, BIG, ALU.is_ge,
                                            ALU.mult))
        p.op("v", lambda v: v.tensor_tensor(pown[:, :], pown[:, :], miscf[:, 0:NT * 8],
                                            ALU.add), inc=VV)
        for i in range(NT):
            p.op("v", lambda v, i=i: v.tensor_copy(rowz[0:1, TS(i, 8)], ebase_sb[0:1, :]),
                 wait=((p.w(TT),) if i == 0 else ()))
        p.op("v", lambda v: v.nop(), inc=VV)
        p.op("t", lambda t: t.matmul(pTf[:, 0:NT * 8], conesf[0:1, :], rowz[0:1, 0:NT * 8],
                                     start=True, stop=True), inc=TT, wait=(p.w(VV),))
        p.op("v", lambda v: v.tensor_tensor(pown[:, :], pown[:, :], pTf[:, 0:NT * 8],
                                            ALU.add), wait=(p.w(TT),))
        p.op("v", lambda v: v.tensor_copy(oidx[:, :], pown[:, :]), inc=VV)

        ssc = LD
        vv_idx = p.val(VV)
        xstage = sbs(scfA, [128, 2, GR], BF16)
        idxs = sbs(scfA, [128, 2 * QT], I32, tag="idxs")
        stg_free = [p.val(LD), p.val(LD)]
        for tb in range(QT):
            sl = tb % 2
            p.dma("g", xstage[:, sl, :], xl_ag[TS(tb), :], LD,
                  wait=(w_xl, (LD, stg_free[sl])))
            lvs = p.val(LD)
            for le in range(2):
                q = tb // NT
                eqcol = tb * 32 + 4 * le + q
                ic = tb * 2 + le
                p.op("v", lambda v, eqcol=eqcol, le=le: v.tensor_scalar(
                    miscf[:, 0:1], ptok[:, eqcol:eqcol + 1], float(le * CAP), None,
                    ALU.add), wait=((VV, vv_idx),))
                p.op("v", lambda v, ic=ic: v.tensor_copy(idxs[:, ic:ic + 1],
                                                         miscf[:, 0:1]), inc=VV)
                p.idma(comp[:, :], idxs[:, ic:ic + 1], xstage[:, sl, :], ssc,
                       wait=((LD, lvs), p.w(VV)), scatter=True, bound=2 * CAP - 1)
            stg_free[sl] = p.val(LD)
        wsc = p.val(ssc)

        p.barrier("tvsg")
        scfA.close()
        sce = ExitStack()
        HCT = CT // 2
        xcT = sbs(sce, [128, KD, CAP // 2], BF16)
        hT = sbs(sce, [128, FF // 128, CAP // 2], BF16)
        wbig = sbs(sce, [128, 16384], BF16)
        wb1 = wbig.ap().rearrange("p (a m) -> p a m", a=KD)
        wb2 = wbig.ap().rearrange("p (a m) -> p a m", a=16)
        for le in range(2):
            for half in range(2):
                base = le * CAP + half * (CAP // 2)
                for ci in range(HCT):
                    p.dma("g", miscb[:, 0:D], comp[base + 128 * ci: base + 128 * (ci + 1), 0:D],
                          LD, wait=((ssc, wsc), p.w(TT)))
                    p.dma("g", miscb[:, D:D + NE],
                          comp[base + 128 * ci: base + 128 * (ci + 1), D:GR], LD)
                    lvv = p.val(LD)
                    p.op("v", lambda v, le=le: v.tensor_tensor(
                        miscf[:, 0:NE], miscb[:, D:D + NE], mygate_sb[:, le, :], ALU.mult),
                        wait=((LD, lvv),))
                    p.op("v", lambda v, ci=ci: v.reduce_sum(miscf[:, 100 + ci:101 + ci],
                                                            miscf[:, 0:NE], axis=AX.X), inc=VV)
                    for kc in range(KD):
                        p.op("t", lambda t, kc=kc: t.transpose(
                            pT[:, TS(kc % 8)], miscb[:, TS(kc)], cid[:, :]), inc=TT,
                            wait=(p.w(VV),))
                        p.op("v", lambda v, kc=kc, ci=ci: v.tensor_copy(
                            xcT[:, kc, TS(ci)], pT[:, TS(kc % 8)]), inc=VV, wait=(p.w(TT),))
                lw1 = p.dma("g", wb1, w1my[le].rearrange("(a p) m -> p a m", p=128), LD,
                            wait=(p.w(TT),))
                for fc in range(FF // 128):
                    for cc in range((CAP // 2) // 512):
                        for kc in range(KD):
                            p.op("t", lambda t, fc=fc, cc=cc, kc=kc: t.matmul(
                                pA[:, TS(cc % 2, 512)], wb1[:, kc, TS(fc)],
                                xcT[:, kc, TS(cc, 512)], start=(kc == 0), stop=(kc == KD - 1)),
                                inc=(TT if kc == KD - 1 else None),
                                wait=(((LD, lw1), p.w(SS), p.w(VV)) if kc == 0 else ()))
                        p.op("s", lambda s, fc=fc, cc=cc: s.activation(
                            hT[:, fc, TS(cc, 512)], pA[:, TS(cc % 2, 512)], AF.Silu),
                            inc=SS, wait=(p.w(TT),))
                lw2 = p.dma("g", wb2, w2my[le].rearrange("(a p) m -> p a m", p=128), LD,
                            wait=(p.w(TT), p.w(SS)))
                for ci in range(HCT):
                    ydp = p.val(YD)
                    for oc in range(2):
                        for kc in range(16):
                            p.op("t", lambda t, ci=ci, oc=oc, kc=kc: t.matmul(
                                pA[:, TS(oc % 2, 512)], hT[:, kc, TS(ci)],
                                wb2[:, kc, TS(oc, 512)], start=(kc == 0), stop=(kc == 15)),
                                inc=(TT if kc == 15 else None),
                                wait=(((LD, lw2), p.w(VV)) if kc == 0 else ()))
                        p.op("v", lambda v, ci=ci, oc=oc, ydp=ydp: v.tensor_scalar(
                            miscf[:, 1024 + oc * 512: 1024 + (oc + 1) * 512],
                            pA[:, TS(oc % 2, 512)], miscf[:, 100 + ci:101 + ci], None,
                            ALU.mult), inc=VV, wait=(p.w(TT), (YD, ydp)))
                    p.dma("y", y_send[base + 128 * ci: base + 128 * (ci + 1), :],
                          miscf[:, 1024:2048], YD, wait=(p.w(VV),))
        p.op("g", lambda g: g.nop(), wait=(p.w(YD),))
        sce.close()
        w_y = p.coll(y_send[:, :], y_ag[:, :])
        gadd = LD
        for i in range(NT):
            for j in range(8):
                col = i * 8 + j
                ww = ((w_y, p.w(VV), p.w(TT)) if (i == 0 and j == 0)
                      else ((gadd, p.val(gadd)),))
                p.idma(x_sb[:, i, :], oidx[:, col:col + 1], y_ag[:, :], gadd,
                       wait=ww, add=True, bound=8 * CAP - 1)
        p.op("g", lambda g: g.nop(), wait=((gadd, p.val(gadd)),))
        p.barrier("tvsg")
        scf.close()

    emit_ffn()

    if DEBUG_DUMPS:
        xd2 = nc.dram_tensor("xdump2", [Q, D], F32, kind="ExternalOutput").ap()
        p.barrier("tvsgy")
        for i in range(NT):
            p.dma("y", xd2[TS(i), :], x_sb[:, i, :], YD)
        p.op("y", lambda y: y.nop(), wait=(p.w(YD),))
        p.barrier("tvsgy")

    sc3w = ExitStack()
    wdt_sb = sbs(sc3w, [65, DSL], BF16)
    sc3 = ExitStack()
    xlnT3 = sbs(sc3, [128, KD, Q], BF16)
    emit_ln(xlnT3)
    xbq = sbs(sc3, [128, 2048])
    cpad = sbs(sc3, [128, 2051])
    xs_acc = sbs(sc3, [33, Q])
    xpw_sb = sbs(sc3, [128, 16, 33], BF16)
    halo_sb = sbs(sc3, [128, 16, 4])
    for oc in range(2 * DI // 128):
        buf = w_a if oc % 2 == 0 else w_b
        ydprev = p.val(YD)
        lw = p.dma("g", buf[:, :, :],
                   inpw[:, TS(oc)].rearrange("(a p) m -> p a m", p=128), LD,
                   wait=(p.w(TT),))
        for tc in range(NC5):
            for kc in range(KD):
                p.op("t", lambda t, buf=buf, kc=kc, tc=tc,
                     rr=xlnT3[:, kc, TS(tc, 512)]: t.matmul(
                    pA[:, TS(tc % 2, 512)], buf[:, kc, :], rr,
                    start=(kc == 0), stop=(kc == KD - 1)),
                    inc=(TT if kc == KD - 1 else None),
                    wait=(((LD, lw), p.w(SS)) if kc == 0 else ()))
            p.op("s", lambda s, tc=tc, ydprev=ydprev: s.activation(
                miscf[:, TS(tc, 512)], pA[:, TS(tc % 2, 512)], AF.Copy),
                inc=SS, wait=(p.w(TT),) + (((YD, ydprev),) if tc == 0 else ()))
        sv = p.val(SS)
        if oc < DI // 128:
            p.dma("y", xbpre_sp[TS(oc), :], miscf[:, 0:Q], YD, wait=((SS, sv),))
            p.dma("y", halo_send[TS(oc), 0:4], miscf[:, Q - 4:Q], YD)
        else:
            p.dma("y", z_spill[TS(oc - DI // 128), :], miscf[:, 0:Q], YD, wait=((SS, sv),))
    p.op("g", lambda g: g.nop(), wait=(p.w(YD),))
    w_halo = p.coll(halo_send[:, :], halo_ag[:, :])
    p.op("v", lambda v: v.memset(halo_sb[:, :, :], 0.0), inc=VV)
    p.dma("g", idxa[:, 0:16], halog, LD)
    p.idma(halo_sb[:, 0, :], idxa[:, 0:1], halo_ag[:, :], LD,
           wait=(w_halo, p.w(LD), p.w(VV)), bound=4 * DI - 1)
    for dtile in range(1, DI // 128):
        p.idma(halo_sb[:, dtile, :], idxa[:, dtile:dtile + 1], halo_ag[:, :],
               LD, bound=4 * DI - 1)
    whalo = p.val(LD)
    p.dma("g", smallw[:, 0:64], convw, LD)
    p.dma("g", smallw[:, 64:80], convb, LD)
    p.dma("g", xpw_sb[:, :, :], xpw.rearrange("(a p) m -> p a m", p=128), LD)
    p.dma("g", wdt_sb[64:65, :], wdtT, LD)
    lvw = p.val(LD)
    p.op("v", lambda v: v.memset(xs_acc[0:33, :], 0.0), inc=VV)
    for oc in range(DI // 128):
        ydprev = p.val(YD)
        lx = p.dma("g", cpad[:, 3:3 + Q], xbpre_sp[TS(oc), :], LD,
                   wait=(((LD, max(whalo, lvw)),) + (p.w(SS), p.w(VV))))
        p.op("v", lambda v, oc=oc: v.tensor_copy(cpad[:, 0:3], halo_sb[:, oc, 1:4]),
             wait=((LD, lx),))
        p.op("v", lambda v, oc=oc, ydprev=ydprev: v.tensor_scalar(
            xbq[:, 0:Q], cpad[:, 0:Q], smallw[:, 0 + oc:1 + oc], None, ALU.mult),
            wait=((YD, ydprev),))
        for k in range(1, 4):
            p.op("v", lambda v, oc=oc, k=k: v.scalar_tensor_tensor(
                out=xbq[:, 0:Q], in0=cpad[:, k:k + Q],
                scalar=smallw[:, 16 * k + oc:16 * k + oc + 1],
                in1=xbq[:, 0:Q], op0=ALU.mult, op1=ALU.add))
        p.op("v", lambda v: v.nop(), inc=VV)
        p.op("s", lambda s, oc=oc: s.activation(xbq[:, 0:Q], xbq[:, 0:Q], AF.Silu,
                                                bias=smallw[:, 64 + oc:65 + oc]),
             inc=SS, wait=(p.w(VV),))
        sv = p.val(SS)
        p.dma("y", xb_send[TS(oc), :], xbq[:, 0:Q], YD, wait=((SS, sv),))
        p.op("v", lambda v: v.tensor_copy(miscb[:, 0:Q], xbq[:, 0:Q]), inc=VV,
             wait=((SS, sv), p.w(TT)))
        for tc in range(NC5):
            p.op("t", lambda t, oc=oc, tc=tc: t.matmul(
                pTf[0:33, 0:512], xpw_sb[:, oc, :], miscb[:, TS(tc, 512)],
                start=True, stop=True), inc=TT, wait=(p.w(VV),))
            p.op("v", lambda v, tc=tc: v.tensor_tensor(xs_acc[0:33, TS(tc, 512)],
                                                       xs_acc[0:33, TS(tc, 512)],
                                                       pTf[0:33, 0:512], ALU.add),
                 inc=VV, wait=(p.w(TT),))
    p.dma("y", xs_send[:, :], xs_acc[0:33, :], YD, wait=(p.w(VV),))
    p.op("g", lambda g: g.nop(), wait=(p.w(YD),))
    p.barrier("tvsgy")
    sc3.close()
    w_xs = p.coll(xs_send[:, :], xs_ag[:, :])
    w_xb = p.coll(xb_send[:, :], xb_ag[:, :])

    sc3b = ExitStack()
    selp = sbs(sc3b, [128, 16, 128], BF16)
    selyp = sbs(sc3b, [128, 16, 128], BF16)
    p.dma("g", selp[:, :, :], sel8.rearrange("p (a c) -> p a c", a=16), LD)
    p.dma("g", selyp[:, :, :], selY.rearrange("p (a c) -> p a c", a=16), LD)
    bc16 = sbs(sc3b, [65, T], BF16)
    brepc = sbs(sc3b, [128, 2048], BF16)
    crepc = sbs(sc3b, [128, 2048], BF16)
    dtb16 = sbs(sc3b, [128, 2048], BF16)
    ub16 = sbs(sc3b, [128, 2048], BF16)
    xbsl = sbs(sc3b, [128, 2048])
    dtf = sbs(sc3b, [128, 512])
    a_sc = sbs(sc3b, [128, 1024])
    bx_sc = sbs(sc3b, [128, 1024])
    h_sc = sbs(sc3b, [128, 1024])
    yt_sc = sbs(sc3b, [128, 1024], BF16)
    hstate = sbs(sc3b, [128, NPAN])
    apan = sbs(sc3b, [128, NPAN])
    ysl = sbs(sc3b, [128, 2048])

    p.dma("g", idxa[0:16, 0:12], bcg, LD)
    lv = p.val(LD)
    scbcf = ExitStack()
    bcfq = sbs(scbcf, [65, 2048])
    for q in range(4):
        lq = p.idma(bcfq[0:16, 0:Q], idxa[0:16, q:q + 1], xs_ag[:, :], LD,
                    wait=(w_xs, (LD, lv), p.w(VV)), bound=4 * 33 - 1)
        p.idma(bcfq[32:48, 0:Q], idxa[0:16, 4 + q:5 + q], xs_ag[:, :], LD,
               bound=4 * 33 - 1)
        wq = p.val(LD)
        p.op("v", lambda v, q=q: v.tensor_copy(bc16[0:16, TS(q, Q)], bcfq[0:16, 0:Q]),
             wait=((LD, wq),))
        p.op("v", lambda v, q=q: v.tensor_copy(bc16[32:48, TS(q, Q)], bcfq[32:48, 0:Q]),
             inc=VV)
    p.idma(miscf[0:4, 0:Q], idxa[0:4, 8:9], xs_ag[:, :], LD, bound=4 * 33 - 1)
    wd4 = p.val(LD)
    p.dma("g", bc16[64:65, :].rearrange("p (a b) -> p a b", a=4),
          miscf[0:4, 0:Q], LD, wait=((LD, wd4),))
    wbc = p.val(LD)
    p.op("v", lambda v: v.nop(), inc=VV, wait=((LD, wbc),))
    p.barrier("tvsg")
    scbcf.close()
    p.dma("g", misc2[:, 0:NPAN], alog, LD)
    p.dma("g", smallw[:, 0:NDT], bdt, LD)
    lv = p.val(LD)
    p.op("s", lambda s: s.activation(apan[:, :], misc2[:, 0:NPAN], AF.Exp),
         inc=SS, wait=((LD, lv), p.w(VV)))
    p.op("v", lambda v: v.tensor_scalar(apan[:, :], apan[:, :], -1.0, None, ALU.mult),
         inc=VV, wait=(p.w(SS),))
    p.op("v", lambda v: v.memset(hstate[:, :], 0.0), inc=VV)
    p.dma("g", idxa[:, 0:16], xbgidx, LD)
    lvxb = p.val(LD)

    NTH = max(1, Q // 1024)
    for dtile in range(NDT):
        for tq in range(4):
            t0q = tq * Q
            wgx = p.idma(xbsl[:, 0:2048], idxa[:, dtile * 4 + tq: dtile * 4 + tq + 1],
                         xb_ag[:, :], LD, wait=(w_xb, (LD, lvxb), p.w(VV)),
                         bound=4 * DI - 1)
            for c2 in range(Q // 512):
                p.op("t", lambda t, t0q=t0q, c2=c2: t.matmul(
                    pB[:, 0:512], s16[0:16, :], bc16[0:16, t0q + 512 * c2: t0q + 512 * (c2 + 1)],
                    start=True, stop=True), wait=(p.w(VV), p.w(SS)))
                p.op("t", lambda t, t0q=t0q, c2=c2: t.matmul(
                    pB[:, 512:1024], s16[32:48, :], bc16[32:48, t0q + 512 * c2: t0q + 512 * (c2 + 1)],
                    start=True, stop=True), inc=TT)
                p.op("v", lambda v, c2=c2: v.tensor_copy(brepc[:, TS(c2, 512)], pB[:, 0:512]),
                     wait=(p.w(TT),))
                p.op("v", lambda v, c2=c2: v.tensor_copy(crepc[:, TS(c2, 512)], pB[:, 512:1024]),
                     inc=VV)
            for tcc in range(Q // 512):
                tg = t0q + tcc * 512
                p.op("t", lambda t, dtile=dtile, tg=tg: t.matmul(
                    pTf[:, 0:512], wdt_sb[64:65, TS(dtile)],
                    bc16[64:65, tg:tg + 512], start=True, stop=True),
                    inc=TT, wait=(p.w(SS), p.w(VV)))
                p.op("v", lambda v, dtile=dtile: v.tensor_scalar(
                    dtf[:, 0:512], pTf[:, 0:512], smallw[:, dtile:dtile + 1], None,
                    ALU.add), inc=VV, wait=(p.w(TT), p.w(VV)))
                p.op("s", lambda s: s.activation(miscf[:, 0:512], dtf[:, 0:512], AF.Abs),
                     wait=(p.w(VV),))
                p.op("s", lambda s: s.activation(miscf[:, 0:512], miscf[:, 0:512],
                                                 AF.Exp, scale=-1.0))
                p.op("s", lambda s: s.activation(miscf[:, 0:512], miscf[:, 0:512],
                                                 AF.Ln, bias=ceps[:, 1:2]), inc=SS)
                p.op("v", lambda v: v.tensor_relu(misc2[:, 0:512], dtf[:, 0:512]),
                     wait=(p.w(SS),))
                p.op("v", lambda v: v.tensor_add(dtf[:, 0:512], misc2[:, 0:512],
                                                 miscf[:, 0:512]))
                p.op("v", lambda v, tcc=tcc: v.tensor_copy(
                    dtb16[:, TS(tcc, 512)], dtf[:, 0:512]))
                p.op("v", lambda v, tcc=tcc, wgx=wgx: v.tensor_tensor(
                    ub16[:, TS(tcc, 512)], dtf[:, 0:512],
                    xbsl[:, TS(tcc, 512)], ALU.mult), inc=VV,
                    wait=((LD, wgx),))
            for j in range(16):
                pan = dtile * 16 + j
                for th in range(NTH):
                    tl = th * 1024
                    for k2 in range(2):
                        p.op("t", lambda t, j=j, tl=tl, k2=k2: t.matmul(
                            pB[:, TS(k2, 512)], selp[:, j, :],
                            dtb16[:, tl + 512 * k2: tl + 512 * (k2 + 1)],
                            start=True, stop=True),
                            inc=(TT if k2 == 1 else None),
                            wait=((p.w(SS), p.w(VV)) if k2 == 0 else ()))
                    p.op("s", lambda s, pan=pan: s.activation(
                        a_sc[:, 0:1024], pB[:, 0:1024], AF.Exp,
                        scale=apan[:, pan:pan + 1]), inc=SS, wait=(p.w(TT),))
                    for k2 in range(2):
                        p.op("t", lambda t, j=j, tl=tl, k2=k2: t.matmul(
                            pB[:, TS(k2, 512)], selp[:, j, :],
                            ub16[:, tl + 512 * k2: tl + 512 * (k2 + 1)],
                            start=True, stop=True),
                            inc=(TT if k2 == 1 else None),
                            wait=((p.w(SS),) if k2 == 0 else ()))
                    p.op("v", lambda v, tl=tl: v.tensor_tensor(
                        bx_sc[:, 0:1024], pB[:, 0:1024], brepc[:, tl:tl + 1024],
                        ALU.mult), inc=VV, wait=(p.w(TT),))
                    p.op("v", lambda v, pan=pan: v.tensor_tensor_scan(
                        h_sc[:, 0:1024], a_sc[:, 0:1024], bx_sc[:, 0:1024],
                        hstate[:, pan:pan + 1], ALU.mult, ALU.add), inc=VV,
                        wait=(p.w(SS),))
                    p.op("v", lambda v, pan=pan: v.tensor_copy(hstate[:, pan:pan + 1],
                                                               h_sc[:, 1023:1024]),
                         wait=(p.w(VV),))
                    p.op("v", lambda v, tl=tl: v.tensor_tensor(
                        yt_sc[:, 0:1024], h_sc[:, 0:1024], crepc[:, tl:tl + 1024],
                        ALU.mult), inc=VV)
                    for k2 in range(2):
                        p.op("t", lambda t, j=j, th=th, k2=k2: t.matmul(
                            pA[:, th * 1024 + 512 * k2: th * 1024 + 512 * (k2 + 1)],
                            selyp[:, j, :], yt_sc[:, TS(k2, 512)],
                            start=(j == 0), stop=(j == 15)),
                            inc=(TT if (j == 15 and k2 == 1) else None),
                            wait=((p.w(VV), p.w(SS)) if k2 == 0 else ()))
            ydp2 = p.val(YD)
            p.op("s", lambda s, ydp2=ydp2: s.activation(ysl[:, 0:Q], pA[:, 0:Q], AF.Copy),
                 inc=SS, wait=(p.w(TT), (YD, ydp2)))
            sv = p.val(SS)
            p.dma("y", ys_send[TS(dtile), t0q:t0q + Q], ysl[:, 0:Q],
                  YD, wait=((SS, sv),))
    p.op("g", lambda g: g.nop(), wait=(p.w(YD),))
    p.barrier("tvsgy")
    sc3b.close()
    sc3w.close()
    w_ys = p.coll(ys_send[:, :], ys_ag[:, :])

    sc3c = ExitStack()
    yprime = sbs(sc3c, [128, DI // 128, Q], BF16)
    ysl2 = sbs(sc3c, [128, 2048])
    zt = sbs(sc3c, [128, 2048])
    wcp = sbs(sc3c, [128, 16, 1024], BF16)
    p.dma("g", idxa[:, 0:16], ygidx, LD)
    p.dma("g", smallw[:, 0:16], dpar, LD)
    lv = p.val(LD)
    flatys = ys_ag.rearrange("r (m one) -> (r m) one", one=1)
    for dtile in range(DI // 128):
        gy = LD
        wys = p.idma(ysl2[:, 0:Q], idxa[:, dtile:dtile + 1], flatys[:, :], gy,
                     wait=(w_ys, (LD, lv), p.w(VV)), bound=4 * DSL * T - 1)
        lz = p.dma("g", zt[:, 0:Q], z_spill[TS(dtile), :], LD, wait=(p.w(SS), p.w(VV)))
        lx2 = p.dma("g", miscf[:, 0:Q], xb_send[TS(dtile), :], LD)
        p.op("s", lambda s: s.activation(zt[:, 0:Q], zt[:, 0:Q], AF.Silu),
             inc=SS, wait=((LD, p.val(LD)),))
        p.op("v", lambda v, dtile=dtile: v.scalar_tensor_tensor(
            out=ysl2[:, 0:Q], in0=miscf[:, 0:Q], scalar=smallw[:, dtile:dtile + 1],
            in1=ysl2[:, 0:Q], op0=ALU.mult, op1=ALU.add),
            wait=((LD, wys), p.w(SS)))
        p.op("v", lambda v, dtile=dtile: v.tensor_tensor(
            yprime[:, dtile, :], ysl2[:, 0:Q], zt[:, 0:Q], ALU.mult), inc=VV)
    lw = p.dma("g", wcp[:, :, :], cpw.rearrange("(a p) m -> p a m", p=128), LD,
               wait=(p.w(TT),))
    for i in range(NT):
        for oc in range(2):
            for kc in range(16):
                p.op("t", lambda t, i=i, oc=oc, kc=kc: t.matmul(
                    pA[:, TS(oc % 2, 512)], yprime[:, kc, TS(i)],
                    wcp[:, kc, TS(oc, 512)], start=(kc == 0), stop=(kc == 15)),
                    inc=(TT if kc == 15 else None),
                    wait=(((LD, lw), p.w(VV)) if kc == 0 else ()))
            p.op("v", lambda v, i=i, oc=oc: v.tensor_tensor(
                x_sb[:, i, TS(oc, 512)], x_sb[:, i, TS(oc, 512)],
                pA[:, TS(oc % 2, 512)], ALU.add), inc=VV, wait=(p.w(TT),))
    p.barrier("tvsg")
    sc3c.close()

    if DEBUG_DUMPS:
        xd3 = nc.dram_tensor("xdump3", [Q, D], F32, kind="ExternalOutput").ap()
        p.barrier("tvsgy")
        for i in range(NT):
            p.dma("y", xd3[TS(i), :], x_sb[:, i, :], YD)
        p.op("y", lambda y: y.nop(), wait=(p.w(YD),))
        p.barrier("tvsgy")

    emit_ffn()

    p.barrier("tvsgy")
    for i in range(NT):
        p.dma("y", xout[TS(i), :], x_sb[:, i, :], YD)
    p.op("y", lambda y: y.nop(), wait=(p.w(YD),))

    with es:
        with nc.Block() as block:
            p.emit(block)
    return nc



def _host_inputs(T, inputs):
    Q = T // 4
    NT = Q // 128
    QT = T // 128
    CAP = T * 2 // NE
    DSL = DI // 4
    NPAN = DSL * DS // 128
    NDT = DSL // 128
    NROW = 4 * 3 * H * HD

    x = np.asarray(inputs["L0_attn_a_x"], np.float32)
    f32 = lambda a: np.ascontiguousarray(np.asarray(a, np.float32))
    wq = f32(inputs["L0_attn_a_Wq"]); wk = f32(inputs["L0_attn_a_Wk"])
    wv = f32(inputs["L0_attn_a_Wv"]); wo = f32(inputs["L0_attn_a_Wo"])
    bo = f32(inputs["L0_attn_a_bo"]).reshape(1, D)
    choice = f32(inputs["L0_ffn_choice_w"])
    w1 = f32(inputs["L0_ffn_w1"]); w2 = f32(inputs["L0_ffn_w2"])
    inpw = f32(inputs["L0_attn_b_in_proj_w"])
    convw = f32(inputs["L0_attn_b_conv_w"]).reshape(DI, 4)
    convb = f32(inputs["L0_attn_b_conv_b"])
    xpw = f32(inputs["L0_attn_b_x_proj_w"])
    dtw = f32(inputs["L0_attn_b_dt_proj_w"]).reshape(DI)
    dtb = f32(inputs["L0_attn_b_dt_proj_b"])
    alog = f32(inputs["L0_attn_b_A_log"])
    dpar = f32(inputs["L0_attn_b_D_param"])
    cpw = f32(inputs["L0_attn_b_c_proj_w"])

    ndt16 = DI // 128
    convw_l = np.zeros((128, 4 * ndt16), np.float32)
    convb_l = np.zeros((128, ndt16), np.float32)
    dpar_l = np.zeros((128, ndt16), np.float32)
    for dt in range(ndt16):
        for k in range(4):
            convw_l[:, k * ndt16 + dt] = convw[TS(dt), k]
        convb_l[:, dt] = convb[TS(dt)]
        dpar_l[:, dt] = dpar[TS(dt)]

    identb = np.eye(128, dtype=BF)
    identf = np.eye(128, dtype=np.float32)
    onesb = np.ones((128, 128), BF)
    r = np.arange(128)
    ltrib = (r[:, None] <= r[None, :]).astype(BF)
    masku = (r[None, :] > r[:, None]).astype(np.uint8)
    maskl = (r[None, :] < r[:, None]).astype(np.uint8)
    s16 = np.zeros((48, 128), BF)
    for p_ in range(128):
        s16[p_ % 16, p_] = 1
        s16[32 + p_ % 16, p_] = 1
    sel8 = np.zeros((128, 16 * 128), BF)
    selY = np.zeros((128, 16 * 128), BF)
    for j in range(16):
        for p_ in range(128):
            sel8[p_ % 8 * 0 + (j * 8 + p_ // 16) if False else (j * 8 + p_ // 16), j * 128 + p_] = 1
            selY[p_, j * 128 + (j * 8 + p_ // 16)] = 1
    sel8 = np.zeros((128, 16 * 128), BF)
    for j in range(16):
        for p_ in range(128):
            sel8[j * 8 + p_ // 16, j * 128 + p_] = 1
    eqsB = np.zeros((32, 8), np.float32)
    eqbc = np.zeros((8, 32), np.float32)
    eqtri = np.zeros((32, 32), np.float32)
    for e in range(8):
        for q in range(4):
            eqsB[e * 4 + q, e] = 1
            eqbc[e, e * 4 + q] = 1
            for q2 in range(q):
                eqtri[e * 4 + q2, e * 4 + q] = 1

    in_maps = []
    for core in range(8):
        b = core // 4
        myq = core % 4
        xq = np.ascontiguousarray(x[b, myq * Q:(myq + 1) * Q, :])
        qkvg = np.zeros((64, 48), np.int32)
        for hh in range(4):
            hglob = myq * 4 + hh
            for s in range(3):
                for q in range(4):
                    qkvg[:, hh * 12 + s * 4 + q] = (q * 3 * H * HD + s * H * HD
                                                    + hglob * HD + np.arange(64))
        aog = np.zeros((128, 8), np.int32)
        for rblk in range(8):
            aog[:, rblk] = (rblk * 128 + np.arange(128)) * T + myq * Q
        eorder = [2 * myq, 2 * myq + 1] + [e for e in range(8) if e // 2 != myq]
        pg = np.zeros((32, 1), np.int32)
        for j in range(8):
            for q in range(4):
                pg[j * 4 + q, 0] = q * NE + eorder[j]
        selmy = np.zeros((32, 8), np.float32)
        for j in range(8):
            selmy[j * 4 + myq, j] = 1
        ebase = np.zeros((1, 8), np.float32)
        for j in range(8):
            e = eorder[j]
            src, le = e // 2, e % 2
            ebase[0, j] = (src * 2 + le) * CAP
        mygate = np.zeros((2, 128, NE), np.float32)
        for le in range(2):
            mygate[le, :, 2 * myq + le] = 1.0
        ds0 = myq * DSL
        wdtT = dtw[ds0:ds0 + DSL].reshape(1, DSL)
        bdt_l = np.zeros((128, NDT), np.float32)
        alog_l = np.zeros((128, NPAN), np.float32)
        for dt in range(NDT):
            bdt_l[:, dt] = dtb[ds0 + dt * 128: ds0 + (dt + 1) * 128]
            for j in range(16):
                for p_ in range(128):
                    alog_l[p_, dt * 16 + j] = alog[ds0 + dt * 128 + j * 8 + p_ // 16,
                                                   p_ % 16]
        bcg = np.zeros((16, 12), np.int32)
        for q in range(4):
            bcg[:, q] = q * 33 + np.arange(16)
            bcg[:, 4 + q] = q * 33 + 16 + np.arange(16)
            bcg[q, 8] = q * 33 + 32
        xbg = np.zeros((128, 16), np.int32)
        for dt in range(NDT):
            for q in range(4):
                xbg[:, dt * 4 + q] = q * DI + ds0 + dt * 128 + np.arange(128)
        yg = np.zeros((128, 16), np.int32)
        for dt in range(DI // 128):
            yg[:, dt] = (dt * 128 + np.arange(128)) * T + myq * Q
        halog_ = np.full((128, 16), 1 << 28, np.int32)
        if myq > 0:
            for dt in range(DI // 128):
                halog_[:, dt] = (myq - 1) * DI + dt * 128 + np.arange(128)
        w1my = np.stack([w1[2 * myq], w1[2 * myq + 1]])
        w2my = np.stack([w2[2 * myq], w2[2 * myq + 1]])

        in_maps.append({
            "xin": xq,
            "wqT": np.ascontiguousarray(wq.T), "wkT": np.ascontiguousarray(wk.T),
            "wvT": np.ascontiguousarray(wv.T), "woT": np.ascontiguousarray(wo.T),
            "bo": bo, "choiceT": np.ascontiguousarray(choice.T),
            "w1my": w1my, "w2my": w2my,
            "inpw": inpw, "convw": convw_l, "convb": convb_l, "xpw": xpw,
            "wdtT": wdtT, "bdt": bdt_l, "alog": alog_l, "dpar": dpar_l,
            "cpw": cpw,
            "identb": identb, "identf": identf, "onesb": onesb, "ltrib": ltrib,
            "masku": masku, "maskl": maskl, "sel16": s16, "sel8": sel8,
            "selY": selY, "eqsB": eqsB, "eqbc": eqbc, "eqtri": eqtri,
            "selmy": selmy, "mygate": mygate,
            "qkvgidx": qkvg, "aogidx": aog, "pgidx": pg, "bcg": bcg,
            "xbgidx": xbg, "ygidx": yg, "halog": halog_, "ebase": ebase,
        })
    return in_maps


def kernel(**inputs):
    T = 8192
    nc = build_program(T)
    in_maps = _host_inputs(T, inputs)
    res = run_bass_kernel_spmd(nc, in_maps, core_ids=list(range(8)))
    Q = T // 4
    out = np.zeros((2, T, D), np.float32)
    for core in range(8):
        b, myq = core // 4, core % 4
        out[b, myq * Q:(myq + 1) * Q, :] = res.results[core]["xout"]
    return out


# version 07
# speedup vs baseline: 1.0155x; 1.0155x over previous
import numpy as np
import ml_dtypes
from contextlib import ExitStack

import concourse.bass as bass
import concourse.mybir as mybir
from concourse.bass import IndirectOffsetOnAxis
from concourse.bass_utils import run_bass_kernel_spmd

F32 = mybir.dt.float32
BF16 = mybir.dt.bfloat16
I32 = mybir.dt.int32
U8 = mybir.dt.uint8
AF = mybir.ActivationFunctionType
ALU = mybir.AluOpType
AX = mybir.AxisListType
BF = ml_dtypes.bfloat16

D = 1024
H = 16
HD = 64
DI = 2048
DS = 16
NE = 8
FF = 2048
KD = D // 128
NEG = -1.0e30
DBG = {}
DEBUG_DUMPS = False
BIG = float(1 << 19)
POLY = (-0.037, 0.481, -2.3, 4.917, -1.791)
LN_EPS = 1e-5
GROUPS = [[0, 1, 2, 3], [4, 5, 6, 7]]


def TS(i, n=128):
    return slice(i * n, (i + 1) * n)


class Plan:

    def __init__(self, nc):
        self.nc = nc
        self.streams = {k: [] for k in "tvsgy"}
        self.es = ExitStack()
        self.sems = {}
        self.cnt = {}
        self.n_coll = 0
        self.nbar = 0
        self.bregs = {}

    def sem(self, name):
        if name not in self.sems:
            self.sems[name] = self.es.enter_context(self.nc.semaphore(name))
            self.cnt[name] = 0
        return self.sems[name]

    def op(self, eng, fn, inc=None, k=1, wait=()):
        if eng == "v":
            wait = tuple(wait) + (self.w("vv"),)
            if inc is None:
                inc = "vv"
        elif eng == "s":
            wait = tuple(wait) + (self.w("ss"),)
            if inc is None:
                inc = "ss"
        sem_h = self.sem(inc) if inc else None
        waits = [(self.sem(s), v) for (s, v) in wait if v > 0]
        import traceback
        stk = traceback.extract_stack()
        site = stk[-3] if len(stk) >= 3 else stk[0]

        def thunk(e, fn=fn, sem_h=sem_h, k=k, waits=waits, site=site):
            for s, v in waits:
                e.wait_ge(s, v)
            try:
                ins = fn(e)
            except Exception:
                print(f"PLAN SITE: {site.filename}:{site.lineno} {site.line}")
                raise
            if sem_h is not None:
                ins.then_inc(sem_h, k)
        self.streams[eng].append(thunk)
        if inc:
            self.cnt[inc] += k
        return self.cnt[inc] if inc else None

    def val(self, name):
        return self.cnt.get(name, 0)

    def w(self, name):
        return (name, self.cnt.get(name, 0))

    def dma(self, eng, out, in_, inc, wait=()):
        return self.op(eng, lambda e, o=out, i=in_: e.dma_start(out=o, in_=i),
                       inc=inc, k=16, wait=wait)

    def idma(self, out, idx_ap, in_, inc, wait=(), scatter=False, add=False,
             bound=None):
        assert bound is not None
        if bound not in self.bregs:
            holder = {}
            self.bregs[bound] = holder

            def mkreg(g, holder=holder, val=bound):
                r = g.alloc_register(f"bnd{val}")
                g.reg_mov(r, val)
                holder["r"] = r
                return None
            self.streams["g"].append(lambda e, f=mkreg: f(e))
        holder = self.bregs[bound]

        def fn(g, out=out, idx_ap=idx_ap, in_=in_, holder=holder):
            bound_r = holder["r"]
            if scatter:
                return g.indirect_dma_start(
                    out=out, out_offset=IndirectOffsetOnAxis(idx_ap, 0),
                    in_=in_, in_offset=None, bounds_check=bound_r,
                    oob_is_err=False,
                    compute_op=(ALU.add if add else ALU.bypass))
            return g.indirect_dma_start(
                out=out, out_offset=None, in_=in_,
                in_offset=IndirectOffsetOnAxis(idx_ap, 0), bounds_check=bound_r,
                oob_is_err=False,
                compute_op=(ALU.add if add else ALU.bypass))
        return self.op("g", fn, inc=inc, k=16, wait=wait)

    def coll(self, ins_, outs_, wait=()):
        name = f"cc{self.n_coll}"
        self.n_coll += 1
        self.op("g", lambda g, i=ins_, o=outs_: g.collective_compute(
            "AllGather", ALU.bypass, replica_groups=GROUPS, ins=[i.opt()],
            outs=[o.opt()]), inc=name, k=1, wait=wait)
        return (name, 1)

    def barrier(self, engines="tvsg"):
        self.nbar += 1
        vals = {}
        for e in engines:
            vals[e] = self.op(e, lambda x: x.nop(), inc=f"bar_{e}", k=1)
        for e in engines:
            ww = tuple((f"bar_{o}", vals[o]) for o in engines if o != e)
            self.op(e, lambda x: x.nop(), wait=ww)
        if not hasattr(self, "cuts"):
            self.cuts = []
        self.cuts.append({k: len(v) for k, v in self.streams.items()})

    def emit(self, block):
        import os
        lim = int(os.environ.get("KLIM", "0"))
        kcut = int(os.environ.get("KCUT", "-1"))
        eng_map = {"t": block.tensor, "v": block.vector, "s": block.scalar,
                   "g": block.gpsimd, "y": block.sync}
        if kcut >= 0:
            print(f"KCUT={kcut} of {len(self.cuts)} barriers")
            cut = self.cuts[min(kcut, len(self.cuts) - 1)]
        for k in "tvsgy":
            thunks = self.streams[k]
            if kcut >= 0:
                thunks = thunks[:cut[k]]
            if lim and k == "g":
                thunks = thunks[:lim]

            def body(e, thunks=thunks):
                for th in thunks:
                    th(e)
            eng_map[k](body)


def build_program(T):
    DBG.clear()
    Q = T // 4
    NT = Q // 128
    QT = T // 128
    NC5 = Q // 512
    QC5 = T // 512
    CAP = T * 2 // NE
    CT = CAP // 128
    CC5 = CAP // 512
    DSL = DI // 4
    NPAN = (DSL * DS) // 128
    NDT = DSL // 128
    GR = D + NE
    NROW = 4 * 3 * H * HD

    nc = bass.Bass("TRN2", target_bir_lowering=False)
    dram = {}

    def din(name, shape, dt=F32):
        dram[name] = nc.dram_tensor(name, list(shape), dt, kind="ExternalInput").ap()
        return dram[name]

    def dint(name, shape, dt=F32):
        dram[name] = nc.dram_tensor(name, list(shape), dt).ap()
        return dram[name]

    xin = din("xin", [Q, D])
    wqT = din("wqT", [D, D]); wkT = din("wkT", [D, D]); wvT = din("wvT", [D, D])
    woT = din("woT", [D, D]); bo_d = din("bo", [1, D])
    choiceT = din("choiceT", [D, NE])
    w1my = din("w1my", [2, D, FF])
    w2my = din("w2my", [2, FF, D])
    inpw = din("inpw", [D, 2 * DI])
    convw = din("convw", [128, 4 * (DI // 128)])
    convb = din("convb", [128, DI // 128])
    xpw = din("xpw", [DI, 2 * DS + 1])
    wdtT = din("wdtT", [1, DSL])
    bdt = din("bdt", [128, NDT])
    alog = din("alog", [128, NPAN])
    dpar = din("dpar", [128, DI // 128])
    cpw = din("cpw", [DI, D])
    identb = din("identb", [128, 128], BF16)
    identf = din("identf", [128, 128], F32)
    onesb = din("onesb", [128, 128], BF16)
    ltrib = din("ltrib", [128, 128], BF16)
    masku = din("masku", [128, 128], U8)
    maskl = din("maskl", [128, 128], U8)
    sel16 = din("sel16", [48, 128], BF16)
    sel8 = din("sel8", [128, 16 * 128], BF16)
    selY = din("selY", [128, 16 * 128], BF16)
    eqsB = din("eqsB", [32, 8])
    eqbc = din("eqbc", [8, 32])
    eqtri = din("eqtri", [32, 32])
    selmy = din("selmy", [32, 8])
    mygate = din("mygate", [2, 128, NE])
    qkvgidx = din("qkvgidx", [64, 48], I32)
    aogidx = din("aogidx", [128, 8], I32)
    pgidx = din("pgidx", [32, 1], I32)
    bcg = din("bcg", [16, 12], I32)
    xbgidx = din("xbgidx", [128, 16], I32)
    ygidx = din("ygidx", [128, 16], I32)
    halog = din("halog", [128, 16], I32)
    ebase = din("ebase", [1, 8])

    xout = nc.dram_tensor("xout", [Q, D], F32, kind="ExternalOutput").ap()

    qkv_send = dint("qkv_send", [3 * H * HD, Q], BF16)
    qkv_ag = dint("qkv_ag", [NROW, Q], BF16)
    ao_send = dint("ao_send", [4 * HD, T], BF16)
    ao_ag = dint("ao_ag", [16 * HD, T], BF16)
    pr_send = dint("pr_send", [NE, Q], F32)
    pr_ag = dint("pr_ag", [4 * NE, Q], F32)
    xl_send = dint("xl_send", [Q, GR], BF16)
    xl_ag = dint("xl_ag", [4 * Q, GR], BF16)
    comp = dint("comp", [2 * CAP, GR], BF16)
    y_send = dint("y_send", [2 * CAP, D], F32)
    y_ag = dint("y_ag", [8 * CAP, D], F32)
    xbpre_sp = dint("xbpre_sp", [DI, Q], F32)
    halo_send = dint("halo_send", [DI, 4], F32)
    halo_ag = dint("halo_ag", [4 * DI, 4], F32)
    xs_send = dint("xs_send", [2 * DS + 1, Q], F32)
    xs_ag = dint("xs_ag", [4 * (2 * DS + 1), Q], F32)
    xb_send = dint("xb_send", [DI, Q], F32)
    xb_ag = dint("xb_ag", [4 * DI, Q], F32)
    ys_send = dint("ys_send", [DSL, T], F32)
    ys_ag = dint("ys_ag", [4 * DSL, T], F32)
    z_spill = dint("z_spill", [DI, Q], F32)

    p = Plan(nc)
    es = p.es

    _nm = [0]

    def sbs(stack, shape, dt=F32, tag=None):
        _nm[0] += 1
        h = stack.enter_context(nc.sbuf_tensor(f"sb{_nm[0]}", list(shape), dt))
        if tag is not None:
            DBG.setdefault(tag, []).append(f"sb{_nm[0]}")
        return h

    def sb(name, shape, dt=F32):
        return es.enter_context(nc.sbuf_tensor(name, list(shape), dt))

    x_sb = sb("x_sb", [128, NT, D])
    cid = sb("cid", [128, 128], BF16)
    cidf = sb("cidf", [128, 128], F32)
    cones = sb("cones", [128, 128], BF16)
    conesf = sb("conesf", [1, 128])
    cltri = sb("cltri", [128, 128], BF16)
    cmu = sb("cmu", [128, 128], U8)
    cml = sb("cml", [128, 128], U8)
    cneg = sb("cneg", [128, 128], F32)
    ln_st = sb("ln_st", [128, 8])
    ceps = sb("ceps", [128, 2])
    ln_bt = sb("ln_bt", [128, D], BF16)
    w_a = sb("w_a", [128, KD, 128], BF16)
    w_b = sb("w_b", [128, KD, 128], BF16)
    bo_sb = sb("bo_sb", [1, D], BF16)
    idxg = sb("idxg", [64, 48], I32)
    idxa = sb("idxa", [128, 16], I32)
    miscf = sb("miscf", [128, 2048])
    miscb = sb("miscb", [128, 2048], BF16)
    misc2 = sb("misc2", [128, 1024])
    rowa = sb("rowa", [1, 64])
    rowb = sb("rowb", [1, 64])
    s16 = sb("s16", [48, 128], BF16)
    eqs_sb = sb("eqs_sb", [32, 8])
    eqb_sb = sb("eqb_sb", [8, 32])
    eqt_sb = sb("eqt_sb", [32, 32])
    selmy_sb = sb("selmy_sb", [32, 8])
    mygate_sb = sb("mygate_sb", [128, 2, NE])
    ebase_sb = sb("ebase_sb", [1, 8])
    smallw = sb("smallw", [128, 80])
    pA = es.enter_context(nc.psum_tensor("pA", [128, 2048], F32))
    pB = es.enter_context(nc.psum_tensor("pB", [128, 1024], F32))
    pT = es.enter_context(nc.psum_tensor("pT", [128, 1024], BF16))
    pTf = es.enter_context(nc.psum_tensor("pTf", [128, 512], F32))

    LD, YD, TT, VV, SS = "ld", "yd", "tt", "vv", "ss"

    for dst, srcd in ((cid, identb), (cidf, identf), (cones, onesb),
                      (cltri, ltrib), (cmu, masku), (cml, maskl)):
        p.dma("g", dst[:, :], srcd, LD)
    p.dma("g", s16[:, :], sel16, LD)
    p.dma("g", eqs_sb[:, :], eqsB, LD)
    p.dma("g", eqb_sb[:, :], eqbc, LD)
    p.dma("g", eqt_sb[:, :], eqtri, LD)
    p.dma("g", selmy_sb[:, :], selmy, LD)
    p.dma("g", mygate_sb[:, :, :], mygate.rearrange("a p c -> p a c"), LD)
    p.dma("g", idxg[:, :], qkvgidx, LD)
    p.dma("g", bo_sb[:, :], bo_d, LD)
    p.dma("g", ebase_sb[:, :], ebase, LD)
    for i in range(NT):
        p.dma("y", x_sb[:, i, :], xin[TS(i), :], YD)
    p.op("v", lambda v: v.memset(cneg[:, :], NEG), inc=VV)
    p.op("v", lambda v: v.memset(conesf[0:1, :], 1.0), inc=VV)
    p.op("v", lambda v: v.memset(ceps[:, 0:1], LN_EPS))
    p.op("v", lambda v: v.memset(ceps[:, 1:2], 1.0), inc=VV)
    p.op("g", lambda g: g.nop(), wait=(p.w(LD),))
    p.op("y", lambda y: y.nop(), wait=(p.w(YD),))
    p.barrier("tvsgy")

    if DEBUG_DUMPS:
        xd0 = nc.dram_tensor("xdump0", [Q, D], F32, kind="ExternalOutput").ap()
        for i in range(NT):
            p.dma("y", xd0[TS(i), :], x_sb[:, i, :], YD)
        p.op("y", lambda y: y.nop(), wait=(p.w(YD),))
        p.barrier("tvsgy")

    def emit_ln(xl, send=False):
        for i in range(NT):
            ydp = p.val(YD)
            p.op("v", lambda v, i=i: v.reduce_sum(ln_st[:, 0:1], x_sb[:, i, :], axis=AX.X), inc=VV)
            p.op("s", lambda s, i=i: s.activation(miscf[:, 0:D], x_sb[:, i, :], AF.Square,
                                                  accum_out=ln_st[:, 1:2]), inc=SS)
            p.op("v", lambda v: v.tensor_scalar(ln_st[:, 2:3], ln_st[:, 0:1], 1.0 / D, None, ALU.mult),
                 wait=(p.w(SS), p.w(VV)))
            p.op("v", lambda v: v.tensor_tensor(ln_st[:, 3:4], ln_st[:, 2:3], ln_st[:, 2:3], ALU.mult))
            p.op("v", lambda v: v.scalar_tensor_tensor(out=ln_st[:, 3:4], in0=ln_st[:, 1:2],
                                                       scalar=1.0 / D, in1=ln_st[:, 3:4],
                                                       op0=ALU.mult, op1=ALU.subtract), inc=VV)
            p.op("s", lambda s: s.activation(ln_st[:, 4:5], ln_st[:, 3:4], AF.Ln, bias=ceps[:, 0:1]),
                 wait=(p.w(VV),))
            p.op("s", lambda s: s.activation(ln_st[:, 4:5], ln_st[:, 4:5], AF.Exp, scale=-0.5), inc=SS)
            p.op("v", lambda v: v.tensor_tensor(ln_st[:, 5:6], ln_st[:, 2:3], ln_st[:, 4:5], ALU.mult),
                 wait=(p.w(SS),))
            p.op("v", lambda v: v.tensor_scalar(ln_st[:, 5:6], ln_st[:, 5:6], -1.0, None, ALU.mult), inc=VV)
            p.op("v", lambda v, i=i, ydp=ydp: v.tensor_scalar(
                ln_bt[:, :], x_sb[:, i, :], ln_st[:, 4:5], ln_st[:, 5:6],
                ALU.mult, ALU.add), inc=VV, wait=(p.w(VV), p.w(TT), (YD, ydp)))
            sv = p.val(VV)
            if send:
                p.dma("y", xl_send[TS(i), 0:D], ln_bt[:, :], YD, wait=((VV, sv),))
            for c in range(KD):
                p.op("t", lambda t, c=c: t.transpose(pT[:, TS(c % 8)], ln_bt[:, TS(c)],
                                                     cid[:, :]),
                     inc=TT, wait=(((VV, sv),) if c == 0 else ()))
            tfull = p.w(TT)
            for c in range(KD):
                p.op("v", lambda v, dst=xl[:, c, TS(i)], c=c: v.tensor_copy(
                    dst, pT[:, TS(c % 8)]), inc=VV,
                    wait=((tfull,) if c == 0 else ()))
        p.barrier("tvsy" if send else "tvs")

    wbuf_free = {0: 0, 1: 0}

    def emit_projT(wsrc, OD, out_sb, xl):
        for oc in range(OD // 128):
            buf = w_a if oc % 2 == 0 else w_b
            lv = p.dma("g", buf[:, :, :],
                       wsrc[:, TS(oc)].rearrange("(a p) m -> p a m", p=128), LD,
                       wait=((TT, wbuf_free[oc % 2]),))
            for tc in range(NC5):
                for kc in range(KD):
                    p.op("t", lambda t, buf=buf, kc=kc, tc=tc,
                         rr=xl[:, kc, TS(tc, 512)]: t.matmul(
                        pA[:, TS(tc % 2, 512)], buf[:, kc, :], rr,
                        start=(kc == 0), stop=(kc == KD - 1)),
                        inc=(TT if kc == KD - 1 else None),
                        wait=(((LD, lv), p.w(SS)) if kc == 0 else ()))
                p.op("s", lambda s, oc=oc, tc=tc: s.activation(
                    out_sb[:, oc, TS(tc, 512)], pA[:, TS(tc % 2, 512)], AF.Copy),
                    inc=SS, wait=(p.w(TT),))
            wbuf_free[oc % 2] = p.val(TT)
        p.barrier("tvsg")

    sc0 = ExitStack()
    xlnT0 = sbs(sc0, [128, KD, Q], BF16)
    projs = sbs(sc0, [128, KD, Q], BF16)
    emit_ln(xlnT0)
    if DEBUG_DUMPS:
        dbg9 = nc.dram_tensor("attdbg9", [128, KD * Q], BF16,
                              kind="ExternalOutput").ap()
        p.dma("y", dbg9.rearrange("p (a m) -> p a m", a=KD), xlnT0[:, :, :], YD,
              wait=(p.w(VV),))
        p.op("y", lambda y: y.nop(), wait=(p.w(YD),))
        p.barrier("tvsgy")
    for si, wsrc in enumerate((wqT, wkT, wvT)):
        emit_projT(wsrc, D, projs, xlnT0)
        p.dma("y", qkv_send[TS(si, D), :].rearrange("(a p) m -> p a m", p=128),
              projs[:, :, :], YD, wait=(p.w(SS),))
        p.op("y", lambda y: y.nop(), wait=(p.w(YD),))
        p.barrier("tvsy")
    sc0.close()
    p.op("g", lambda g: g.nop(), wait=(p.w(YD),))
    w_qkv = p.coll(qkv_send[:, :], qkv_ag[:, :])

    if DEBUG_DUMPS:
        dbg7 = nc.dram_tensor("attdbg7", [3 * H * HD, Q], BF16,
                              kind="ExternalOutput").ap()
        dbg8 = nc.dram_tensor("attdbg8", [NROW, Q], BF16,
                              kind="ExternalOutput").ap()
        p.dma("g", dbg7[:, :], qkv_send[:, :], LD, wait=(w_qkv,))
        p.dma("g", dbg8[:, :], qkv_ag[:, :], LD)
        p.op("g", lambda g: g.nop(), wait=(p.w(LD),))
        p.barrier("tvsgy")

    sc1 = ExitStack()
    kT_h = sbs(sc1, [64, T], BF16)
    qT_h = sbs(sc1, [64, T], BF16)
    vqT_h = sbs(sc1, [64, T], BF16)
    v_to = sbs(sc1, [128, T // 128, 66], BF16)
    e1sc = sbs(sc1, [128, 2048])
    z1p = sbs(sc1, [128, 8 * QT])
    t1p = sbs(sc1, [128, 8 * QT])
    statA = sbs(sc1, [128, QT])
    statB = sbs(sc1, [128, QT])
    statC = sbs(sc1, [128, QT])
    beta = sbs(sc1, [128, QT])
    qcountf = sbs(sc1, [128, QT])
    maskq = sbs(sc1, [128, QT], U8)
    e2t = sbs(sc1, [128, 1024], BF16)
    av_f = sbs(sc1, [65, 512])
    rz2 = sbs(sc1, [128, 4])
    aost = sbs(sc1, [64, 512], BF16)

    p.op("g", lambda g: g.iota(idxa[:, 0:1], pattern=[[1, 1]], base=1,
                               channel_multiplier=1), inc="gg", k=1)
    p.op("v", lambda v: v.tensor_copy(statA[:, 0:1], idxa[:, 0:1]), inc=VV,
         wait=(p.w("gg"),))
    for qi in range(QT):
        p.op("v", lambda v, qi=qi: v.tensor_scalar(qcountf[:, qi:qi + 1], statA[:, 0:1],
                                                   float(128 * qi), None, ALU.add),
             wait=((p.w(VV),) if qi == 0 else ()))
    p.op("v", lambda v: v.memset(v_to[:, :, 64:66], 0.0))
    p.op("v", lambda v: v.memset(v_to[:, :, 64:65], 1.0), inc=VV)

    for hh in range(4):
        gsem = LD
        for si, dst in ((0, qT_h), (1, kT_h), (2, vqT_h)):
            for q4 in range(4):
                col = hh * 12 + si * 4 + q4
                p.idma(dst[0:64, TS(q4, Q)], idxg[0:64, col:col + 1],
                       qkv_ag[:, :], gsem,
                       wait=((w_qkv, p.w(VV), p.w(TT)) if (si == 0 and q4 == 0) else ()),
                       bound=NROW - 1)
        gv = p.val(gsem)
        for jt in range(T // 128):
            p.op("t", lambda t, jt=jt: t.transpose(pT[:, TS(jt % 8)][:, 0:64],
                                                   vqT_h[:, TS(jt)],
                                                   cid[0:64, 0:64]), inc=TT,
                 wait=((gsem, gv), p.w(VV)))
            p.op("v", lambda v, jt=jt: v.tensor_copy(v_to[:, jt, 0:64],
                                                     pT[:, TS(jt % 8)][:, 0:64]),
                 inc=VV, wait=(p.w(TT),))
        p.op("v", lambda v: v.memset(z1p[:, :], 0.0))
        p.op("v", lambda v: v.memset(t1p[:, :], 0.0), inc=VV)
        half_done = [p.val(VV), p.val(VV)]
        half_sdone = [p.val(SS), p.val(SS)]
        hidx = 0
        for qi in range(QT):
            jend = 128 * (qi + 1)
            for js in range(0, jend, 1024):
                wid = min(1024, jend - js)
                hh2 = hidx % 2
                hidx += 1
                base = hh2 * 1024
                for mm in range(0, wid, 512):
                    nn = min(512, wid - mm)
                    p.op("t", lambda t, qi=qi, js=js, mm=mm, nn=nn, base=base: t.matmul(
                        pA[:, base + mm: base + mm + nn], qT_h[:, TS(qi)],
                        kT_h[:, js + mm: js + mm + nn], start=True, stop=True),
                        inc=(TT if mm + 512 >= wid else None),
                        wait=(((VV, half_done[hh2]), (SS, half_sdone[hh2]))
                              if mm == 0 else ()))
                if js <= jend - 128 < js + 1024:
                    off = jend - 128 - js
                    p.op("v", lambda v, off=off, base=base: v.copy_predicated(
                        pA[:, base + off: base + off + 128], cmu[:, :], cneg[:, :]),
                        inc=VV, wait=(p.w(TT),))
                pcol = qi * 8 + js // 1024
                p.op("s", lambda s, wid=wid, pcol=pcol, base=base: s.activation(
                    e1sc[:, base:base + wid], pA[:, base:base + wid], AF.Exp,
                    scale=0.125, accum_out=z1p[:, pcol:pcol + 1]), inc=SS,
                    wait=(p.w(TT), p.w(VV)))
                half_sdone[hh2] = p.val(SS)
                p.op("v", lambda v, wid=wid, pcol=pcol, base=base: v.scalar_tensor_tensor(
                    out=miscf[:, base:base + wid], in0=pA[:, base:base + wid],
                    scalar=0.125, in1=e1sc[:, base:base + wid], op0=ALU.mult,
                    op1=ALU.mult, accum_out=t1p[:, pcol:pcol + 1]), inc=VV,
                    wait=(p.w(SS),))
                half_done[hh2] = p.val(VV)
        p.op("v", lambda v: v.reduce_sum(statA[:, :],
                                         z1p.ap().rearrange("p (a b) -> p a b", b=8), axis=AX.X),
             inc=VV, wait=(p.w(VV),))
        p.op("v", lambda v: v.reduce_sum(statB[:, :],
                                         t1p.ap().rearrange("p (a b) -> p a b", b=8), axis=AX.X), inc=VV)
        p.op("v", lambda v: v.reciprocal(statC[:, :], statA[:, :]), wait=(p.w(VV),))
        p.op("v", lambda v: v.tensor_tensor(statB[:, :], statB[:, :], statC[:, :], ALU.mult), inc=VV)
        p.op("s", lambda s: s.activation(statA[:, :], statA[:, :], AF.Ln), inc=SS, wait=(p.w(VV),))
        p.op("v", lambda v: v.tensor_sub(statA[:, :], statA[:, :], statB[:, :]), wait=(p.w(SS),))
        p.op("v", lambda v: v.tensor_scalar(statB[:, :], qcountf[:, :], 1e-9, None, ALU.mult))
        p.op("v", lambda v: v.tensor_sub(statA[:, :], statA[:, :], statB[:, :]))
        p.op("v", lambda v: v.tensor_scalar(statB[:, :], statA[:, :], POLY[0], None, ALU.mult))
        for ci, c in enumerate(POLY[1:]):
            p.op("v", lambda v, c=c: v.tensor_scalar(statB[:, :], statB[:, :], c, None, ALU.add))
            if ci < 3:
                p.op("v", lambda v: v.tensor_tensor(statB[:, :], statB[:, :], statA[:, :], ALU.mult))
        p.op("v", lambda v: v.tensor_scalar(statB[:, :], statB[:, :], 1.0, None, ALU.max))
        p.op("v", lambda v: v.tensor_scalar(maskq[:, :], statA[:, :], 0.5, None, ALU.is_gt))
        p.op("v", lambda v: v.memset(statC[:, :], 1.0))
        p.op("v", lambda v: v.select(beta[:, :], maskq[:, :], statB[:, :], statC[:, :]))
        p.op("v", lambda v: v.tensor_scalar(beta[:, :], beta[:, :], 0.125, None, ALU.mult), inc=VV)
        for qi in range(QT):
            p.op("t", lambda t, qi=qi: t.transpose(pT[:, 0:64], qT_h[:, TS(qi)],
                                                   cid[0:64, 0:64]), inc=TT,
                 wait=(p.w(VV),))
            p.op("v", lambda v, qi=qi: v.tensor_scalar(miscb[:, 0:64], pT[:, 0:64],
                                                       beta[:, qi:qi + 1], None, ALU.mult),
                 inc=VV, wait=(p.w(TT),))
            p.op("t", lambda t: t.transpose(pT[0:64, TS(1)], miscb[0:128, 0:64],
                                            cid[:, :]), inc=TT, wait=(p.w(VV),))
            p.op("v", lambda v, qi=qi: v.tensor_copy(vqT_h[0:64, TS(qi)], pT[0:64, TS(1)]),
                 inc=VV, wait=(p.w(TT),))
        p.barrier("tvs")
        for qc in range(QC5):
            npair = 4 * qc + 4
            exp_done = [p.val(SS), p.val(SS)]
            av_done = [p.val(TT), p.val(TT)]
            for ji in range(npair):
                off = max(0, 128 * ji - 512 * qc)
                nn = 512 - off
                slot = ji % 2
                p.op("t", lambda t, ji=ji, qc=qc, off=off, nn=nn, slot=slot: t.matmul(
                    pA[:, slot * 512 + off: slot * 512 + off + nn], kT_h[:, TS(ji)],
                    vqT_h[:, 512 * qc + off: 512 * qc + off + nn], start=True, stop=True),
                    inc=TT, wait=((SS, exp_done[slot]),))
                mv = None
                if ji >= 4 * qc:
                    p.op("v", lambda v, off=off, slot=slot: v.copy_predicated(
                        pA[:, slot * 512 + off: slot * 512 + off + 128], cml[:, :],
                        cneg[:, :]), inc=VV, wait=(p.w(TT),))
                    mv = p.val(VV)
                p.op("s", lambda s, off=off, nn=nn, slot=slot: s.activation(
                    e2t.ap().rearrange("p (a b) -> p a b", a=2)[:, slot, off:off + nn],
                    pA[:, slot * 512 + off: slot * 512 + off + nn], AF.Exp), inc=SS,
                    wait=(p.w(TT), (TT, av_done[slot])) + (((VV, mv),) if mv else ()))
                exp_done[slot] = p.val(SS)
                p.op("t", lambda t, ji=ji, off=off, nn=nn, npair=npair, slot=slot: t.matmul(
                    pB[0:65, off:off + nn], v_to[:, ji, 0:65],
                    e2t.ap().rearrange("p (a b) -> p a b", a=2)[:, slot, off:off + nn],
                    start=(ji == 0), stop=(ji == npair - 1)),
                    inc=TT, wait=(p.w(SS),))
                av_done[slot] = p.val(TT)
            ydao = p.val(YD)
            p.op("s", lambda s: s.activation(av_f[:, :], pB[0:65, 0:512], AF.Copy),
                 inc=SS, wait=(p.w(TT),))
            p.op("v", lambda v: v.nop(), wait=((YD, ydao),))
            for a in range(4):
                p.op("t", lambda t, a=a: t.transpose(pTf[:, TS(a)][:, 0:65],
                                                     av_f[:, TS(a)], cidf[0:65, 0:65]),
                     inc=TT, wait=(p.w(SS), p.w(VV)))
                tv = p.val(TT)
                p.op("v", lambda v, a=a: v.reciprocal(rz2[:, 0:1], pTf[:, TS(a)][:, 64:65]),
                     inc=VV, wait=((TT, tv),))
                p.op("v", lambda v, a=a: v.tensor_scalar(miscb[:, 0:64], pTf[:, TS(a)][:, 0:64],
                                                         rz2[:, 0:1], None, ALU.mult),
                     inc=VV, wait=(p.w(VV),))
                p.op("t", lambda t: t.transpose(pT[0:64, TS(2)], miscb[0:128, 0:64],
                                                cid[:, :]), inc=TT, wait=(p.w(VV),))
                p.op("v", lambda v, a=a: v.tensor_copy(
                    aost[0:64, TS(a)], pT[0:64, TS(2)]),
                    inc=VV, wait=(p.w(TT),))
            p.dma("y", ao_send[TS(hh, HD), qc * 512:(qc + 1) * 512], aost[0:64, :],
                  YD, wait=(p.w(VV),))
    if DEBUG_DUMPS:
        add0 = nc.dram_tensor("attdbg0", [64, T], F32, kind="ExternalOutput").ap()
        add1 = nc.dram_tensor("attdbg1", [64, T], F32, kind="ExternalOutput").ap()
        add2 = nc.dram_tensor("attdbg2", [128, 8 * QT], F32, kind="ExternalOutput").ap()
        add3 = nc.dram_tensor("attdbg3", [128, 8 * QT], F32, kind="ExternalOutput").ap()
        add4 = nc.dram_tensor("attdbg4", [128, QT], F32, kind="ExternalOutput").ap()
        add5 = nc.dram_tensor("attdbg5", [128, QT], F32, kind="ExternalOutput").ap()
        p.barrier("tvsgy")
        for blk in range(T // 2048):
            p.op("v", lambda v, blk=blk: v.tensor_copy(miscf[0:64, 0:2048],
                                                       qT_h[0:64, TS(blk, 2048)]), inc=VV)
            p.dma("y", add0[:, TS(blk, 2048)], miscf[0:64, 0:2048], YD, wait=(p.w(VV),))
            p.op("v", lambda v, blk=blk: v.tensor_copy(miscf[64:128, 0:2048],
                                                       kT_h[0:64, TS(blk, 2048)]), inc=VV)
            p.dma("y", add1[:, TS(blk, 2048)], miscf[64:128, 0:2048], YD, wait=(p.w(VV),))
            p.op("v", lambda v: v.nop(), inc=VV, wait=(p.w(YD),))
        p.dma("y", add2[:, :], z1p[:, :], YD)
        p.dma("y", add3[:, :], t1p[:, :], YD)
        p.dma("y", add4[:, :], statA[:, :], YD)
        p.dma("y", add5[:, :], beta[:, :], YD)
        p.op("y", lambda y: y.nop(), wait=(p.w(YD),))
        p.barrier("tvsgy")

    p.op("g", lambda g: g.nop(), wait=(p.w(YD),))
    p.barrier("tvsg")
    sc1.close()
    w_ao = p.coll(ao_send[:, :], ao_ag[:, :])

    if DEBUG_DUMPS:
        add6 = nc.dram_tensor("attdbg6", [4 * HD, T], F32, kind="ExternalOutput").ap()
        p.barrier("tvsgy")
        for blk in range(4 * HD // 128):
            for cc in range(T // 2048):
                lx = p.dma("g", miscb[:, 0:2048], ao_send[TS(blk), TS(cc, 2048)], LD,
                           wait=(p.w(VV),))
                p.op("v", lambda v: v.tensor_copy(miscf[:, 0:2048], miscb[:, 0:2048]),
                     inc=VV, wait=((LD, lx),))
                p.dma("y", add6[TS(blk), TS(cc, 2048)], miscf[:, 0:2048], YD,
                      wait=(p.w(VV),))
                p.op("v", lambda v: v.nop(), inc=VV, wait=(p.w(YD),))
        p.op("y", lambda y: y.nop(), wait=(p.w(YD),))
        p.barrier("tvsgy")

    sc1b = ExitStack()
    aoT = sbs(sc1b, [128, KD, Q], BF16)
    wop = sbs(sc1b, [128, KD, 512], BF16)
    p.dma("g", idxa[:, 0:8], aogidx, LD)
    lv = p.val(LD)
    gsem = LD
    for r in range(8):
        p.idma(aoT[:, r, :], idxa[:, r:r + 1],
               ao_ag.rearrange("r (m one) -> (r m) one", one=1)[:, :], gsem,
               wait=(w_ao, (LD, lv)), bound=16 * HD * T - 1)
    gv = p.val(gsem)
    for oc in range(2):
        lw = p.dma("g", wop[:, :, :],
                   woT[:, TS(oc, 512)].rearrange("(a p) m -> p a m", p=128), LD,
                   wait=(p.w(TT),))
        for i in range(NT):
            for kc in range(KD):
                p.op("t", lambda t, i=i, kc=kc: t.matmul(
                    pA[:, TS(i % 2, 512)], aoT[:, kc, TS(i)], wop[:, kc, :],
                    start=(kc == 0), stop=False),
                    wait=(((LD, lw), (gsem, gv), p.w(VV)) if kc == 0 else ()))
            p.op("t", lambda t, i=i, oc=oc: t.matmul(
                pA[:, TS(i % 2, 512)], cones[0:1, :], bo_sb[0:1, TS(oc, 512)],
                start=False, stop=True), inc=TT)
            p.op("v", lambda v, i=i, oc=oc: v.tensor_tensor(
                x_sb[:, i, TS(oc, 512)], x_sb[:, i, TS(oc, 512)], pA[:, TS(i % 2, 512)],
                ALU.add), inc=VV, wait=(p.w(TT),))
    p.barrier("tvsg")
    sc1b.close()

    if DEBUG_DUMPS:
        xd1 = nc.dram_tensor("xdump1", [Q, D], F32, kind="ExternalOutput").ap()
        p.barrier("tvsgy")
        for i in range(NT):
            p.dma("y", xd1[TS(i), :], x_sb[:, i, :], YD)
        p.op("y", lambda y: y.nop(), wait=(p.w(YD),))
        p.barrier("tvsgy")

    def emit_ffn():
        scf = ExitStack()
        oidx = sbs(scf, [128, NT * 8], I32, tag="oidx")
        rz2f = sbs(scf, [128, 4])
        mq8 = sbs(scf, [8, 2], U8)
        scfA = ExitStack()
        xlnT = sbs(scfA, [128, KD, Q], BF16)
        emit_ln(xlnT, send=True)
        probs = sbs(scfA, [128, NT, NE])
        gatecol = sbs(scfA, [128, NT, NE], BF16)
        prf = sbs(scfA, [32, Q], tag="prf")
        prfmy = sbs(scfA, [8, Q], tag="prfmy")
        bisv = sbs(scfA, [32, 16], tag="bisv")
        mfm = sbs(scfA, [32, Q], BF16, tag="mfm")
        mfm8 = sbs(scfA, [8, Q], BF16, tag="mfm8")
        mtok = sbs(scfA, [128, QT * 32], BF16, tag="mtok")
        ptok = sbs(scfA, [128, QT * 32], tag="ptok")
        mown = sbs(scfA, [128, NT * 8], BF16, tag="mown")
        pown = sbs(scfA, [128, NT * 8], tag="pown")
        rowz = sbs(scfA, [1, 2048], tag="rowz")
        vsc = sbs(scfA, [128, 2048])
        lw = p.dma("g", w_a[:, :, 0:8], choiceT.rearrange("(a p) m -> p a m", p=128), LD)
        for i in range(NT):
            for kc in range(KD):
                p.op("t", lambda t, i=i, kc=kc, ll=xlnT[:, kc, TS(i)]: t.matmul(
                    pTf[:, 0:8], ll, w_a[:, kc, 0:8],
                    start=(kc == 0), stop=(kc == KD - 1)),
                    inc=(TT if kc == KD - 1 else None),
                    wait=(((LD, lw), p.w(VV)) if kc == 0 else ()))
            p.op("v", lambda v: v.reduce_max(rz2f[:, 0:1], pTf[:, 0:8], axis=AX.X),
                 inc=VV, wait=(p.w(TT),))
            p.op("v", lambda v: v.tensor_scalar(rz2f[:, 1:2], rz2f[:, 0:1], -1.0, None, ALU.mult),
                 inc=VV, wait=(p.w(VV),))
            p.op("s", lambda s, i=i: s.activation(probs[:, i, :], pTf[:, 0:8], AF.Exp,
                                                  bias=rz2f[:, 1:2]), inc=SS, wait=(p.w(VV),))
            p.op("v", lambda v, i=i: v.reduce_sum(rz2f[:, 2:3], probs[:, i, :], axis=AX.X),
                 inc=VV, wait=(p.w(SS),))
            p.op("v", lambda v: v.reciprocal(rz2f[:, 3:4], rz2f[:, 2:3]), inc=VV,
                 wait=(p.w(VV),))
            p.op("v", lambda v, i=i: v.tensor_scalar(probs[:, i, :], probs[:, i, :],
                                                     rz2f[:, 3:4], None, ALU.mult), inc=VV,
                 wait=(p.w(VV),))
            p.op("v", lambda v, i=i: v.tensor_copy(gatecol[:, i, :], probs[:, i, :]), inc=VV)
        for i in range(NT):
            p.op("t", lambda t, i=i: t.transpose(pTf[0:8, 0:128], probs[:, i, :],
                                                 cidf[:, :]), inc=TT, wait=(p.w(VV),))
            p.op("v", lambda v, i=i: v.tensor_copy(prf[0:8, TS(i)], pTf[0:8, 0:128]),
                 inc=VV, wait=(p.w(TT),))
            p.dma("y", xl_send[TS(i), D:GR], gatecol[:, i, :], YD, wait=(p.w(VV),))
        p.dma("y", pr_send[:, :], prf[0:8, :], YD, wait=(p.w(VV),))
        p.op("g", lambda g: g.nop(), wait=(p.w(YD),))
        w_pr = p.coll(pr_send[:, :], pr_ag[:, :])
        w_xl = p.coll(xl_send[:, :], xl_ag[:, :])
        p.dma("g", idxa[0:32, 8:9], pgidx, LD)
        wgpr = p.idma(prf[0:32, :], idxa[0:32, 8:9], pr_ag[:, :], LD,
                      wait=(w_pr, p.w(LD), p.w(VV), p.w(TT)), bound=4 * NE - 1)
        for tc in range(NC5):
            p.op("t", lambda t, tc=tc: t.matmul(
                pTf[0:8, 0:512], selmy_sb[0:32, 0:8], prf[0:32, TS(tc, 512)],
                start=True, stop=True), inc=TT, wait=((LD, wgpr), p.w(VV)))
            p.op("v", lambda v, tc=tc: v.tensor_copy(prfmy[0:8, TS(tc, 512)], pTf[0:8, 0:512]),
                 inc=VV, wait=(p.w(TT),))
        p.op("v", lambda v: v.memset(bisv[0:8, 0:1], 0.0))
        p.op("v", lambda v: v.memset(bisv[0:8, 1:2], 1.0), inc=VV)
        for it in range(40):
            p.op("v", lambda v: v.tensor_tensor(bisv[0:8, 2:3], bisv[0:8, 0:1],
                                                bisv[0:8, 1:2], ALU.add))
            p.op("v", lambda v: v.tensor_scalar(bisv[0:8, 2:3], bisv[0:8, 2:3], 0.5,
                                                None, ALU.mult), inc=VV)
            p.op("t", lambda t: t.matmul(pTf[0:32, 0:1], eqb_sb[0:8, :], bisv[0:8, 2:3],
                                         start=True, stop=True), inc=TT, wait=(p.w(VV),))
            p.op("v", lambda v: v.tensor_copy(bisv[0:32, 3:4], pTf[0:32, 0:1]), inc=VV,
                 wait=(p.w(TT),))
            p.op("v", lambda v: v.tensor_scalar(miscf[0:32, 0:Q], prf[0:32, :],
                                                bisv[0:32, 3:4], 0.0, ALU.is_gt, ALU.add,
                                                accum_out=bisv[0:32, 4:5]), inc=VV,
                 wait=(p.w(VV),))
            p.op("t", lambda t: t.matmul(pTf[0:8, 1:2], eqs_sb[0:32, :], bisv[0:32, 4:5],
                                         start=True, stop=True), inc=TT, wait=(p.w(VV),))
            p.op("v", lambda v: v.tensor_scalar(mq8[0:8, 0:1], pTf[0:8, 1:2],
                                                float(CAP) - 0.5, None, ALU.is_ge),
                 inc=VV, wait=(p.w(TT),))
            p.op("v", lambda v: v.tensor_scalar(mq8[0:8, 1:2], pTf[0:8, 1:2],
                                                float(CAP) - 0.5, None, ALU.is_lt))
            p.op("v", lambda v: v.select(bisv[0:8, 0:1], mq8[0:8, 0:1], bisv[0:8, 2:3],
                                         bisv[0:8, 0:1]))
            p.op("v", lambda v: v.select(bisv[0:8, 1:2], mq8[0:8, 1:2], bisv[0:8, 2:3],
                                         bisv[0:8, 1:2]), inc=VV)
        p.op("t", lambda t: t.matmul(pTf[0:32, 0:1], eqb_sb[0:8, :], bisv[0:8, 0:1],
                                     start=True, stop=True), inc=TT, wait=(p.w(VV),))
        p.op("v", lambda v: v.tensor_copy(bisv[0:32, 3:4], pTf[0:32, 0:1]), inc=VV,
             wait=(p.w(TT),))
        p.op("v", lambda v: v.tensor_scalar(mfm[0:32, :], prf[0:32, :], bisv[0:32, 3:4],
                                            0.0, ALU.is_gt, ALU.add,
                                            accum_out=bisv[0:32, 4:5]), inc=VV,
             wait=(p.w(VV),))
        p.op("t", lambda t: t.matmul(pTf[0:32, 2:3], eqt_sb[0:32, :], bisv[0:32, 4:5],
                                     start=True, stop=True), inc=TT, wait=(p.w(VV),))
        p.op("v", lambda v: v.tensor_copy(bisv[0:32, 5:6], pTf[0:32, 2:3]), inc=VV,
             wait=(p.w(TT),))
        p.op("t", lambda t: t.matmul(pTf[0:8, 3:4], selmy_sb[0:32, :], bisv[0:32, 3:4],
                                     start=True, stop=True), wait=(p.w(VV),))
        p.op("t", lambda t: t.matmul(pTf[0:8, 4:5], selmy_sb[0:32, :], bisv[0:32, 5:6],
                                     start=True, stop=True), inc=TT)
        p.op("v", lambda v: v.tensor_copy(bisv[0:8, 6:7], pTf[0:8, 3:4]), wait=(p.w(TT),))
        p.op("v", lambda v: v.tensor_copy(bisv[0:8, 7:8], pTf[0:8, 4:5]), inc=VV)
        p.op("v", lambda v: v.tensor_scalar(mfm8[0:8, :], prfmy[0:8, :], bisv[0:8, 6:7],
                                            0.0, ALU.is_gt, ALU.add,
                                            accum_out=bisv[0:8, 8:9]), inc=VV,
             wait=(p.w(VV),))
        p.dma("g", rowa[0:1, 0:32], bisv[0:32, 5:6], LD, wait=(p.w(VV),))
        p.dma("g", rowb[0:1, 0:8], bisv[0:8, 7:8], LD)
        lqoff = p.val(LD)

        for tb in range(QT):
            p.op("t", lambda t, tb=tb: t.transpose(pT[:, TS(tb % 8)][:, 0:32],
                                                   mfm[0:32, TS(tb % NT)],
                                                   cid[0:32, 0:32]), inc=TT,
                 wait=(p.w(VV),))
            p.op("v", lambda v, tb=tb: v.tensor_copy(mtok[:, TS(tb, 32)],
                                                     pT[:, TS(tb % 8)][:, 0:32]),
                 inc=VV, wait=(p.w(TT),))
        for mc in range(QT * 32 // 512):
            p.op("t", lambda t, mc=mc: t.matmul(pTf[:, 0:512], cltri[:, :],
                                                mtok[:, TS(mc, 512)], start=True,
                                                stop=True), inc=TT, wait=(p.w(VV),))
            p.op("v", lambda v, mc=mc: v.tensor_copy(ptok[:, TS(mc, 512)], pTf[:, 0:512]),
                 inc=VV, wait=(p.w(TT),))
        p.dma("g", rowz[0:1, 0:QT * 32], ptok[127:128, :], LD, wait=(p.w(VV),))
        lbs = p.val(LD)
        p.op("v", lambda v: v.memset(rowa[0:1, 32:64], 0.0), wait=((LD, lbs),))
        p.op("v", lambda v: v.tensor_copy(miscf[0:1, 0:QT * 32], rowz[0:1, 0:QT * 32]))
        rz3 = rowz.ap().rearrange("p (a b) -> p a b", b=32)
        nqb = QT // 4
        for j in range(2):
            for q in range(4):
                sl = rz3[0:1, q * nqb:(q + 1) * nqb, 4 * j + q:4 * j + q + 1]
                p.op("v", lambda v, sl=sl: v.tensor_copy(rowa[0:1, 48:48 + nqb], sl))
                p.op("v", lambda v: v.tensor_tensor_scan(
                    rowa[0:1, 48:48 + nqb], rowa[0:1, 48:48 + nqb],
                    rowa[0:1, 32:32 + nqb], 0.0, ALU.add, ALU.add), inc=VV)
                p.op("v", lambda v, sl=sl: v.tensor_copy(sl, rowa[0:1, 48:48 + nqb]),
                     wait=(p.w(VV),))
        p.op("v", lambda v: v.tensor_sub(rowz[0:1, 0:QT * 32], rowz[0:1, 0:QT * 32],
                                         miscf[0:1, 0:QT * 32]), inc=VV)
        p.op("v", lambda v: v.nop(), wait=((LD, lqoff),))
        for tb in range(QT):
            p.op("v", lambda v, tb=tb: v.tensor_tensor(rowz[0:1, TS(tb, 32)],
                                                       rowz[0:1, TS(tb, 32)],
                                                       rowa[0:1, 0:32], ALU.add))
        p.op("v", lambda v: v.tensor_scalar(rowz[0:1, 0:QT * 32], rowz[0:1, 0:QT * 32],
                                            -1.0, None, ALU.add), inc=VV)
        for mc in range(QT * 32 // 512):
            p.op("t", lambda t, mc=mc: t.matmul(pTf[:, 0:512], conesf[0:1, :],
                                                rowz[0:1, TS(mc, 512)],
                                                start=True, stop=True), inc=TT,
                 wait=(p.w(VV),))
            p.op("v", lambda v, mc=mc: v.tensor_tensor(ptok[:, TS(mc, 512)],
                                                       ptok[:, TS(mc, 512)],
                                                       pTf[:, 0:512], ALU.add), inc=VV,
                 wait=(p.w(TT),))
        CHW = min(2048, QT * 32)
        for mc in range(QT * 32 // CHW):
            sl = TS(mc, CHW)
            p.op("v", lambda v, sl=sl: v.tensor_scalar(miscf[:, 0:CHW], mtok[:, sl],
                                                       -BIG, None, ALU.mult))
            p.op("v", lambda v, sl=sl: v.tensor_scalar(miscf[:, 0:CHW], miscf[:, 0:CHW],
                                                       BIG, None, ALU.add))
            p.op("v", lambda v, sl=sl: v.tensor_scalar(vsc[:, 0:CHW], ptok[:, sl],
                                                       float(CAP) - 0.5, None, ALU.is_ge))
            p.op("v", lambda v, sl=sl: v.scalar_tensor_tensor(out=miscf[:, 0:CHW],
                                                              in0=vsc[:, 0:CHW],
                                                              scalar=BIG, in1=miscf[:, 0:CHW],
                                                              op0=ALU.mult, op1=ALU.add))
            p.op("v", lambda v, sl=sl: v.tensor_tensor(ptok[:, sl], ptok[:, sl],
                                                       miscf[:, 0:CHW], ALU.add), inc=VV)
        for i in range(NT):
            p.op("t", lambda t, i=i: t.transpose(pT[:, TS(i % 8)][:, 0:8],
                                                 mfm8[0:8, TS(i)],
                                                 cid[0:8, 0:8]), inc=TT, wait=(p.w(VV),))
            p.op("v", lambda v, i=i: v.tensor_copy(mown[:, TS(i, 8)],
                                                   pT[:, TS(i % 8)][:, 0:8]), inc=VV,
                 wait=(p.w(TT),))
        p.op("t", lambda t: t.matmul(pTf[:, 0:NT * 8], cltri[:, :], mown[:, :],
                                     start=True, stop=True), inc=TT, wait=(p.w(VV),))
        p.op("v", lambda v: v.tensor_copy(pown[:, :], pTf[:, 0:NT * 8]), inc=VV,
             wait=(p.w(TT),))
        p.dma("g", rowz[0:1, 0:NT * 8], pown[127:128, :], LD, wait=(p.w(VV),))
        lbs = p.val(LD)
        p.op("v", lambda v: v.tensor_copy(miscf[0:1, 0:NT * 8], rowz[0:1, 0:NT * 8]),
             wait=((LD, lbs),))
        rz8 = rowz.ap().rearrange("p (a b) -> p a b", b=8)
        for j in range(8):
            sl = rz8[0:1, 0:NT, j:j + 1]
            p.op("v", lambda v, sl=sl: v.tensor_copy(rowa[0:1, 48:48 + NT], sl))
            p.op("v", lambda v: v.tensor_tensor_scan(
                rowa[0:1, 48:48 + NT], rowa[0:1, 48:48 + NT],
                rowa[0:1, 32:32 + NT], 0.0, ALU.add, ALU.add), inc=VV)
            p.op("v", lambda v, sl=sl: v.tensor_copy(sl, rowa[0:1, 48:48 + NT]),
                 wait=(p.w(VV),))
        p.op("v", lambda v: v.tensor_sub(rowz[0:1, 0:NT * 8], rowz[0:1, 0:NT * 8],
                                         miscf[0:1, 0:NT * 8]))
        p.op("v", lambda v: v.tensor_scalar(rowb[0:1, 0:8], rowb[0:1, 0:8], -1.0,
                                            None, ALU.add), inc=VV)
        for i in range(NT):
            p.op("v", lambda v, i=i: v.tensor_tensor(rowz[0:1, TS(i, 8)],
                                                     rowz[0:1, TS(i, 8)],
                                                     rowb[0:1, 0:8], ALU.add))
        p.op("v", lambda v: v.nop(), inc=VV)
        p.op("t", lambda t: t.matmul(pTf[:, 0:NT * 8], conesf[0:1, :], rowz[0:1, 0:NT * 8],
                                     start=True, stop=True), inc=TT, wait=(p.w(VV),))
        p.op("v", lambda v: v.tensor_tensor(pown[:, :], pown[:, :], pTf[:, 0:NT * 8],
                                            ALU.add), inc=VV, wait=(p.w(TT),))
        p.op("v", lambda v: v.tensor_scalar(miscf[:, 0:NT * 8], mown[:, :], -BIG, None,
                                            ALU.mult))
        p.op("v", lambda v: v.tensor_scalar(miscf[:, 0:NT * 8], miscf[:, 0:NT * 8], BIG,
                                            None, ALU.add))
        p.op("v", lambda v: v.tensor_tensor(pown[:, :], pown[:, :], miscf[:, 0:NT * 8],
                                            ALU.add))
        p.op("v", lambda v: v.tensor_scalar(miscf[:, 0:NT * 8], pown[:, :],
                                            float(CAP) - 0.5, BIG, ALU.is_ge,
                                            ALU.mult))
        p.op("v", lambda v: v.tensor_tensor(pown[:, :], pown[:, :], miscf[:, 0:NT * 8],
                                            ALU.add), inc=VV)
        for i in range(NT):
            p.op("v", lambda v, i=i: v.tensor_copy(rowz[0:1, TS(i, 8)], ebase_sb[0:1, :]),
                 wait=((p.w(TT),) if i == 0 else ()))
        p.op("v", lambda v: v.nop(), inc=VV)
        p.op("t", lambda t: t.matmul(pTf[:, 0:NT * 8], conesf[0:1, :], rowz[0:1, 0:NT * 8],
                                     start=True, stop=True), inc=TT, wait=(p.w(VV),))
        p.op("v", lambda v: v.tensor_tensor(pown[:, :], pown[:, :], pTf[:, 0:NT * 8],
                                            ALU.add), wait=(p.w(TT),))
        p.op("v", lambda v: v.tensor_copy(oidx[:, :], pown[:, :]), inc=VV)

        ssc = LD
        vv_idx = p.val(VV)
        xstage = sbs(scfA, [128, 2, GR], BF16)
        idxs = sbs(scfA, [128, 2 * QT], I32, tag="idxs")
        stg_free = [p.val(LD), p.val(LD)]
        for tb in range(QT):
            sl = tb % 2
            p.dma("g", xstage[:, sl, :], xl_ag[TS(tb), :], LD,
                  wait=(w_xl, (LD, stg_free[sl])))
            lvs = p.val(LD)
            for le in range(2):
                q = tb // NT
                eqcol = tb * 32 + 4 * le + q
                ic = tb * 2 + le
                p.op("v", lambda v, eqcol=eqcol, le=le: v.tensor_scalar(
                    miscf[:, 0:1], ptok[:, eqcol:eqcol + 1], float(le * CAP), None,
                    ALU.add), wait=((VV, vv_idx),))
                p.op("v", lambda v, ic=ic: v.tensor_copy(idxs[:, ic:ic + 1],
                                                         miscf[:, 0:1]), inc=VV)
                p.idma(comp[:, :], idxs[:, ic:ic + 1], xstage[:, sl, :], ssc,
                       wait=((LD, lvs), p.w(VV)), scatter=True, bound=2 * CAP - 1)
            stg_free[sl] = p.val(LD)
        wsc = p.val(ssc)

        p.barrier("tvsg")
        scfA.close()
        sce = ExitStack()
        HCT = CT // 2
        xcT = sbs(sce, [128, KD, CAP // 2], BF16)
        hT = sbs(sce, [128, FF // 128, CAP // 2], BF16)
        wbig = sbs(sce, [128, 16384], BF16)
        wb1 = wbig.ap().rearrange("p (a m) -> p a m", a=KD)
        wb2 = wbig.ap().rearrange("p (a m) -> p a m", a=16)
        for le in range(2):
            for half in range(2):
                base = le * CAP + half * (CAP // 2)
                for ci in range(HCT):
                    p.dma("g", miscb[:, 0:D], comp[base + 128 * ci: base + 128 * (ci + 1), 0:D],
                          LD, wait=((ssc, wsc), p.w(TT)))
                    p.dma("g", miscb[:, D:D + NE],
                          comp[base + 128 * ci: base + 128 * (ci + 1), D:GR], LD)
                    lvv = p.val(LD)
                    p.op("v", lambda v, le=le: v.tensor_tensor(
                        miscf[:, 0:NE], miscb[:, D:D + NE], mygate_sb[:, le, :], ALU.mult),
                        wait=((LD, lvv),))
                    p.op("v", lambda v, ci=ci: v.reduce_sum(miscf[:, 100 + ci:101 + ci],
                                                            miscf[:, 0:NE], axis=AX.X), inc=VV)
                    for kc in range(KD):
                        p.op("t", lambda t, kc=kc: t.transpose(
                            pT[:, TS(kc % 8)], miscb[:, TS(kc)], cid[:, :]), inc=TT,
                            wait=(p.w(VV),))
                        p.op("v", lambda v, kc=kc, ci=ci: v.tensor_copy(
                            xcT[:, kc, TS(ci)], pT[:, TS(kc % 8)]), inc=VV, wait=(p.w(TT),))
                lw1 = p.dma("g", wb1, w1my[le].rearrange("(a p) m -> p a m", p=128), LD,
                            wait=(p.w(TT),))
                for fc in range(FF // 128):
                    for cc in range((CAP // 2) // 512):
                        for kc in range(KD):
                            p.op("t", lambda t, fc=fc, cc=cc, kc=kc: t.matmul(
                                pA[:, TS(cc % 2, 512)], wb1[:, kc, TS(fc)],
                                xcT[:, kc, TS(cc, 512)], start=(kc == 0), stop=(kc == KD - 1)),
                                inc=(TT if kc == KD - 1 else None),
                                wait=(((LD, lw1), p.w(SS), p.w(VV)) if kc == 0 else ()))
                        p.op("s", lambda s, fc=fc, cc=cc: s.activation(
                            hT[:, fc, TS(cc, 512)], pA[:, TS(cc % 2, 512)], AF.Silu),
                            inc=SS, wait=(p.w(TT),))
                lw2 = p.dma("g", wb2, w2my[le].rearrange("(a p) m -> p a m", p=128), LD,
                            wait=(p.w(TT), p.w(SS)))
                for ci in range(HCT):
                    ydp = p.val(YD)
                    for oc in range(2):
                        for kc in range(16):
                            p.op("t", lambda t, ci=ci, oc=oc, kc=kc: t.matmul(
                                pA[:, TS(oc % 2, 512)], hT[:, kc, TS(ci)],
                                wb2[:, kc, TS(oc, 512)], start=(kc == 0), stop=(kc == 15)),
                                inc=(TT if kc == 15 else None),
                                wait=(((LD, lw2), p.w(VV)) if kc == 0 else ()))
                        p.op("v", lambda v, ci=ci, oc=oc, ydp=ydp: v.tensor_scalar(
                            miscf[:, 1024 + oc * 512: 1024 + (oc + 1) * 512],
                            pA[:, TS(oc % 2, 512)], miscf[:, 100 + ci:101 + ci], None,
                            ALU.mult), inc=VV, wait=(p.w(TT), (YD, ydp)))
                    p.dma("y", y_send[base + 128 * ci: base + 128 * (ci + 1), :],
                          miscf[:, 1024:2048], YD, wait=(p.w(VV),))
        p.op("g", lambda g: g.nop(), wait=(p.w(YD),))
        sce.close()
        w_y = p.coll(y_send[:, :], y_ag[:, :])
        gadd = LD
        for i in range(NT):
            for j in range(8):
                col = i * 8 + j
                ww = ((w_y, p.w(VV), p.w(TT)) if (i == 0 and j == 0)
                      else ((gadd, p.val(gadd)),))
                p.idma(x_sb[:, i, :], oidx[:, col:col + 1], y_ag[:, :], gadd,
                       wait=ww, add=True, bound=8 * CAP - 1)
        p.op("g", lambda g: g.nop(), wait=((gadd, p.val(gadd)),))
        p.barrier("tvsg")
        scf.close()

    emit_ffn()

    if DEBUG_DUMPS:
        xd2 = nc.dram_tensor("xdump2", [Q, D], F32, kind="ExternalOutput").ap()
        p.barrier("tvsgy")
        for i in range(NT):
            p.dma("y", xd2[TS(i), :], x_sb[:, i, :], YD)
        p.op("y", lambda y: y.nop(), wait=(p.w(YD),))
        p.barrier("tvsgy")

    sc3w = ExitStack()
    wdt_sb = sbs(sc3w, [65, DSL], BF16)
    sc3 = ExitStack()
    xlnT3 = sbs(sc3, [128, KD, Q], BF16)
    emit_ln(xlnT3)
    xbq = sbs(sc3, [128, 2048])
    cpad = sbs(sc3, [128, 2051])
    xs_acc = sbs(sc3, [33, Q])
    xpw_sb = sbs(sc3, [128, 16, 33], BF16)
    halo_sb = sbs(sc3, [128, 16, 4])
    for oc in range(2 * DI // 128):
        buf = w_a if oc % 2 == 0 else w_b
        ydprev = p.val(YD)
        lw = p.dma("g", buf[:, :, :],
                   inpw[:, TS(oc)].rearrange("(a p) m -> p a m", p=128), LD,
                   wait=((TT, wbuf_free[oc % 2]),))
        for tc in range(NC5):
            for kc in range(KD):
                p.op("t", lambda t, buf=buf, kc=kc, tc=tc,
                     rr=xlnT3[:, kc, TS(tc, 512)]: t.matmul(
                    pA[:, TS(tc % 2, 512)], buf[:, kc, :], rr,
                    start=(kc == 0), stop=(kc == KD - 1)),
                    inc=(TT if kc == KD - 1 else None),
                    wait=(((LD, lw), p.w(SS)) if kc == 0 else ()))
            p.op("s", lambda s, tc=tc, ydprev=ydprev: s.activation(
                miscf[:, TS(tc, 512)], pA[:, TS(tc % 2, 512)], AF.Copy),
                inc=SS, wait=(p.w(TT),) + (((YD, ydprev),) if tc == 0 else ()))
        wbuf_free[oc % 2] = p.val(TT)
        sv = p.val(SS)
        if oc < DI // 128:
            p.dma("y", xbpre_sp[TS(oc), :], miscf[:, 0:Q], YD, wait=((SS, sv),))
            p.dma("y", halo_send[TS(oc), 0:4], miscf[:, Q - 4:Q], YD)
        else:
            p.dma("y", z_spill[TS(oc - DI // 128), :], miscf[:, 0:Q], YD, wait=((SS, sv),))
    p.op("g", lambda g: g.nop(), wait=(p.w(YD),))
    w_halo = p.coll(halo_send[:, :], halo_ag[:, :])
    p.op("v", lambda v: v.memset(halo_sb[:, :, :], 0.0), inc=VV)
    p.dma("g", idxa[:, 0:16], halog, LD)
    p.idma(halo_sb[:, 0, :], idxa[:, 0:1], halo_ag[:, :], LD,
           wait=(w_halo, p.w(LD), p.w(VV)), bound=4 * DI - 1)
    for dtile in range(1, DI // 128):
        p.idma(halo_sb[:, dtile, :], idxa[:, dtile:dtile + 1], halo_ag[:, :],
               LD, bound=4 * DI - 1)
    whalo = p.val(LD)
    p.dma("g", smallw[:, 0:64], convw, LD)
    p.dma("g", smallw[:, 64:80], convb, LD)
    p.dma("g", xpw_sb[:, :, :], xpw.rearrange("(a p) m -> p a m", p=128), LD)
    p.dma("g", wdt_sb[64:65, :], wdtT, LD)
    lvw = p.val(LD)
    p.op("v", lambda v: v.memset(xs_acc[0:33, :], 0.0), inc=VV)
    for oc in range(DI // 128):
        ydprev = p.val(YD)
        lx = p.dma("g", cpad[:, 3:3 + Q], xbpre_sp[TS(oc), :], LD,
                   wait=(((LD, max(whalo, lvw)),) + (p.w(SS), p.w(VV))))
        p.op("v", lambda v, oc=oc: v.tensor_copy(cpad[:, 0:3], halo_sb[:, oc, 1:4]),
             wait=((LD, lx),))
        p.op("v", lambda v, oc=oc, ydprev=ydprev: v.tensor_scalar(
            xbq[:, 0:Q], cpad[:, 0:Q], smallw[:, 0 + oc:1 + oc], None, ALU.mult),
            wait=((YD, ydprev),))
        for k in range(1, 4):
            p.op("v", lambda v, oc=oc, k=k: v.scalar_tensor_tensor(
                out=xbq[:, 0:Q], in0=cpad[:, k:k + Q],
                scalar=smallw[:, 16 * k + oc:16 * k + oc + 1],
                in1=xbq[:, 0:Q], op0=ALU.mult, op1=ALU.add))
        p.op("v", lambda v: v.nop(), inc=VV)
        p.op("s", lambda s, oc=oc: s.activation(xbq[:, 0:Q], xbq[:, 0:Q], AF.Silu,
                                                bias=smallw[:, 64 + oc:65 + oc]),
             inc=SS, wait=(p.w(VV),))
        sv = p.val(SS)
        p.dma("y", xb_send[TS(oc), :], xbq[:, 0:Q], YD, wait=((SS, sv),))
        p.op("v", lambda v: v.tensor_copy(miscb[:, 0:Q], xbq[:, 0:Q]), inc=VV,
             wait=((SS, sv), p.w(TT)))
        for tc in range(NC5):
            p.op("t", lambda t, oc=oc, tc=tc: t.matmul(
                pTf[0:33, 0:512], xpw_sb[:, oc, :], miscb[:, TS(tc, 512)],
                start=True, stop=True), inc=TT, wait=(p.w(VV),))
            p.op("v", lambda v, tc=tc: v.tensor_tensor(xs_acc[0:33, TS(tc, 512)],
                                                       xs_acc[0:33, TS(tc, 512)],
                                                       pTf[0:33, 0:512], ALU.add),
                 inc=VV, wait=(p.w(TT),))
    p.dma("y", xs_send[:, :], xs_acc[0:33, :], YD, wait=(p.w(VV),))
    p.op("g", lambda g: g.nop(), wait=(p.w(YD),))
    p.barrier("tvsgy")
    sc3.close()
    w_xs = p.coll(xs_send[:, :], xs_ag[:, :])
    w_xb = p.coll(xb_send[:, :], xb_ag[:, :])

    sc3b = ExitStack()
    selp = sbs(sc3b, [128, 16, 128], BF16)
    selyp = sbs(sc3b, [128, 16, 128], BF16)
    p.dma("g", selp[:, :, :], sel8.rearrange("p (a c) -> p a c", a=16), LD)
    p.dma("g", selyp[:, :, :], selY.rearrange("p (a c) -> p a c", a=16), LD)
    bc16 = sbs(sc3b, [65, T], BF16)
    brepc = sbs(sc3b, [128, 2048], BF16)
    crepc = sbs(sc3b, [128, 2048], BF16)
    dtb16 = sbs(sc3b, [128, 2048], BF16)
    ub16 = sbs(sc3b, [128, 2048], BF16)
    xbsl = sbs(sc3b, [128, 2048])
    dtf = sbs(sc3b, [128, 512])
    a_sc = sbs(sc3b, [128, 1024])
    bx_sc = sbs(sc3b, [128, 1024])
    h_sc = sbs(sc3b, [128, 1024])
    yt_sc = sbs(sc3b, [128, 1024], BF16)
    hstate = sbs(sc3b, [128, NPAN])
    apan = sbs(sc3b, [128, NPAN])
    ysl = sbs(sc3b, [128, 2048])

    p.dma("g", idxa[0:16, 0:12], bcg, LD)
    lv = p.val(LD)
    scbcf = ExitStack()
    bcfq = sbs(scbcf, [65, 2048])
    for q in range(4):
        lq = p.idma(bcfq[0:16, 0:Q], idxa[0:16, q:q + 1], xs_ag[:, :], LD,
                    wait=(w_xs, (LD, lv), p.w(VV)), bound=4 * 33 - 1)
        p.idma(bcfq[32:48, 0:Q], idxa[0:16, 4 + q:5 + q], xs_ag[:, :], LD,
               bound=4 * 33 - 1)
        wq = p.val(LD)
        p.op("v", lambda v, q=q: v.tensor_copy(bc16[0:16, TS(q, Q)], bcfq[0:16, 0:Q]),
             wait=((LD, wq),))
        p.op("v", lambda v, q=q: v.tensor_copy(bc16[32:48, TS(q, Q)], bcfq[32:48, 0:Q]),
             inc=VV)
    p.idma(miscf[0:4, 0:Q], idxa[0:4, 8:9], xs_ag[:, :], LD, bound=4 * 33 - 1)
    wd4 = p.val(LD)
    p.dma("g", bc16[64:65, :].rearrange("p (a b) -> p a b", a=4),
          miscf[0:4, 0:Q], LD, wait=((LD, wd4),))
    wbc = p.val(LD)
    p.op("v", lambda v: v.nop(), inc=VV, wait=((LD, wbc),))
    p.barrier("tvsg")
    scbcf.close()
    p.dma("g", misc2[:, 0:NPAN], alog, LD)
    p.dma("g", smallw[:, 0:NDT], bdt, LD)
    lv = p.val(LD)
    p.op("s", lambda s: s.activation(apan[:, :], misc2[:, 0:NPAN], AF.Exp),
         inc=SS, wait=((LD, lv), p.w(VV)))
    p.op("v", lambda v: v.tensor_scalar(apan[:, :], apan[:, :], -1.0, None, ALU.mult),
         inc=VV, wait=(p.w(SS),))
    p.op("v", lambda v: v.memset(hstate[:, :], 0.0), inc=VV)
    p.dma("g", idxa[:, 0:16], xbgidx, LD)
    lvxb = p.val(LD)

    NTH = max(1, Q // 1024)
    for dtile in range(NDT):
        for tq in range(4):
            t0q = tq * Q
            wgx = p.idma(xbsl[:, 0:2048], idxa[:, dtile * 4 + tq: dtile * 4 + tq + 1],
                         xb_ag[:, :], LD, wait=(w_xb, (LD, lvxb), p.w(VV)),
                         bound=4 * DI - 1)
            for c2 in range(Q // 512):
                p.op("t", lambda t, t0q=t0q, c2=c2: t.matmul(
                    pB[:, 0:512], s16[0:16, :], bc16[0:16, t0q + 512 * c2: t0q + 512 * (c2 + 1)],
                    start=True, stop=True), wait=(p.w(VV), p.w(SS)))
                p.op("t", lambda t, t0q=t0q, c2=c2: t.matmul(
                    pB[:, 512:1024], s16[32:48, :], bc16[32:48, t0q + 512 * c2: t0q + 512 * (c2 + 1)],
                    start=True, stop=True), inc=TT)
                p.op("v", lambda v, c2=c2: v.tensor_copy(brepc[:, TS(c2, 512)], pB[:, 0:512]),
                     wait=(p.w(TT),))
                p.op("v", lambda v, c2=c2: v.tensor_copy(crepc[:, TS(c2, 512)], pB[:, 512:1024]),
                     inc=VV)
            for tcc in range(Q // 512):
                tg = t0q + tcc * 512
                p.op("t", lambda t, dtile=dtile, tg=tg: t.matmul(
                    pTf[:, 0:512], wdt_sb[64:65, TS(dtile)],
                    bc16[64:65, tg:tg + 512], start=True, stop=True),
                    inc=TT, wait=(p.w(SS), p.w(VV)))
                p.op("v", lambda v, dtile=dtile: v.tensor_scalar(
                    dtf[:, 0:512], pTf[:, 0:512], smallw[:, dtile:dtile + 1], None,
                    ALU.add), inc=VV, wait=(p.w(TT), p.w(VV)))
                p.op("s", lambda s: s.activation(miscf[:, 0:512], dtf[:, 0:512], AF.Abs),
                     wait=(p.w(VV),))
                p.op("s", lambda s: s.activation(miscf[:, 0:512], miscf[:, 0:512],
                                                 AF.Exp, scale=-1.0))
                p.op("s", lambda s: s.activation(miscf[:, 0:512], miscf[:, 0:512],
                                                 AF.Ln, bias=ceps[:, 1:2]), inc=SS)
                p.op("v", lambda v: v.tensor_relu(misc2[:, 0:512], dtf[:, 0:512]),
                     wait=(p.w(SS),))
                p.op("v", lambda v: v.tensor_add(dtf[:, 0:512], misc2[:, 0:512],
                                                 miscf[:, 0:512]))
                p.op("v", lambda v, tcc=tcc: v.tensor_copy(
                    dtb16[:, TS(tcc, 512)], dtf[:, 0:512]))
                p.op("v", lambda v, tcc=tcc, wgx=wgx: v.tensor_tensor(
                    ub16[:, TS(tcc, 512)], dtf[:, 0:512],
                    xbsl[:, TS(tcc, 512)], ALU.mult), inc=VV,
                    wait=((LD, wgx),))
            for j in range(16):
                pan = dtile * 16 + j
                for th in range(NTH):
                    tl = th * 1024
                    for k2 in range(2):
                        p.op("t", lambda t, j=j, tl=tl, k2=k2: t.matmul(
                            pB[:, TS(k2, 512)], selp[:, j, :],
                            dtb16[:, tl + 512 * k2: tl + 512 * (k2 + 1)],
                            start=True, stop=True),
                            inc=(TT if k2 == 1 else None),
                            wait=((p.w(SS), p.w(VV)) if k2 == 0 else ()))
                    p.op("s", lambda s, pan=pan: s.activation(
                        a_sc[:, 0:1024], pB[:, 0:1024], AF.Exp,
                        scale=apan[:, pan:pan + 1]), inc=SS, wait=(p.w(TT),))
                    for k2 in range(2):
                        p.op("t", lambda t, j=j, tl=tl, k2=k2: t.matmul(
                            pB[:, TS(k2, 512)], selp[:, j, :],
                            ub16[:, tl + 512 * k2: tl + 512 * (k2 + 1)],
                            start=True, stop=True),
                            inc=(TT if k2 == 1 else None),
                            wait=((p.w(SS),) if k2 == 0 else ()))
                    p.op("v", lambda v, tl=tl: v.tensor_tensor(
                        bx_sc[:, 0:1024], pB[:, 0:1024], brepc[:, tl:tl + 1024],
                        ALU.mult), inc=VV, wait=(p.w(TT),))
                    p.op("v", lambda v, pan=pan: v.tensor_tensor_scan(
                        h_sc[:, 0:1024], a_sc[:, 0:1024], bx_sc[:, 0:1024],
                        hstate[:, pan:pan + 1], ALU.mult, ALU.add), inc=VV,
                        wait=(p.w(SS),))
                    p.op("v", lambda v, pan=pan: v.tensor_copy(hstate[:, pan:pan + 1],
                                                               h_sc[:, 1023:1024]),
                         wait=(p.w(VV),))
                    p.op("v", lambda v, tl=tl: v.tensor_tensor(
                        yt_sc[:, 0:1024], h_sc[:, 0:1024], crepc[:, tl:tl + 1024],
                        ALU.mult), inc=VV)
                    for k2 in range(2):
                        p.op("t", lambda t, j=j, th=th, k2=k2: t.matmul(
                            pA[:, th * 1024 + 512 * k2: th * 1024 + 512 * (k2 + 1)],
                            selyp[:, j, :], yt_sc[:, TS(k2, 512)],
                            start=(j == 0), stop=(j == 15)),
                            inc=(TT if (j == 15 and k2 == 1) else None),
                            wait=((p.w(VV), p.w(SS)) if k2 == 0 else ()))
            ydp2 = p.val(YD)
            p.op("s", lambda s, ydp2=ydp2: s.activation(ysl[:, 0:Q], pA[:, 0:Q], AF.Copy),
                 inc=SS, wait=(p.w(TT), (YD, ydp2)))
            sv = p.val(SS)
            p.dma("y", ys_send[TS(dtile), t0q:t0q + Q], ysl[:, 0:Q],
                  YD, wait=((SS, sv),))
    p.op("g", lambda g: g.nop(), wait=(p.w(YD),))
    p.barrier("tvsgy")
    sc3b.close()
    sc3w.close()
    w_ys = p.coll(ys_send[:, :], ys_ag[:, :])

    sc3c = ExitStack()
    yprime = sbs(sc3c, [128, DI // 128, Q], BF16)
    ysl2 = sbs(sc3c, [128, 2048])
    zt = sbs(sc3c, [128, 2048])
    wcp = sbs(sc3c, [128, 16, 1024], BF16)
    p.dma("g", idxa[:, 0:16], ygidx, LD)
    p.dma("g", smallw[:, 0:16], dpar, LD)
    lv = p.val(LD)
    flatys = ys_ag.rearrange("r (m one) -> (r m) one", one=1)
    for dtile in range(DI // 128):
        gy = LD
        wys = p.idma(ysl2[:, 0:Q], idxa[:, dtile:dtile + 1], flatys[:, :], gy,
                     wait=(w_ys, (LD, lv), p.w(VV)), bound=4 * DSL * T - 1)
        lz = p.dma("g", zt[:, 0:Q], z_spill[TS(dtile), :], LD, wait=(p.w(SS), p.w(VV)))
        lx2 = p.dma("g", miscf[:, 0:Q], xb_send[TS(dtile), :], LD)
        p.op("s", lambda s: s.activation(zt[:, 0:Q], zt[:, 0:Q], AF.Silu),
             inc=SS, wait=((LD, p.val(LD)),))
        p.op("v", lambda v, dtile=dtile: v.scalar_tensor_tensor(
            out=ysl2[:, 0:Q], in0=miscf[:, 0:Q], scalar=smallw[:, dtile:dtile + 1],
            in1=ysl2[:, 0:Q], op0=ALU.mult, op1=ALU.add),
            wait=((LD, wys), p.w(SS)))
        p.op("v", lambda v, dtile=dtile: v.tensor_tensor(
            yprime[:, dtile, :], ysl2[:, 0:Q], zt[:, 0:Q], ALU.mult), inc=VV)
    lw = p.dma("g", wcp[:, :, :], cpw.rearrange("(a p) m -> p a m", p=128), LD,
               wait=(p.w(TT),))
    for i in range(NT):
        for oc in range(2):
            for kc in range(16):
                p.op("t", lambda t, i=i, oc=oc, kc=kc: t.matmul(
                    pA[:, TS(oc % 2, 512)], yprime[:, kc, TS(i)],
                    wcp[:, kc, TS(oc, 512)], start=(kc == 0), stop=(kc == 15)),
                    inc=(TT if kc == 15 else None),
                    wait=(((LD, lw), p.w(VV)) if kc == 0 else ()))
            p.op("v", lambda v, i=i, oc=oc: v.tensor_tensor(
                x_sb[:, i, TS(oc, 512)], x_sb[:, i, TS(oc, 512)],
                pA[:, TS(oc % 2, 512)], ALU.add), inc=VV, wait=(p.w(TT),))
    p.barrier("tvsg")
    sc3c.close()

    if DEBUG_DUMPS:
        xd3 = nc.dram_tensor("xdump3", [Q, D], F32, kind="ExternalOutput").ap()
        p.barrier("tvsgy")
        for i in range(NT):
            p.dma("y", xd3[TS(i), :], x_sb[:, i, :], YD)
        p.op("y", lambda y: y.nop(), wait=(p.w(YD),))
        p.barrier("tvsgy")

    emit_ffn()

    p.barrier("tvsgy")
    for i in range(NT):
        p.dma("y", xout[TS(i), :], x_sb[:, i, :], YD)
    p.op("y", lambda y: y.nop(), wait=(p.w(YD),))

    with es:
        with nc.Block() as block:
            p.emit(block)
    return nc



def _host_inputs(T, inputs):
    Q = T // 4
    NT = Q // 128
    QT = T // 128
    CAP = T * 2 // NE
    DSL = DI // 4
    NPAN = DSL * DS // 128
    NDT = DSL // 128
    NROW = 4 * 3 * H * HD

    x = np.asarray(inputs["L0_attn_a_x"], np.float32)
    f32 = lambda a: np.ascontiguousarray(np.asarray(a, np.float32))
    wq = f32(inputs["L0_attn_a_Wq"]); wk = f32(inputs["L0_attn_a_Wk"])
    wv = f32(inputs["L0_attn_a_Wv"]); wo = f32(inputs["L0_attn_a_Wo"])
    bo = f32(inputs["L0_attn_a_bo"]).reshape(1, D)
    choice = f32(inputs["L0_ffn_choice_w"])
    w1 = f32(inputs["L0_ffn_w1"]); w2 = f32(inputs["L0_ffn_w2"])
    inpw = f32(inputs["L0_attn_b_in_proj_w"])
    convw = f32(inputs["L0_attn_b_conv_w"]).reshape(DI, 4)
    convb = f32(inputs["L0_attn_b_conv_b"])
    xpw = f32(inputs["L0_attn_b_x_proj_w"])
    dtw = f32(inputs["L0_attn_b_dt_proj_w"]).reshape(DI)
    dtb = f32(inputs["L0_attn_b_dt_proj_b"])
    alog = f32(inputs["L0_attn_b_A_log"])
    dpar = f32(inputs["L0_attn_b_D_param"])
    cpw = f32(inputs["L0_attn_b_c_proj_w"])

    ndt16 = DI // 128
    convw_l = np.zeros((128, 4 * ndt16), np.float32)
    convb_l = np.zeros((128, ndt16), np.float32)
    dpar_l = np.zeros((128, ndt16), np.float32)
    for dt in range(ndt16):
        for k in range(4):
            convw_l[:, k * ndt16 + dt] = convw[TS(dt), k]
        convb_l[:, dt] = convb[TS(dt)]
        dpar_l[:, dt] = dpar[TS(dt)]

    identb = np.eye(128, dtype=BF)
    identf = np.eye(128, dtype=np.float32)
    onesb = np.ones((128, 128), BF)
    r = np.arange(128)
    ltrib = (r[:, None] <= r[None, :]).astype(BF)
    masku = (r[None, :] > r[:, None]).astype(np.uint8)
    maskl = (r[None, :] < r[:, None]).astype(np.uint8)
    s16 = np.zeros((48, 128), BF)
    for p_ in range(128):
        s16[p_ % 16, p_] = 1
        s16[32 + p_ % 16, p_] = 1
    sel8 = np.zeros((128, 16 * 128), BF)
    selY = np.zeros((128, 16 * 128), BF)
    for j in range(16):
        for p_ in range(128):
            sel8[p_ % 8 * 0 + (j * 8 + p_ // 16) if False else (j * 8 + p_ // 16), j * 128 + p_] = 1
            selY[p_, j * 128 + (j * 8 + p_ // 16)] = 1
    sel8 = np.zeros((128, 16 * 128), BF)
    for j in range(16):
        for p_ in range(128):
            sel8[j * 8 + p_ // 16, j * 128 + p_] = 1
    eqsB = np.zeros((32, 8), np.float32)
    eqbc = np.zeros((8, 32), np.float32)
    eqtri = np.zeros((32, 32), np.float32)
    for e in range(8):
        for q in range(4):
            eqsB[e * 4 + q, e] = 1
            eqbc[e, e * 4 + q] = 1
            for q2 in range(q):
                eqtri[e * 4 + q2, e * 4 + q] = 1

    in_maps = []
    for core in range(8):
        b = core // 4
        myq = core % 4
        xq = np.ascontiguousarray(x[b, myq * Q:(myq + 1) * Q, :])
        qkvg = np.zeros((64, 48), np.int32)
        for hh in range(4):
            hglob = myq * 4 + hh
            for s in range(3):
                for q in range(4):
                    qkvg[:, hh * 12 + s * 4 + q] = (q * 3 * H * HD + s * H * HD
                                                    + hglob * HD + np.arange(64))
        aog = np.zeros((128, 8), np.int32)
        for rblk in range(8):
            aog[:, rblk] = (rblk * 128 + np.arange(128)) * T + myq * Q
        eorder = [2 * myq, 2 * myq + 1] + [e for e in range(8) if e // 2 != myq]
        pg = np.zeros((32, 1), np.int32)
        for j in range(8):
            for q in range(4):
                pg[j * 4 + q, 0] = q * NE + eorder[j]
        selmy = np.zeros((32, 8), np.float32)
        for j in range(8):
            selmy[j * 4 + myq, j] = 1
        ebase = np.zeros((1, 8), np.float32)
        for j in range(8):
            e = eorder[j]
            src, le = e // 2, e % 2
            ebase[0, j] = (src * 2 + le) * CAP
        mygate = np.zeros((2, 128, NE), np.float32)
        for le in range(2):
            mygate[le, :, 2 * myq + le] = 1.0
        ds0 = myq * DSL
        wdtT = dtw[ds0:ds0 + DSL].reshape(1, DSL)
        bdt_l = np.zeros((128, NDT), np.float32)
        alog_l = np.zeros((128, NPAN), np.float32)
        for dt in range(NDT):
            bdt_l[:, dt] = dtb[ds0 + dt * 128: ds0 + (dt + 1) * 128]
            for j in range(16):
                for p_ in range(128):
                    alog_l[p_, dt * 16 + j] = alog[ds0 + dt * 128 + j * 8 + p_ // 16,
                                                   p_ % 16]
        bcg = np.zeros((16, 12), np.int32)
        for q in range(4):
            bcg[:, q] = q * 33 + np.arange(16)
            bcg[:, 4 + q] = q * 33 + 16 + np.arange(16)
            bcg[q, 8] = q * 33 + 32
        xbg = np.zeros((128, 16), np.int32)
        for dt in range(NDT):
            for q in range(4):
                xbg[:, dt * 4 + q] = q * DI + ds0 + dt * 128 + np.arange(128)
        yg = np.zeros((128, 16), np.int32)
        for dt in range(DI // 128):
            yg[:, dt] = (dt * 128 + np.arange(128)) * T + myq * Q
        halog_ = np.full((128, 16), 1 << 28, np.int32)
        if myq > 0:
            for dt in range(DI // 128):
                halog_[:, dt] = (myq - 1) * DI + dt * 128 + np.arange(128)
        w1my = np.stack([w1[2 * myq], w1[2 * myq + 1]])
        w2my = np.stack([w2[2 * myq], w2[2 * myq + 1]])

        in_maps.append({
            "xin": xq,
            "wqT": np.ascontiguousarray(wq.T), "wkT": np.ascontiguousarray(wk.T),
            "wvT": np.ascontiguousarray(wv.T), "woT": np.ascontiguousarray(wo.T),
            "bo": bo, "choiceT": np.ascontiguousarray(choice.T),
            "w1my": w1my, "w2my": w2my,
            "inpw": inpw, "convw": convw_l, "convb": convb_l, "xpw": xpw,
            "wdtT": wdtT, "bdt": bdt_l, "alog": alog_l, "dpar": dpar_l,
            "cpw": cpw,
            "identb": identb, "identf": identf, "onesb": onesb, "ltrib": ltrib,
            "masku": masku, "maskl": maskl, "sel16": s16, "sel8": sel8,
            "selY": selY, "eqsB": eqsB, "eqbc": eqbc, "eqtri": eqtri,
            "selmy": selmy, "mygate": mygate,
            "qkvgidx": qkvg, "aogidx": aog, "pgidx": pg, "bcg": bcg,
            "xbgidx": xbg, "ygidx": yg, "halog": halog_, "ebase": ebase,
        })
    return in_maps


def kernel(**inputs):
    T = 8192
    nc = build_program(T)
    in_maps = _host_inputs(T, inputs)
    res = run_bass_kernel_spmd(nc, in_maps, core_ids=list(range(8)))
    Q = T // 4
    out = np.zeros((2, T, D), np.float32)
    for core in range(8):
        b, myq = core // 4, core % 4
        out[b, myq * Q:(myq + 1) * Q, :] = res.results[core]["xout"]
    return out


# version 08 (final)
# speedup vs baseline: 1.0312x; 1.0154x over previous
import numpy as np
import ml_dtypes
from contextlib import ExitStack

import concourse.bass as bass
import concourse.mybir as mybir
from concourse.bass import IndirectOffsetOnAxis
from concourse.bass_utils import run_bass_kernel_spmd

F32 = mybir.dt.float32
BF16 = mybir.dt.bfloat16
I32 = mybir.dt.int32
U8 = mybir.dt.uint8
AF = mybir.ActivationFunctionType
ALU = mybir.AluOpType
AX = mybir.AxisListType
BF = ml_dtypes.bfloat16

D = 1024
H = 16
HD = 64
DI = 2048
DS = 16
NE = 8
FF = 2048
KD = D // 128
NEG = -1.0e30
DBG = {}
DEBUG_DUMPS = False
BIG = float(1 << 19)
POLY = (-0.037, 0.481, -2.3, 4.917, -1.791)
LN_EPS = 1e-5
GROUPS = [[0, 1, 2, 3], [4, 5, 6, 7]]


def TS(i, n=128):
    return slice(i * n, (i + 1) * n)


class Plan:

    def __init__(self, nc):
        self.nc = nc
        self.streams = {k: [] for k in "tvsgy"}
        self.es = ExitStack()
        self.sems = {}
        self.cnt = {}
        self.n_coll = 0
        self.nbar = 0
        self.bregs = {}

    def sem(self, name):
        if name not in self.sems:
            self.sems[name] = self.es.enter_context(self.nc.semaphore(name))
            self.cnt[name] = 0
        return self.sems[name]

    def op(self, eng, fn, inc=None, k=1, wait=()):
        if eng == "v":
            wait = tuple(wait) + (self.w("vv"),)
            if inc is None:
                inc = "vv"
        elif eng == "s":
            wait = tuple(wait) + (self.w("ss"),)
            if inc is None:
                inc = "ss"
        sem_h = self.sem(inc) if inc else None
        waits = [(self.sem(s), v) for (s, v) in wait if v > 0]
        import traceback
        stk = traceback.extract_stack()
        site = stk[-3] if len(stk) >= 3 else stk[0]

        def thunk(e, fn=fn, sem_h=sem_h, k=k, waits=waits, site=site):
            for s, v in waits:
                e.wait_ge(s, v)
            try:
                ins = fn(e)
            except Exception:
                print(f"PLAN SITE: {site.filename}:{site.lineno} {site.line}")
                raise
            if sem_h is not None:
                ins.then_inc(sem_h, k)
        self.streams[eng].append(thunk)
        if inc:
            self.cnt[inc] += k
        return self.cnt[inc] if inc else None

    def val(self, name):
        return self.cnt.get(name, 0)

    def w(self, name):
        return (name, self.cnt.get(name, 0))

    def dma(self, eng, out, in_, inc, wait=()):
        return self.op(eng, lambda e, o=out, i=in_: e.dma_start(out=o, in_=i),
                       inc=inc, k=16, wait=wait)

    def idma(self, out, idx_ap, in_, inc, wait=(), scatter=False, add=False,
             bound=None):
        assert bound is not None
        if bound not in self.bregs:
            holder = {}
            self.bregs[bound] = holder

            def mkreg(g, holder=holder, val=bound):
                r = g.alloc_register(f"bnd{val}")
                g.reg_mov(r, val)
                holder["r"] = r
                return None
            self.streams["g"].append(lambda e, f=mkreg: f(e))
        holder = self.bregs[bound]

        def fn(g, out=out, idx_ap=idx_ap, in_=in_, holder=holder):
            bound_r = holder["r"]
            if scatter:
                return g.indirect_dma_start(
                    out=out, out_offset=IndirectOffsetOnAxis(idx_ap, 0),
                    in_=in_, in_offset=None, bounds_check=bound_r,
                    oob_is_err=False,
                    compute_op=(ALU.add if add else ALU.bypass))
            return g.indirect_dma_start(
                out=out, out_offset=None, in_=in_,
                in_offset=IndirectOffsetOnAxis(idx_ap, 0), bounds_check=bound_r,
                oob_is_err=False,
                compute_op=(ALU.add if add else ALU.bypass))
        return self.op("g", fn, inc=inc, k=16, wait=wait)

    def coll(self, ins_, outs_, wait=()):
        name = f"cc{self.n_coll}"
        self.n_coll += 1
        self.op("g", lambda g, i=ins_, o=outs_: g.collective_compute(
            "AllGather", ALU.bypass, replica_groups=GROUPS, ins=[i.opt()],
            outs=[o.opt()]), inc=name, k=1, wait=wait)
        return (name, 1)

    def barrier(self, engines="tvsg"):
        self.nbar += 1
        vals = {}
        for e in engines:
            vals[e] = self.op(e, lambda x: x.nop(), inc=f"bar_{e}", k=1)
        for e in engines:
            ww = tuple((f"bar_{o}", vals[o]) for o in engines if o != e)
            self.op(e, lambda x: x.nop(), wait=ww)
        if not hasattr(self, "cuts"):
            self.cuts = []
        self.cuts.append({k: len(v) for k, v in self.streams.items()})

    def emit(self, block):
        import os
        lim = int(os.environ.get("KLIM", "0"))
        kcut = int(os.environ.get("KCUT", "-1"))
        eng_map = {"t": block.tensor, "v": block.vector, "s": block.scalar,
                   "g": block.gpsimd, "y": block.sync}
        if kcut >= 0:
            print(f"KCUT={kcut} of {len(self.cuts)} barriers")
            cut = self.cuts[min(kcut, len(self.cuts) - 1)]
        for k in "tvsgy":
            thunks = self.streams[k]
            if kcut >= 0:
                thunks = thunks[:cut[k]]
            if lim and k == "g":
                thunks = thunks[:lim]

            def body(e, thunks=thunks):
                for th in thunks:
                    th(e)
            eng_map[k](body)


def build_program(T):
    DBG.clear()
    Q = T // 4
    NT = Q // 128
    QT = T // 128
    NC5 = Q // 512
    QC5 = T // 512
    CAP = T * 2 // NE
    CT = CAP // 128
    CC5 = CAP // 512
    DSL = DI // 4
    NPAN = (DSL * DS) // 128
    NDT = DSL // 128
    GR = D + NE
    NROW = 4 * 3 * H * HD

    nc = bass.Bass("TRN2", target_bir_lowering=False)
    dram = {}

    def din(name, shape, dt=F32):
        dram[name] = nc.dram_tensor(name, list(shape), dt, kind="ExternalInput").ap()
        return dram[name]

    def dint(name, shape, dt=F32):
        dram[name] = nc.dram_tensor(name, list(shape), dt).ap()
        return dram[name]

    xin = din("xin", [Q, D])
    wqT = din("wqT", [D, D]); wkT = din("wkT", [D, D]); wvT = din("wvT", [D, D])
    woT = din("woT", [D, D]); bo_d = din("bo", [1, D])
    choiceT = din("choiceT", [D, NE])
    w1my = din("w1my", [2, D, FF])
    w2my = din("w2my", [2, FF, D])
    inpw = din("inpw", [D, 2 * DI])
    convw = din("convw", [128, 4 * (DI // 128)])
    convb = din("convb", [128, DI // 128])
    xpw = din("xpw", [DI, 2 * DS + 1])
    wdtT = din("wdtT", [1, DSL])
    bdt = din("bdt", [128, NDT])
    alog = din("alog", [128, NPAN])
    dpar = din("dpar", [128, DI // 128])
    cpw = din("cpw", [DI, D])
    identb = din("identb", [128, 128], BF16)
    identf = din("identf", [128, 128], F32)
    onesb = din("onesb", [128, 128], BF16)
    ltrib = din("ltrib", [128, 128], BF16)
    masku = din("masku", [128, 128], U8)
    maskl = din("maskl", [128, 128], U8)
    sel16 = din("sel16", [48, 128], BF16)
    sel8 = din("sel8", [128, 16 * 128], BF16)
    selY = din("selY", [128, 16 * 128], BF16)
    eqsB = din("eqsB", [32, 8])
    eqbc = din("eqbc", [8, 32])
    eqtri = din("eqtri", [32, 32])
    selmy = din("selmy", [32, 8])
    mygate = din("mygate", [2, 128, NE])
    qkvgidx = din("qkvgidx", [64, 48], I32)
    aogidx = din("aogidx", [128, 8], I32)
    pgidx = din("pgidx", [32, 1], I32)
    bcg = din("bcg", [16, 12], I32)
    xbgidx = din("xbgidx", [128, 16], I32)
    ygidx = din("ygidx", [128, 16], I32)
    halog = din("halog", [128, 16], I32)
    ebase = din("ebase", [1, 8])

    xout = nc.dram_tensor("xout", [Q, D], F32, kind="ExternalOutput").ap()

    qkv_send = dint("qkv_send", [3 * H * HD, Q], BF16)
    qkv_ag = dint("qkv_ag", [NROW, Q], BF16)
    ao_send = dint("ao_send", [4 * HD, T], BF16)
    ao_ag = dint("ao_ag", [16 * HD, T], BF16)
    pr_send = dint("pr_send", [NE, Q], F32)
    pr_ag = dint("pr_ag", [4 * NE, Q], F32)
    xl_send = dint("xl_send", [Q, GR], BF16)
    xl_ag = dint("xl_ag", [4 * Q, GR], BF16)
    comp = dint("comp", [2 * CAP, GR], BF16)
    y_send = dint("y_send", [2 * CAP, D], F32)
    y_ag = dint("y_ag", [8 * CAP, D], F32)
    xbpre_sp = dint("xbpre_sp", [DI, Q], F32)
    halo_send = dint("halo_send", [DI, 4], F32)
    halo_ag = dint("halo_ag", [4 * DI, 4], F32)
    xs_send = dint("xs_send", [2 * DS + 1, Q], F32)
    xs_ag = dint("xs_ag", [4 * (2 * DS + 1), Q], F32)
    xb_send = dint("xb_send", [DI, Q], F32)
    xb_ag = dint("xb_ag", [4 * DI, Q], F32)
    ys_send = dint("ys_send", [DSL, T], F32)
    ys_ag = dint("ys_ag", [4 * DSL, T], F32)
    z_spill = dint("z_spill", [DI, Q], F32)

    p = Plan(nc)
    es = p.es

    _nm = [0]

    def sbs(stack, shape, dt=F32, tag=None):
        _nm[0] += 1
        h = stack.enter_context(nc.sbuf_tensor(f"sb{_nm[0]}", list(shape), dt))
        if tag is not None:
            DBG.setdefault(tag, []).append(f"sb{_nm[0]}")
        return h

    def sb(name, shape, dt=F32):
        return es.enter_context(nc.sbuf_tensor(name, list(shape), dt))

    x_sb = sb("x_sb", [128, NT, D])
    cid = sb("cid", [128, 128], BF16)
    cidf = sb("cidf", [128, 128], F32)
    cones = sb("cones", [128, 128], BF16)
    conesf = sb("conesf", [1, 128])
    cltri = sb("cltri", [128, 128], BF16)
    cmu = sb("cmu", [128, 128], U8)
    cml = sb("cml", [128, 128], U8)
    cneg = sb("cneg", [128, 128], F32)
    ln_st = sb("ln_st", [128, 8])
    ceps = sb("ceps", [128, 2])
    ln_bt = sb("ln_bt", [128, D], BF16)
    w_a = sb("w_a", [128, KD, 128], BF16)
    w_b = sb("w_b", [128, KD, 128], BF16)
    bo_sb = sb("bo_sb", [1, D], BF16)
    idxg = sb("idxg", [64, 48], I32)
    idxa = sb("idxa", [128, 16], I32)
    miscf = sb("miscf", [128, 2048])
    miscb = sb("miscb", [128, 2048], BF16)
    misc2 = sb("misc2", [128, 1024])
    rowa = sb("rowa", [1, 64])
    rowb = sb("rowb", [1, 64])
    s16 = sb("s16", [48, 128], BF16)
    eqs_sb = sb("eqs_sb", [32, 8])
    eqb_sb = sb("eqb_sb", [8, 32])
    eqt_sb = sb("eqt_sb", [32, 32])
    selmy_sb = sb("selmy_sb", [32, 8])
    mygate_sb = sb("mygate_sb", [128, 2, NE])
    ebase_sb = sb("ebase_sb", [1, 8])
    smallw = sb("smallw", [128, 80])
    pA = es.enter_context(nc.psum_tensor("pA", [128, 2048], F32))
    pB = es.enter_context(nc.psum_tensor("pB", [128, 1024], F32))
    pT = es.enter_context(nc.psum_tensor("pT", [128, 1024], BF16))
    pTf = es.enter_context(nc.psum_tensor("pTf", [128, 512], F32))

    LD, YD, TT, VV, SS = "ld", "yd", "tt", "vv", "ss"

    for dst, srcd in ((cid, identb), (cidf, identf), (cones, onesb),
                      (cltri, ltrib), (cmu, masku), (cml, maskl)):
        p.dma("g", dst[:, :], srcd, LD)
    p.dma("g", s16[:, :], sel16, LD)
    p.dma("g", eqs_sb[:, :], eqsB, LD)
    p.dma("g", eqb_sb[:, :], eqbc, LD)
    p.dma("g", eqt_sb[:, :], eqtri, LD)
    p.dma("g", selmy_sb[:, :], selmy, LD)
    p.dma("g", mygate_sb[:, :, :], mygate.rearrange("a p c -> p a c"), LD)
    p.dma("g", idxg[:, :], qkvgidx, LD)
    p.dma("g", bo_sb[:, :], bo_d, LD)
    p.dma("g", ebase_sb[:, :], ebase, LD)
    for i in range(NT):
        p.dma("y", x_sb[:, i, :], xin[TS(i), :], YD)
    p.op("v", lambda v: v.memset(cneg[:, :], NEG), inc=VV)
    p.op("v", lambda v: v.memset(conesf[0:1, :], 1.0), inc=VV)
    p.op("v", lambda v: v.memset(ceps[:, 0:1], LN_EPS))
    p.op("v", lambda v: v.memset(ceps[:, 1:2], 1.0), inc=VV)
    p.op("g", lambda g: g.nop(), wait=(p.w(LD),))
    p.op("y", lambda y: y.nop(), wait=(p.w(YD),))
    p.barrier("tvsgy")

    if DEBUG_DUMPS:
        xd0 = nc.dram_tensor("xdump0", [Q, D], F32, kind="ExternalOutput").ap()
        for i in range(NT):
            p.dma("y", xd0[TS(i), :], x_sb[:, i, :], YD)
        p.op("y", lambda y: y.nop(), wait=(p.w(YD),))
        p.barrier("tvsgy")

    def emit_ln(xl, send=False):
        for i in range(NT):
            ydp = p.val(YD)
            p.op("v", lambda v, i=i: v.reduce_sum(ln_st[:, 0:1], x_sb[:, i, :], axis=AX.X), inc=VV)
            p.op("s", lambda s, i=i: s.activation(miscf[:, 0:D], x_sb[:, i, :], AF.Square,
                                                  accum_out=ln_st[:, 1:2]), inc=SS)
            p.op("v", lambda v: v.tensor_scalar(ln_st[:, 2:3], ln_st[:, 0:1], 1.0 / D, None, ALU.mult),
                 wait=(p.w(SS), p.w(VV)))
            p.op("v", lambda v: v.tensor_tensor(ln_st[:, 3:4], ln_st[:, 2:3], ln_st[:, 2:3], ALU.mult))
            p.op("v", lambda v: v.scalar_tensor_tensor(out=ln_st[:, 3:4], in0=ln_st[:, 1:2],
                                                       scalar=1.0 / D, in1=ln_st[:, 3:4],
                                                       op0=ALU.mult, op1=ALU.subtract), inc=VV)
            p.op("s", lambda s: s.activation(ln_st[:, 4:5], ln_st[:, 3:4], AF.Ln, bias=ceps[:, 0:1]),
                 wait=(p.w(VV),))
            p.op("s", lambda s: s.activation(ln_st[:, 4:5], ln_st[:, 4:5], AF.Exp, scale=-0.5), inc=SS)
            p.op("v", lambda v: v.tensor_tensor(ln_st[:, 5:6], ln_st[:, 2:3], ln_st[:, 4:5], ALU.mult),
                 wait=(p.w(SS),))
            p.op("v", lambda v: v.tensor_scalar(ln_st[:, 5:6], ln_st[:, 5:6], -1.0, None, ALU.mult), inc=VV)
            p.op("v", lambda v, i=i, ydp=ydp: v.tensor_scalar(
                ln_bt[:, :], x_sb[:, i, :], ln_st[:, 4:5], ln_st[:, 5:6],
                ALU.mult, ALU.add), inc=VV, wait=(p.w(VV), p.w(TT), (YD, ydp)))
            sv = p.val(VV)
            if send:
                p.dma("y", xl_send[TS(i), 0:D], ln_bt[:, :], YD, wait=((VV, sv),))
            for c in range(KD):
                p.op("t", lambda t, c=c: t.transpose(pT[:, TS(c % 8)], ln_bt[:, TS(c)],
                                                     cid[:, :]),
                     inc=TT, wait=(((VV, sv),) if c == 0 else ()))
            tfull = p.w(TT)
            for c in range(KD):
                p.op("v", lambda v, dst=xl[:, c, TS(i)], c=c: v.tensor_copy(
                    dst, pT[:, TS(c % 8)]), inc=VV,
                    wait=((tfull,) if c == 0 else ()))
        p.barrier("tvsy" if send else "tvs")

    wbuf_free = {0: 0, 1: 0}

    def emit_projT(wsrc, OD, out_sb, xl):
        for oc in range(OD // 128):
            buf = w_a if oc % 2 == 0 else w_b
            lv = p.dma("g", buf[:, :, :],
                       wsrc[:, TS(oc)].rearrange("(a p) m -> p a m", p=128), LD,
                       wait=((TT, wbuf_free[oc % 2]),))
            for tc in range(NC5):
                for kc in range(KD):
                    p.op("t", lambda t, buf=buf, kc=kc, tc=tc,
                         rr=xl[:, kc, TS(tc, 512)]: t.matmul(
                        pA[:, TS(tc % 2, 512)], buf[:, kc, :], rr,
                        start=(kc == 0), stop=(kc == KD - 1)),
                        inc=(TT if kc == KD - 1 else None),
                        wait=(((LD, lv), p.w(SS)) if kc == 0 else ()))
                p.op("s", lambda s, oc=oc, tc=tc: s.activation(
                    out_sb[:, oc, TS(tc, 512)], pA[:, TS(tc % 2, 512)], AF.Copy),
                    inc=SS, wait=(p.w(TT),))
            wbuf_free[oc % 2] = p.val(TT)
        p.barrier("tvsg")

    sc0 = ExitStack()
    xlnT0 = sbs(sc0, [128, KD, Q], BF16)
    projs = sbs(sc0, [128, KD, Q], BF16)
    emit_ln(xlnT0)
    if DEBUG_DUMPS:
        dbg9 = nc.dram_tensor("attdbg9", [128, KD * Q], BF16,
                              kind="ExternalOutput").ap()
        p.dma("y", dbg9.rearrange("p (a m) -> p a m", a=KD), xlnT0[:, :, :], YD,
              wait=(p.w(VV),))
        p.op("y", lambda y: y.nop(), wait=(p.w(YD),))
        p.barrier("tvsgy")
    for si, wsrc in enumerate((wqT, wkT, wvT)):
        emit_projT(wsrc, D, projs, xlnT0)
        p.dma("y", qkv_send[TS(si, D), :].rearrange("(a p) m -> p a m", p=128),
              projs[:, :, :], YD, wait=(p.w(SS),))
        p.op("y", lambda y: y.nop(), wait=(p.w(YD),))
        p.barrier("tvsy")
    sc0.close()
    p.op("g", lambda g: g.nop(), wait=(p.w(YD),))
    w_qkv = p.coll(qkv_send[:, :], qkv_ag[:, :])

    if DEBUG_DUMPS:
        dbg7 = nc.dram_tensor("attdbg7", [3 * H * HD, Q], BF16,
                              kind="ExternalOutput").ap()
        dbg8 = nc.dram_tensor("attdbg8", [NROW, Q], BF16,
                              kind="ExternalOutput").ap()
        p.dma("g", dbg7[:, :], qkv_send[:, :], LD, wait=(w_qkv,))
        p.dma("g", dbg8[:, :], qkv_ag[:, :], LD)
        p.op("g", lambda g: g.nop(), wait=(p.w(LD),))
        p.barrier("tvsgy")

    sc1 = ExitStack()
    kT_h = sbs(sc1, [64, T], BF16)
    qT_h = sbs(sc1, [64, T], BF16)
    vqT_h = sbs(sc1, [64, T], BF16)
    v_to = sbs(sc1, [128, T // 128, 66], BF16)
    e1sc = sbs(sc1, [128, 2048])
    z1p = sbs(sc1, [128, 8 * QT])
    t1p = sbs(sc1, [128, 8 * QT])
    statA = sbs(sc1, [128, QT])
    statB = sbs(sc1, [128, QT])
    statC = sbs(sc1, [128, QT])
    beta = sbs(sc1, [128, QT])
    qcountf = sbs(sc1, [128, QT])
    maskq = sbs(sc1, [128, QT], U8)
    e2t = sbs(sc1, [128, 1024], BF16)
    av_f = sbs(sc1, [65, 512])
    rz2 = sbs(sc1, [128, 4])
    aost = sbs(sc1, [64, 512], BF16)

    p.op("g", lambda g: g.iota(idxa[:, 0:1], pattern=[[1, 1]], base=1,
                               channel_multiplier=1), inc="gg", k=1)
    p.op("v", lambda v: v.tensor_copy(statA[:, 0:1], idxa[:, 0:1]), inc=VV,
         wait=(p.w("gg"),))
    for qi in range(QT):
        p.op("v", lambda v, qi=qi: v.tensor_scalar(qcountf[:, qi:qi + 1], statA[:, 0:1],
                                                   float(128 * qi), None, ALU.add),
             wait=((p.w(VV),) if qi == 0 else ()))
    p.op("v", lambda v: v.memset(v_to[:, :, 64:66], 0.0))
    p.op("v", lambda v: v.memset(v_to[:, :, 64:65], 1.0), inc=VV)

    for hh in range(4):
        gsem = LD
        for si, dst in ((0, qT_h), (1, kT_h), (2, vqT_h)):
            for q4 in range(4):
                col = hh * 12 + si * 4 + q4
                p.idma(dst[0:64, TS(q4, Q)], idxg[0:64, col:col + 1],
                       qkv_ag[:, :], gsem,
                       wait=((w_qkv, p.w(VV), p.w(TT)) if (si == 0 and q4 == 0) else ()),
                       bound=NROW - 1)
        gv = p.val(gsem)
        for g8 in range(T // 128 // 8):
            sv8 = p.val(VV)
            for k8 in range(8):
                jt = g8 * 8 + k8
                p.op("t", lambda t, jt=jt: t.transpose(pT[:, TS(jt % 8)][:, 0:64],
                                                       vqT_h[:, TS(jt)],
                                                       cid[0:64, 0:64]), inc=TT,
                     wait=(((gsem, gv), (VV, sv8)) if k8 == 0 else ()))
            tf8 = p.w(TT)
            for k8 in range(8):
                jt = g8 * 8 + k8
                p.op("v", lambda v, jt=jt: v.tensor_copy(v_to[:, jt, 0:64],
                                                         pT[:, TS(jt % 8)][:, 0:64]),
                     inc=VV, wait=((tf8,) if k8 == 0 else ()))
        p.op("v", lambda v: v.memset(z1p[:, :], 0.0))
        p.op("v", lambda v: v.memset(t1p[:, :], 0.0), inc=VV)
        half_done = [p.val(VV), p.val(VV)]
        half_sdone = [p.val(SS), p.val(SS)]
        hidx = 0
        for qi in range(QT):
            jend = 128 * (qi + 1)
            for js in range(0, jend, 1024):
                wid = min(1024, jend - js)
                hh2 = hidx % 2
                hidx += 1
                base = hh2 * 1024
                for mm in range(0, wid, 512):
                    nn = min(512, wid - mm)
                    p.op("t", lambda t, qi=qi, js=js, mm=mm, nn=nn, base=base: t.matmul(
                        pA[:, base + mm: base + mm + nn], qT_h[:, TS(qi)],
                        kT_h[:, js + mm: js + mm + nn], start=True, stop=True),
                        inc=(TT if mm + 512 >= wid else None),
                        wait=(((VV, half_done[hh2]), (SS, half_sdone[hh2]))
                              if mm == 0 else ()))
                if js <= jend - 128 < js + 1024:
                    off = jend - 128 - js
                    p.op("v", lambda v, off=off, base=base: v.copy_predicated(
                        pA[:, base + off: base + off + 128], cmu[:, :], cneg[:, :]),
                        inc=VV, wait=(p.w(TT),))
                pcol = qi * 8 + js // 1024
                p.op("s", lambda s, wid=wid, pcol=pcol, base=base: s.activation(
                    e1sc[:, base:base + wid], pA[:, base:base + wid], AF.Exp,
                    scale=0.125, accum_out=z1p[:, pcol:pcol + 1]), inc=SS,
                    wait=(p.w(TT), p.w(VV)))
                half_sdone[hh2] = p.val(SS)
                p.op("v", lambda v, wid=wid, pcol=pcol, base=base: v.scalar_tensor_tensor(
                    out=miscf[:, base:base + wid], in0=pA[:, base:base + wid],
                    scalar=0.125, in1=e1sc[:, base:base + wid], op0=ALU.mult,
                    op1=ALU.mult, accum_out=t1p[:, pcol:pcol + 1]), inc=VV,
                    wait=(p.w(SS),))
                half_done[hh2] = p.val(VV)
        p.op("v", lambda v: v.reduce_sum(statA[:, :],
                                         z1p.ap().rearrange("p (a b) -> p a b", b=8), axis=AX.X),
             inc=VV, wait=(p.w(VV),))
        p.op("v", lambda v: v.reduce_sum(statB[:, :],
                                         t1p.ap().rearrange("p (a b) -> p a b", b=8), axis=AX.X), inc=VV)
        p.op("v", lambda v: v.reciprocal(statC[:, :], statA[:, :]), wait=(p.w(VV),))
        p.op("v", lambda v: v.tensor_tensor(statB[:, :], statB[:, :], statC[:, :], ALU.mult), inc=VV)
        p.op("s", lambda s: s.activation(statA[:, :], statA[:, :], AF.Ln), inc=SS, wait=(p.w(VV),))
        p.op("v", lambda v: v.tensor_sub(statA[:, :], statA[:, :], statB[:, :]), wait=(p.w(SS),))
        p.op("v", lambda v: v.tensor_scalar(statB[:, :], qcountf[:, :], 1e-9, None, ALU.mult))
        p.op("v", lambda v: v.tensor_sub(statA[:, :], statA[:, :], statB[:, :]))
        p.op("v", lambda v: v.tensor_scalar(statB[:, :], statA[:, :], POLY[0], None, ALU.mult))
        for ci, c in enumerate(POLY[1:]):
            p.op("v", lambda v, c=c: v.tensor_scalar(statB[:, :], statB[:, :], c, None, ALU.add))
            if ci < 3:
                p.op("v", lambda v: v.tensor_tensor(statB[:, :], statB[:, :], statA[:, :], ALU.mult))
        p.op("v", lambda v: v.tensor_scalar(statB[:, :], statB[:, :], 1.0, None, ALU.max))
        p.op("v", lambda v: v.tensor_scalar(maskq[:, :], statA[:, :], 0.5, None, ALU.is_gt))
        p.op("v", lambda v: v.memset(statC[:, :], 1.0))
        p.op("v", lambda v: v.select(beta[:, :], maskq[:, :], statB[:, :], statC[:, :]))
        p.op("v", lambda v: v.tensor_scalar(beta[:, :], beta[:, :], 0.125, None, ALU.mult), inc=VV)
        for qi in range(QT):
            p.op("t", lambda t, qi=qi: t.transpose(pT[:, 0:64], qT_h[:, TS(qi)],
                                                   cid[0:64, 0:64]), inc=TT,
                 wait=(p.w(VV),))
            p.op("v", lambda v, qi=qi: v.tensor_scalar(miscb[:, 0:64], pT[:, 0:64],
                                                       beta[:, qi:qi + 1], None, ALU.mult),
                 inc=VV, wait=(p.w(TT),))
            p.op("t", lambda t: t.transpose(pT[0:64, TS(1)], miscb[0:128, 0:64],
                                            cid[:, :]), inc=TT, wait=(p.w(VV),))
            p.op("v", lambda v, qi=qi: v.tensor_copy(vqT_h[0:64, TS(qi)], pT[0:64, TS(1)]),
                 inc=VV, wait=(p.w(TT),))
        p.barrier("tvs")
        for qc in range(QC5):
            npair = 4 * qc + 4
            exp_done = [p.val(SS), p.val(SS)]
            av_done = [p.val(TT), p.val(TT)]
            for ji in range(npair):
                off = max(0, 128 * ji - 512 * qc)
                nn = 512 - off
                slot = ji % 2
                p.op("t", lambda t, ji=ji, qc=qc, off=off, nn=nn, slot=slot: t.matmul(
                    pA[:, slot * 512 + off: slot * 512 + off + nn], kT_h[:, TS(ji)],
                    vqT_h[:, 512 * qc + off: 512 * qc + off + nn], start=True, stop=True),
                    inc=TT, wait=((SS, exp_done[slot]),))
                mv = None
                if ji >= 4 * qc:
                    p.op("v", lambda v, off=off, slot=slot: v.copy_predicated(
                        pA[:, slot * 512 + off: slot * 512 + off + 128], cml[:, :],
                        cneg[:, :]), inc=VV, wait=(p.w(TT),))
                    mv = p.val(VV)
                p.op("s", lambda s, off=off, nn=nn, slot=slot: s.activation(
                    e2t.ap().rearrange("p (a b) -> p a b", a=2)[:, slot, off:off + nn],
                    pA[:, slot * 512 + off: slot * 512 + off + nn], AF.Exp), inc=SS,
                    wait=(p.w(TT), (TT, av_done[slot])) + (((VV, mv),) if mv else ()))
                exp_done[slot] = p.val(SS)
                p.op("t", lambda t, ji=ji, off=off, nn=nn, npair=npair, slot=slot: t.matmul(
                    pB[0:65, off:off + nn], v_to[:, ji, 0:65],
                    e2t.ap().rearrange("p (a b) -> p a b", a=2)[:, slot, off:off + nn],
                    start=(ji == 0), stop=(ji == npair - 1)),
                    inc=TT, wait=(p.w(SS),))
                av_done[slot] = p.val(TT)
            ydao = p.val(YD)
            p.op("s", lambda s: s.activation(av_f[:, :], pB[0:65, 0:512], AF.Copy),
                 inc=SS, wait=(p.w(TT),))
            p.op("v", lambda v: v.nop(), wait=((YD, ydao),))
            for a in range(4):
                p.op("t", lambda t, a=a: t.transpose(pTf[:, TS(a)][:, 0:65],
                                                     av_f[:, TS(a)], cidf[0:65, 0:65]),
                     inc=TT, wait=(p.w(SS), p.w(VV)))
                tv = p.val(TT)
                p.op("v", lambda v, a=a: v.reciprocal(rz2[:, 0:1], pTf[:, TS(a)][:, 64:65]),
                     inc=VV, wait=((TT, tv),))
                p.op("v", lambda v, a=a: v.tensor_scalar(miscb[:, 0:64], pTf[:, TS(a)][:, 0:64],
                                                         rz2[:, 0:1], None, ALU.mult),
                     inc=VV, wait=(p.w(VV),))
                p.op("t", lambda t: t.transpose(pT[0:64, TS(2)], miscb[0:128, 0:64],
                                                cid[:, :]), inc=TT, wait=(p.w(VV),))
                p.op("v", lambda v, a=a: v.tensor_copy(
                    aost[0:64, TS(a)], pT[0:64, TS(2)]),
                    inc=VV, wait=(p.w(TT),))
            p.dma("y", ao_send[TS(hh, HD), qc * 512:(qc + 1) * 512], aost[0:64, :],
                  YD, wait=(p.w(VV),))
    if DEBUG_DUMPS:
        add0 = nc.dram_tensor("attdbg0", [64, T], F32, kind="ExternalOutput").ap()
        add1 = nc.dram_tensor("attdbg1", [64, T], F32, kind="ExternalOutput").ap()
        add2 = nc.dram_tensor("attdbg2", [128, 8 * QT], F32, kind="ExternalOutput").ap()
        add3 = nc.dram_tensor("attdbg3", [128, 8 * QT], F32, kind="ExternalOutput").ap()
        add4 = nc.dram_tensor("attdbg4", [128, QT], F32, kind="ExternalOutput").ap()
        add5 = nc.dram_tensor("attdbg5", [128, QT], F32, kind="ExternalOutput").ap()
        p.barrier("tvsgy")
        for blk in range(T // 2048):
            p.op("v", lambda v, blk=blk: v.tensor_copy(miscf[0:64, 0:2048],
                                                       qT_h[0:64, TS(blk, 2048)]), inc=VV)
            p.dma("y", add0[:, TS(blk, 2048)], miscf[0:64, 0:2048], YD, wait=(p.w(VV),))
            p.op("v", lambda v, blk=blk: v.tensor_copy(miscf[64:128, 0:2048],
                                                       kT_h[0:64, TS(blk, 2048)]), inc=VV)
            p.dma("y", add1[:, TS(blk, 2048)], miscf[64:128, 0:2048], YD, wait=(p.w(VV),))
            p.op("v", lambda v: v.nop(), inc=VV, wait=(p.w(YD),))
        p.dma("y", add2[:, :], z1p[:, :], YD)
        p.dma("y", add3[:, :], t1p[:, :], YD)
        p.dma("y", add4[:, :], statA[:, :], YD)
        p.dma("y", add5[:, :], beta[:, :], YD)
        p.op("y", lambda y: y.nop(), wait=(p.w(YD),))
        p.barrier("tvsgy")

    p.op("g", lambda g: g.nop(), wait=(p.w(YD),))
    p.barrier("tvsg")
    sc1.close()
    w_ao = p.coll(ao_send[:, :], ao_ag[:, :])

    if DEBUG_DUMPS:
        add6 = nc.dram_tensor("attdbg6", [4 * HD, T], F32, kind="ExternalOutput").ap()
        p.barrier("tvsgy")
        for blk in range(4 * HD // 128):
            for cc in range(T // 2048):
                lx = p.dma("g", miscb[:, 0:2048], ao_send[TS(blk), TS(cc, 2048)], LD,
                           wait=(p.w(VV),))
                p.op("v", lambda v: v.tensor_copy(miscf[:, 0:2048], miscb[:, 0:2048]),
                     inc=VV, wait=((LD, lx),))
                p.dma("y", add6[TS(blk), TS(cc, 2048)], miscf[:, 0:2048], YD,
                      wait=(p.w(VV),))
                p.op("v", lambda v: v.nop(), inc=VV, wait=(p.w(YD),))
        p.op("y", lambda y: y.nop(), wait=(p.w(YD),))
        p.barrier("tvsgy")

    sc1b = ExitStack()
    aoT = sbs(sc1b, [128, KD, Q], BF16)
    wop = sbs(sc1b, [128, KD, 512], BF16)
    p.dma("g", idxa[:, 0:8], aogidx, LD)
    lv = p.val(LD)
    gsem = LD
    for r in range(8):
        p.idma(aoT[:, r, :], idxa[:, r:r + 1],
               ao_ag.rearrange("r (m one) -> (r m) one", one=1)[:, :], gsem,
               wait=(w_ao, (LD, lv)), bound=16 * HD * T - 1)
    gv = p.val(gsem)
    for oc in range(2):
        lw = p.dma("g", wop[:, :, :],
                   woT[:, TS(oc, 512)].rearrange("(a p) m -> p a m", p=128), LD,
                   wait=(p.w(TT),))
        for i in range(NT):
            for kc in range(KD):
                p.op("t", lambda t, i=i, kc=kc: t.matmul(
                    pA[:, TS(i % 2, 512)], aoT[:, kc, TS(i)], wop[:, kc, :],
                    start=(kc == 0), stop=False),
                    wait=(((LD, lw), (gsem, gv), p.w(VV)) if kc == 0 else ()))
            p.op("t", lambda t, i=i, oc=oc: t.matmul(
                pA[:, TS(i % 2, 512)], cones[0:1, :], bo_sb[0:1, TS(oc, 512)],
                start=False, stop=True), inc=TT)
            p.op("v", lambda v, i=i, oc=oc: v.tensor_tensor(
                x_sb[:, i, TS(oc, 512)], x_sb[:, i, TS(oc, 512)], pA[:, TS(i % 2, 512)],
                ALU.add), inc=VV, wait=(p.w(TT),))
    p.barrier("tvsg")
    sc1b.close()

    if DEBUG_DUMPS:
        xd1 = nc.dram_tensor("xdump1", [Q, D], F32, kind="ExternalOutput").ap()
        p.barrier("tvsgy")
        for i in range(NT):
            p.dma("y", xd1[TS(i), :], x_sb[:, i, :], YD)
        p.op("y", lambda y: y.nop(), wait=(p.w(YD),))
        p.barrier("tvsgy")

    def emit_ffn():
        scf = ExitStack()
        oidx = sbs(scf, [128, NT * 8], I32, tag="oidx")
        rz2f = sbs(scf, [128, 4])
        mq8 = sbs(scf, [8, 2], U8)
        scfA = ExitStack()
        xlnT = sbs(scfA, [128, KD, Q], BF16)
        emit_ln(xlnT, send=True)
        probs = sbs(scfA, [128, NT, NE])
        gatecol = sbs(scfA, [128, NT, NE], BF16)
        prf = sbs(scfA, [32, Q], tag="prf")
        prfmy = sbs(scfA, [8, Q], tag="prfmy")
        bisv = sbs(scfA, [32, 16], tag="bisv")
        mfm = sbs(scfA, [32, Q], BF16, tag="mfm")
        mfm8 = sbs(scfA, [8, Q], BF16, tag="mfm8")
        mtok = sbs(scfA, [128, QT * 32], BF16, tag="mtok")
        ptok = sbs(scfA, [128, QT * 32], tag="ptok")
        mown = sbs(scfA, [128, NT * 8], BF16, tag="mown")
        pown = sbs(scfA, [128, NT * 8], tag="pown")
        rowz = sbs(scfA, [1, 2048], tag="rowz")
        vsc = sbs(scfA, [128, 2048])
        lw = p.dma("g", w_a[:, :, 0:8], choiceT.rearrange("(a p) m -> p a m", p=128), LD)
        for i in range(NT):
            for kc in range(KD):
                p.op("t", lambda t, i=i, kc=kc, ll=xlnT[:, kc, TS(i)]: t.matmul(
                    pTf[:, 0:8], ll, w_a[:, kc, 0:8],
                    start=(kc == 0), stop=(kc == KD - 1)),
                    inc=(TT if kc == KD - 1 else None),
                    wait=(((LD, lw), p.w(VV)) if kc == 0 else ()))
            p.op("v", lambda v: v.reduce_max(rz2f[:, 0:1], pTf[:, 0:8], axis=AX.X),
                 inc=VV, wait=(p.w(TT),))
            p.op("v", lambda v: v.tensor_scalar(rz2f[:, 1:2], rz2f[:, 0:1], -1.0, None, ALU.mult),
                 inc=VV, wait=(p.w(VV),))
            p.op("s", lambda s, i=i: s.activation(probs[:, i, :], pTf[:, 0:8], AF.Exp,
                                                  bias=rz2f[:, 1:2]), inc=SS, wait=(p.w(VV),))
            p.op("v", lambda v, i=i: v.reduce_sum(rz2f[:, 2:3], probs[:, i, :], axis=AX.X),
                 inc=VV, wait=(p.w(SS),))
            p.op("v", lambda v: v.reciprocal(rz2f[:, 3:4], rz2f[:, 2:3]), inc=VV,
                 wait=(p.w(VV),))
            p.op("v", lambda v, i=i: v.tensor_scalar(probs[:, i, :], probs[:, i, :],
                                                     rz2f[:, 3:4], None, ALU.mult), inc=VV,
                 wait=(p.w(VV),))
            p.op("v", lambda v, i=i: v.tensor_copy(gatecol[:, i, :], probs[:, i, :]), inc=VV)
        for i in range(NT):
            p.op("t", lambda t, i=i: t.transpose(pTf[0:8, 0:128], probs[:, i, :],
                                                 cidf[:, :]), inc=TT, wait=(p.w(VV),))
            p.op("v", lambda v, i=i: v.tensor_copy(prf[0:8, TS(i)], pTf[0:8, 0:128]),
                 inc=VV, wait=(p.w(TT),))
            p.dma("y", xl_send[TS(i), D:GR], gatecol[:, i, :], YD, wait=(p.w(VV),))
        p.dma("y", pr_send[:, :], prf[0:8, :], YD, wait=(p.w(VV),))
        p.op("g", lambda g: g.nop(), wait=(p.w(YD),))
        w_pr = p.coll(pr_send[:, :], pr_ag[:, :])
        w_xl = p.coll(xl_send[:, :], xl_ag[:, :])
        p.dma("g", idxa[0:32, 8:9], pgidx, LD)
        wgpr = p.idma(prf[0:32, :], idxa[0:32, 8:9], pr_ag[:, :], LD,
                      wait=(w_pr, p.w(LD), p.w(VV), p.w(TT)), bound=4 * NE - 1)
        for tc in range(NC5):
            p.op("t", lambda t, tc=tc: t.matmul(
                pTf[0:8, 0:512], selmy_sb[0:32, 0:8], prf[0:32, TS(tc, 512)],
                start=True, stop=True), inc=TT, wait=((LD, wgpr), p.w(VV)))
            p.op("v", lambda v, tc=tc: v.tensor_copy(prfmy[0:8, TS(tc, 512)], pTf[0:8, 0:512]),
                 inc=VV, wait=(p.w(TT),))
        p.op("v", lambda v: v.memset(bisv[0:8, 0:1], 0.0))
        p.op("v", lambda v: v.memset(bisv[0:8, 1:2], 1.0), inc=VV)
        for it in range(40):
            p.op("v", lambda v: v.tensor_tensor(bisv[0:8, 2:3], bisv[0:8, 0:1],
                                                bisv[0:8, 1:2], ALU.add))
            p.op("v", lambda v: v.tensor_scalar(bisv[0:8, 2:3], bisv[0:8, 2:3], 0.5,
                                                None, ALU.mult), inc=VV)
            p.op("t", lambda t: t.matmul(pTf[0:32, 0:1], eqb_sb[0:8, :], bisv[0:8, 2:3],
                                         start=True, stop=True), inc=TT, wait=(p.w(VV),))
            p.op("v", lambda v: v.tensor_copy(bisv[0:32, 3:4], pTf[0:32, 0:1]), inc=VV,
                 wait=(p.w(TT),))
            p.op("v", lambda v: v.tensor_scalar(miscf[0:32, 0:Q], prf[0:32, :],
                                                bisv[0:32, 3:4], 0.0, ALU.is_gt, ALU.add,
                                                accum_out=bisv[0:32, 4:5]), inc=VV,
                 wait=(p.w(VV),))
            p.op("t", lambda t: t.matmul(pTf[0:8, 1:2], eqs_sb[0:32, :], bisv[0:32, 4:5],
                                         start=True, stop=True), inc=TT, wait=(p.w(VV),))
            p.op("v", lambda v: v.tensor_scalar(mq8[0:8, 0:1], pTf[0:8, 1:2],
                                                float(CAP) - 0.5, None, ALU.is_ge),
                 inc=VV, wait=(p.w(TT),))
            p.op("v", lambda v: v.tensor_scalar(mq8[0:8, 1:2], pTf[0:8, 1:2],
                                                float(CAP) - 0.5, None, ALU.is_lt))
            p.op("v", lambda v: v.select(bisv[0:8, 0:1], mq8[0:8, 0:1], bisv[0:8, 2:3],
                                         bisv[0:8, 0:1]))
            p.op("v", lambda v: v.select(bisv[0:8, 1:2], mq8[0:8, 1:2], bisv[0:8, 2:3],
                                         bisv[0:8, 1:2]), inc=VV)
        p.op("t", lambda t: t.matmul(pTf[0:32, 0:1], eqb_sb[0:8, :], bisv[0:8, 0:1],
                                     start=True, stop=True), inc=TT, wait=(p.w(VV),))
        p.op("v", lambda v: v.tensor_copy(bisv[0:32, 3:4], pTf[0:32, 0:1]), inc=VV,
             wait=(p.w(TT),))
        p.op("v", lambda v: v.tensor_scalar(mfm[0:32, :], prf[0:32, :], bisv[0:32, 3:4],
                                            0.0, ALU.is_gt, ALU.add,
                                            accum_out=bisv[0:32, 4:5]), inc=VV,
             wait=(p.w(VV),))
        p.op("t", lambda t: t.matmul(pTf[0:32, 2:3], eqt_sb[0:32, :], bisv[0:32, 4:5],
                                     start=True, stop=True), inc=TT, wait=(p.w(VV),))
        p.op("v", lambda v: v.tensor_copy(bisv[0:32, 5:6], pTf[0:32, 2:3]), inc=VV,
             wait=(p.w(TT),))
        p.op("t", lambda t: t.matmul(pTf[0:8, 3:4], selmy_sb[0:32, :], bisv[0:32, 3:4],
                                     start=True, stop=True), wait=(p.w(VV),))
        p.op("t", lambda t: t.matmul(pTf[0:8, 4:5], selmy_sb[0:32, :], bisv[0:32, 5:6],
                                     start=True, stop=True), inc=TT)
        p.op("v", lambda v: v.tensor_copy(bisv[0:8, 6:7], pTf[0:8, 3:4]), wait=(p.w(TT),))
        p.op("v", lambda v: v.tensor_copy(bisv[0:8, 7:8], pTf[0:8, 4:5]), inc=VV)
        p.op("v", lambda v: v.tensor_scalar(mfm8[0:8, :], prfmy[0:8, :], bisv[0:8, 6:7],
                                            0.0, ALU.is_gt, ALU.add,
                                            accum_out=bisv[0:8, 8:9]), inc=VV,
             wait=(p.w(VV),))
        p.dma("g", rowa[0:1, 0:32], bisv[0:32, 5:6], LD, wait=(p.w(VV),))
        p.dma("g", rowb[0:1, 0:8], bisv[0:8, 7:8], LD)
        lqoff = p.val(LD)

        for tb in range(QT):
            p.op("t", lambda t, tb=tb: t.transpose(pT[:, TS(tb % 8)][:, 0:32],
                                                   mfm[0:32, TS(tb % NT)],
                                                   cid[0:32, 0:32]), inc=TT,
                 wait=(p.w(VV),))
            p.op("v", lambda v, tb=tb: v.tensor_copy(mtok[:, TS(tb, 32)],
                                                     pT[:, TS(tb % 8)][:, 0:32]),
                 inc=VV, wait=(p.w(TT),))
        for mc in range(QT * 32 // 512):
            p.op("t", lambda t, mc=mc: t.matmul(pTf[:, 0:512], cltri[:, :],
                                                mtok[:, TS(mc, 512)], start=True,
                                                stop=True), inc=TT, wait=(p.w(VV),))
            p.op("v", lambda v, mc=mc: v.tensor_copy(ptok[:, TS(mc, 512)], pTf[:, 0:512]),
                 inc=VV, wait=(p.w(TT),))
        p.dma("g", rowz[0:1, 0:QT * 32], ptok[127:128, :], LD, wait=(p.w(VV),))
        lbs = p.val(LD)
        p.op("v", lambda v: v.memset(rowa[0:1, 32:64], 0.0), wait=((LD, lbs),))
        p.op("v", lambda v: v.tensor_copy(miscf[0:1, 0:QT * 32], rowz[0:1, 0:QT * 32]))
        rz3 = rowz.ap().rearrange("p (a b) -> p a b", b=32)
        nqb = QT // 4
        for j in range(2):
            for q in range(4):
                sl = rz3[0:1, q * nqb:(q + 1) * nqb, 4 * j + q:4 * j + q + 1]
                p.op("v", lambda v, sl=sl: v.tensor_copy(rowa[0:1, 48:48 + nqb], sl))
                p.op("v", lambda v: v.tensor_tensor_scan(
                    rowa[0:1, 48:48 + nqb], rowa[0:1, 48:48 + nqb],
                    rowa[0:1, 32:32 + nqb], 0.0, ALU.add, ALU.add), inc=VV)
                p.op("v", lambda v, sl=sl: v.tensor_copy(sl, rowa[0:1, 48:48 + nqb]),
                     wait=(p.w(VV),))
        p.op("v", lambda v: v.tensor_sub(rowz[0:1, 0:QT * 32], rowz[0:1, 0:QT * 32],
                                         miscf[0:1, 0:QT * 32]), inc=VV)
        p.op("v", lambda v: v.nop(), wait=((LD, lqoff),))
        for tb in range(QT):
            p.op("v", lambda v, tb=tb: v.tensor_tensor(rowz[0:1, TS(tb, 32)],
                                                       rowz[0:1, TS(tb, 32)],
                                                       rowa[0:1, 0:32], ALU.add))
        p.op("v", lambda v: v.tensor_scalar(rowz[0:1, 0:QT * 32], rowz[0:1, 0:QT * 32],
                                            -1.0, None, ALU.add), inc=VV)
        for mc in range(QT * 32 // 512):
            p.op("t", lambda t, mc=mc: t.matmul(pTf[:, 0:512], conesf[0:1, :],
                                                rowz[0:1, TS(mc, 512)],
                                                start=True, stop=True), inc=TT,
                 wait=(p.w(VV),))
            p.op("v", lambda v, mc=mc: v.tensor_tensor(ptok[:, TS(mc, 512)],
                                                       ptok[:, TS(mc, 512)],
                                                       pTf[:, 0:512], ALU.add), inc=VV,
                 wait=(p.w(TT),))
        CHW = min(2048, QT * 32)
        for mc in range(QT * 32 // CHW):
            sl = TS(mc, CHW)
            p.op("v", lambda v, sl=sl: v.tensor_scalar(miscf[:, 0:CHW], mtok[:, sl],
                                                       -BIG, None, ALU.mult))
            p.op("v", lambda v, sl=sl: v.tensor_scalar(miscf[:, 0:CHW], miscf[:, 0:CHW],
                                                       BIG, None, ALU.add))
            p.op("v", lambda v, sl=sl: v.tensor_scalar(vsc[:, 0:CHW], ptok[:, sl],
                                                       float(CAP) - 0.5, None, ALU.is_ge))
            p.op("v", lambda v, sl=sl: v.scalar_tensor_tensor(out=miscf[:, 0:CHW],
                                                              in0=vsc[:, 0:CHW],
                                                              scalar=BIG, in1=miscf[:, 0:CHW],
                                                              op0=ALU.mult, op1=ALU.add))
            p.op("v", lambda v, sl=sl: v.tensor_tensor(ptok[:, sl], ptok[:, sl],
                                                       miscf[:, 0:CHW], ALU.add), inc=VV)
        for i in range(NT):
            p.op("t", lambda t, i=i: t.transpose(pT[:, TS(i % 8)][:, 0:8],
                                                 mfm8[0:8, TS(i)],
                                                 cid[0:8, 0:8]), inc=TT, wait=(p.w(VV),))
            p.op("v", lambda v, i=i: v.tensor_copy(mown[:, TS(i, 8)],
                                                   pT[:, TS(i % 8)][:, 0:8]), inc=VV,
                 wait=(p.w(TT),))
        p.op("t", lambda t: t.matmul(pTf[:, 0:NT * 8], cltri[:, :], mown[:, :],
                                     start=True, stop=True), inc=TT, wait=(p.w(VV),))
        p.op("v", lambda v: v.tensor_copy(pown[:, :], pTf[:, 0:NT * 8]), inc=VV,
             wait=(p.w(TT),))
        p.dma("g", rowz[0:1, 0:NT * 8], pown[127:128, :], LD, wait=(p.w(VV),))
        lbs = p.val(LD)
        p.op("v", lambda v: v.tensor_copy(miscf[0:1, 0:NT * 8], rowz[0:1, 0:NT * 8]),
             wait=((LD, lbs),))
        rz8 = rowz.ap().rearrange("p (a b) -> p a b", b=8)
        for j in range(8):
            sl = rz8[0:1, 0:NT, j:j + 1]
            p.op("v", lambda v, sl=sl: v.tensor_copy(rowa[0:1, 48:48 + NT], sl))
            p.op("v", lambda v: v.tensor_tensor_scan(
                rowa[0:1, 48:48 + NT], rowa[0:1, 48:48 + NT],
                rowa[0:1, 32:32 + NT], 0.0, ALU.add, ALU.add), inc=VV)
            p.op("v", lambda v, sl=sl: v.tensor_copy(sl, rowa[0:1, 48:48 + NT]),
                 wait=(p.w(VV),))
        p.op("v", lambda v: v.tensor_sub(rowz[0:1, 0:NT * 8], rowz[0:1, 0:NT * 8],
                                         miscf[0:1, 0:NT * 8]))
        p.op("v", lambda v: v.tensor_scalar(rowb[0:1, 0:8], rowb[0:1, 0:8], -1.0,
                                            None, ALU.add), inc=VV)
        for i in range(NT):
            p.op("v", lambda v, i=i: v.tensor_tensor(rowz[0:1, TS(i, 8)],
                                                     rowz[0:1, TS(i, 8)],
                                                     rowb[0:1, 0:8], ALU.add))
        p.op("v", lambda v: v.nop(), inc=VV)
        p.op("t", lambda t: t.matmul(pTf[:, 0:NT * 8], conesf[0:1, :], rowz[0:1, 0:NT * 8],
                                     start=True, stop=True), inc=TT, wait=(p.w(VV),))
        p.op("v", lambda v: v.tensor_tensor(pown[:, :], pown[:, :], pTf[:, 0:NT * 8],
                                            ALU.add), inc=VV, wait=(p.w(TT),))
        p.op("v", lambda v: v.tensor_scalar(miscf[:, 0:NT * 8], mown[:, :], -BIG, None,
                                            ALU.mult))
        p.op("v", lambda v: v.tensor_scalar(miscf[:, 0:NT * 8], miscf[:, 0:NT * 8], BIG,
                                            None, ALU.add))
        p.op("v", lambda v: v.tensor_tensor(pown[:, :], pown[:, :], miscf[:, 0:NT * 8],
                                            ALU.add))
        p.op("v", lambda v: v.tensor_scalar(miscf[:, 0:NT * 8], pown[:, :],
                                            float(CAP) - 0.5, BIG, ALU.is_ge,
                                            ALU.mult))
        p.op("v", lambda v: v.tensor_tensor(pown[:, :], pown[:, :], miscf[:, 0:NT * 8],
                                            ALU.add), inc=VV)
        for i in range(NT):
            p.op("v", lambda v, i=i: v.tensor_copy(rowz[0:1, TS(i, 8)], ebase_sb[0:1, :]),
                 wait=((p.w(TT),) if i == 0 else ()))
        p.op("v", lambda v: v.nop(), inc=VV)
        p.op("t", lambda t: t.matmul(pTf[:, 0:NT * 8], conesf[0:1, :], rowz[0:1, 0:NT * 8],
                                     start=True, stop=True), inc=TT, wait=(p.w(VV),))
        p.op("v", lambda v: v.tensor_tensor(pown[:, :], pown[:, :], pTf[:, 0:NT * 8],
                                            ALU.add), wait=(p.w(TT),))
        p.op("v", lambda v: v.tensor_copy(oidx[:, :], pown[:, :]), inc=VV)

        ssc = LD
        vv_idx = p.val(VV)
        xstage = sbs(scfA, [128, 2, GR], BF16)
        idxs = sbs(scfA, [128, 2 * QT], I32, tag="idxs")
        stg_free = [p.val(LD), p.val(LD)]
        for tb in range(QT):
            sl = tb % 2
            p.dma("g", xstage[:, sl, :], xl_ag[TS(tb), :], LD,
                  wait=(w_xl, (LD, stg_free[sl])))
            lvs = p.val(LD)
            for le in range(2):
                q = tb // NT
                eqcol = tb * 32 + 4 * le + q
                ic = tb * 2 + le
                p.op("v", lambda v, eqcol=eqcol, le=le: v.tensor_scalar(
                    miscf[:, 0:1], ptok[:, eqcol:eqcol + 1], float(le * CAP), None,
                    ALU.add), wait=((VV, vv_idx),))
                p.op("v", lambda v, ic=ic: v.tensor_copy(idxs[:, ic:ic + 1],
                                                         miscf[:, 0:1]), inc=VV)
                p.idma(comp[:, :], idxs[:, ic:ic + 1], xstage[:, sl, :], ssc,
                       wait=((LD, lvs), p.w(VV)), scatter=True, bound=2 * CAP - 1)
            stg_free[sl] = p.val(LD)
        wsc = p.val(ssc)

        p.barrier("tvsg")
        scfA.close()
        sce = ExitStack()
        HCT = CT // 2
        xcT = sbs(sce, [128, KD, CAP // 2], BF16)
        hT = sbs(sce, [128, FF // 128, CAP // 2], BF16)
        wbig = sbs(sce, [128, 16384], BF16)
        wb1 = wbig.ap().rearrange("p (a m) -> p a m", a=KD)
        wb2 = wbig.ap().rearrange("p (a m) -> p a m", a=16)
        for le in range(2):
            for half in range(2):
                base = le * CAP + half * (CAP // 2)
                for ci in range(HCT):
                    p.dma("g", miscb[:, 0:D], comp[base + 128 * ci: base + 128 * (ci + 1), 0:D],
                          LD, wait=((ssc, wsc), p.w(TT)))
                    p.dma("g", miscb[:, D:D + NE],
                          comp[base + 128 * ci: base + 128 * (ci + 1), D:GR], LD)
                    lvv = p.val(LD)
                    p.op("v", lambda v, le=le: v.tensor_tensor(
                        miscf[:, 0:NE], miscb[:, D:D + NE], mygate_sb[:, le, :], ALU.mult),
                        wait=((LD, lvv),))
                    p.op("v", lambda v, ci=ci: v.reduce_sum(miscf[:, 100 + ci:101 + ci],
                                                            miscf[:, 0:NE], axis=AX.X), inc=VV)
                    svx = p.val(VV)
                    for kc in range(KD):
                        p.op("t", lambda t, kc=kc: t.transpose(
                            pT[:, TS(kc % 8)], miscb[:, TS(kc)], cid[:, :]), inc=TT,
                            wait=(((VV, svx),) if kc == 0 else ()))
                    tfx = p.w(TT)
                    for kc in range(KD):
                        p.op("v", lambda v, kc=kc, ci=ci: v.tensor_copy(
                            xcT[:, kc, TS(ci)], pT[:, TS(kc % 8)]), inc=VV,
                            wait=((tfx,) if kc == 0 else ()))
                lw1 = p.dma("g", wb1, w1my[le].rearrange("(a p) m -> p a m", p=128), LD,
                            wait=(p.w(TT),))
                for fc in range(FF // 128):
                    for cc in range((CAP // 2) // 512):
                        for kc in range(KD):
                            p.op("t", lambda t, fc=fc, cc=cc, kc=kc: t.matmul(
                                pA[:, TS(cc % 2, 512)], wb1[:, kc, TS(fc)],
                                xcT[:, kc, TS(cc, 512)], start=(kc == 0), stop=(kc == KD - 1)),
                                inc=(TT if kc == KD - 1 else None),
                                wait=(((LD, lw1), p.w(SS), p.w(VV)) if kc == 0 else ()))
                        p.op("s", lambda s, fc=fc, cc=cc: s.activation(
                            hT[:, fc, TS(cc, 512)], pA[:, TS(cc % 2, 512)], AF.Silu),
                            inc=SS, wait=(p.w(TT),))
                lw2 = p.dma("g", wb2, w2my[le].rearrange("(a p) m -> p a m", p=128), LD,
                            wait=(p.w(TT), p.w(SS)))
                for ci in range(HCT):
                    ydp = p.val(YD)
                    for oc in range(2):
                        for kc in range(16):
                            p.op("t", lambda t, ci=ci, oc=oc, kc=kc: t.matmul(
                                pA[:, TS(oc % 2, 512)], hT[:, kc, TS(ci)],
                                wb2[:, kc, TS(oc, 512)], start=(kc == 0), stop=(kc == 15)),
                                inc=(TT if kc == 15 else None),
                                wait=(((LD, lw2), p.w(VV)) if kc == 0 else ()))
                        p.op("v", lambda v, ci=ci, oc=oc, ydp=ydp: v.tensor_scalar(
                            miscf[:, 1024 + oc * 512: 1024 + (oc + 1) * 512],
                            pA[:, TS(oc % 2, 512)], miscf[:, 100 + ci:101 + ci], None,
                            ALU.mult), inc=VV, wait=(p.w(TT), (YD, ydp)))
                    p.dma("y", y_send[base + 128 * ci: base + 128 * (ci + 1), :],
                          miscf[:, 1024:2048], YD, wait=(p.w(VV),))
        p.op("g", lambda g: g.nop(), wait=(p.w(YD),))
        sce.close()
        w_y = p.coll(y_send[:, :], y_ag[:, :])
        gadd = LD
        for i in range(NT):
            for j in range(8):
                col = i * 8 + j
                ww = ((w_y, p.w(VV), p.w(TT)) if (i == 0 and j == 0)
                      else ((gadd, p.val(gadd)),))
                p.idma(x_sb[:, i, :], oidx[:, col:col + 1], y_ag[:, :], gadd,
                       wait=ww, add=True, bound=8 * CAP - 1)
        p.op("g", lambda g: g.nop(), wait=((gadd, p.val(gadd)),))
        p.barrier("tvsg")
        scf.close()

    emit_ffn()

    if DEBUG_DUMPS:
        xd2 = nc.dram_tensor("xdump2", [Q, D], F32, kind="ExternalOutput").ap()
        p.barrier("tvsgy")
        for i in range(NT):
            p.dma("y", xd2[TS(i), :], x_sb[:, i, :], YD)
        p.op("y", lambda y: y.nop(), wait=(p.w(YD),))
        p.barrier("tvsgy")

    sc3w = ExitStack()
    wdt_sb = sbs(sc3w, [65, DSL], BF16)
    sc3 = ExitStack()
    xlnT3 = sbs(sc3, [128, KD, Q], BF16)
    emit_ln(xlnT3)
    xbq = sbs(sc3, [128, 2048])
    cpad = sbs(sc3, [128, 2051])
    xs_acc = sbs(sc3, [33, Q])
    xpw_sb = sbs(sc3, [128, 16, 33], BF16)
    halo_sb = sbs(sc3, [128, 16, 4])
    for oc in range(2 * DI // 128):
        buf = w_a if oc % 2 == 0 else w_b
        ydprev = p.val(YD)
        lw = p.dma("g", buf[:, :, :],
                   inpw[:, TS(oc)].rearrange("(a p) m -> p a m", p=128), LD,
                   wait=((TT, wbuf_free[oc % 2]),))
        for tc in range(NC5):
            for kc in range(KD):
                p.op("t", lambda t, buf=buf, kc=kc, tc=tc,
                     rr=xlnT3[:, kc, TS(tc, 512)]: t.matmul(
                    pA[:, TS(tc % 2, 512)], buf[:, kc, :], rr,
                    start=(kc == 0), stop=(kc == KD - 1)),
                    inc=(TT if kc == KD - 1 else None),
                    wait=(((LD, lw), p.w(SS)) if kc == 0 else ()))
            p.op("s", lambda s, tc=tc, ydprev=ydprev: s.activation(
                miscf[:, TS(tc, 512)], pA[:, TS(tc % 2, 512)], AF.Copy),
                inc=SS, wait=(p.w(TT),) + (((YD, ydprev),) if tc == 0 else ()))
        wbuf_free[oc % 2] = p.val(TT)
        sv = p.val(SS)
        if oc < DI // 128:
            p.dma("y", xbpre_sp[TS(oc), :], miscf[:, 0:Q], YD, wait=((SS, sv),))
            p.dma("y", halo_send[TS(oc), 0:4], miscf[:, Q - 4:Q], YD)
        else:
            p.dma("y", z_spill[TS(oc - DI // 128), :], miscf[:, 0:Q], YD, wait=((SS, sv),))
    p.op("g", lambda g: g.nop(), wait=(p.w(YD),))
    w_halo = p.coll(halo_send[:, :], halo_ag[:, :])
    p.op("v", lambda v: v.memset(halo_sb[:, :, :], 0.0), inc=VV)
    p.dma("g", idxa[:, 0:16], halog, LD)
    p.idma(halo_sb[:, 0, :], idxa[:, 0:1], halo_ag[:, :], LD,
           wait=(w_halo, p.w(LD), p.w(VV)), bound=4 * DI - 1)
    for dtile in range(1, DI // 128):
        p.idma(halo_sb[:, dtile, :], idxa[:, dtile:dtile + 1], halo_ag[:, :],
               LD, bound=4 * DI - 1)
    whalo = p.val(LD)
    p.dma("g", smallw[:, 0:64], convw, LD)
    p.dma("g", smallw[:, 64:80], convb, LD)
    p.dma("g", xpw_sb[:, :, :], xpw.rearrange("(a p) m -> p a m", p=128), LD)
    p.dma("g", wdt_sb[64:65, :], wdtT, LD)
    lvw = p.val(LD)
    p.op("v", lambda v: v.memset(xs_acc[0:33, :], 0.0), inc=VV)
    for oc in range(DI // 128):
        ydprev = p.val(YD)
        lx = p.dma("g", cpad[:, 3:3 + Q], xbpre_sp[TS(oc), :], LD,
                   wait=(((LD, max(whalo, lvw)),) + (p.w(SS), p.w(VV))))
        p.op("v", lambda v, oc=oc: v.tensor_copy(cpad[:, 0:3], halo_sb[:, oc, 1:4]),
             wait=((LD, lx),))
        p.op("v", lambda v, oc=oc, ydprev=ydprev: v.tensor_scalar(
            xbq[:, 0:Q], cpad[:, 0:Q], smallw[:, 0 + oc:1 + oc], None, ALU.mult),
            wait=((YD, ydprev),))
        for k in range(1, 4):
            p.op("v", lambda v, oc=oc, k=k: v.scalar_tensor_tensor(
                out=xbq[:, 0:Q], in0=cpad[:, k:k + Q],
                scalar=smallw[:, 16 * k + oc:16 * k + oc + 1],
                in1=xbq[:, 0:Q], op0=ALU.mult, op1=ALU.add))
        p.op("v", lambda v: v.nop(), inc=VV)
        p.op("s", lambda s, oc=oc: s.activation(xbq[:, 0:Q], xbq[:, 0:Q], AF.Silu,
                                                bias=smallw[:, 64 + oc:65 + oc]),
             inc=SS, wait=(p.w(VV),))
        sv = p.val(SS)
        p.dma("y", xb_send[TS(oc), :], xbq[:, 0:Q], YD, wait=((SS, sv),))
        p.op("v", lambda v: v.tensor_copy(miscb[:, 0:Q], xbq[:, 0:Q]), inc=VV,
             wait=((SS, sv), p.w(TT)))
        for tc in range(NC5):
            p.op("t", lambda t, oc=oc, tc=tc: t.matmul(
                pTf[0:33, 0:512], xpw_sb[:, oc, :], miscb[:, TS(tc, 512)],
                start=True, stop=True), inc=TT, wait=(p.w(VV),))
            p.op("v", lambda v, tc=tc: v.tensor_tensor(xs_acc[0:33, TS(tc, 512)],
                                                       xs_acc[0:33, TS(tc, 512)],
                                                       pTf[0:33, 0:512], ALU.add),
                 inc=VV, wait=(p.w(TT),))
    p.dma("y", xs_send[:, :], xs_acc[0:33, :], YD, wait=(p.w(VV),))
    p.op("g", lambda g: g.nop(), wait=(p.w(YD),))
    p.barrier("tvsgy")
    sc3.close()
    w_xs = p.coll(xs_send[:, :], xs_ag[:, :])
    w_xb = p.coll(xb_send[:, :], xb_ag[:, :])

    sc3b = ExitStack()
    selp = sbs(sc3b, [128, 16, 128], BF16)
    selyp = sbs(sc3b, [128, 16, 128], BF16)
    p.dma("g", selp[:, :, :], sel8.rearrange("p (a c) -> p a c", a=16), LD)
    p.dma("g", selyp[:, :, :], selY.rearrange("p (a c) -> p a c", a=16), LD)
    bc16 = sbs(sc3b, [65, T], BF16)
    brepc = sbs(sc3b, [128, 2048], BF16)
    crepc = sbs(sc3b, [128, 2048], BF16)
    dtb16 = sbs(sc3b, [128, 2048], BF16)
    ub16 = sbs(sc3b, [128, 2048], BF16)
    xbsl = sbs(sc3b, [128, 2048])
    dtf = sbs(sc3b, [128, 512])
    a_sc = sbs(sc3b, [128, 1024])
    bx_sc = sbs(sc3b, [128, 1024])
    h_sc = sbs(sc3b, [128, 1024])
    yt_sc = sbs(sc3b, [128, 1024], BF16)
    hstate = sbs(sc3b, [128, NPAN])
    apan = sbs(sc3b, [128, NPAN])
    ysl = sbs(sc3b, [128, 2048])

    p.dma("g", idxa[0:16, 0:12], bcg, LD)
    lv = p.val(LD)
    scbcf = ExitStack()
    bcfq = sbs(scbcf, [65, 2048])
    for q in range(4):
        lq = p.idma(bcfq[0:16, 0:Q], idxa[0:16, q:q + 1], xs_ag[:, :], LD,
                    wait=(w_xs, (LD, lv), p.w(VV)), bound=4 * 33 - 1)
        p.idma(bcfq[32:48, 0:Q], idxa[0:16, 4 + q:5 + q], xs_ag[:, :], LD,
               bound=4 * 33 - 1)
        wq = p.val(LD)
        p.op("v", lambda v, q=q: v.tensor_copy(bc16[0:16, TS(q, Q)], bcfq[0:16, 0:Q]),
             wait=((LD, wq),))
        p.op("v", lambda v, q=q: v.tensor_copy(bc16[32:48, TS(q, Q)], bcfq[32:48, 0:Q]),
             inc=VV)
    p.idma(miscf[0:4, 0:Q], idxa[0:4, 8:9], xs_ag[:, :], LD, bound=4 * 33 - 1)
    wd4 = p.val(LD)
    p.dma("g", bc16[64:65, :].rearrange("p (a b) -> p a b", a=4),
          miscf[0:4, 0:Q], LD, wait=((LD, wd4),))
    wbc = p.val(LD)
    p.op("v", lambda v: v.nop(), inc=VV, wait=((LD, wbc),))
    p.barrier("tvsg")
    scbcf.close()
    p.dma("g", misc2[:, 0:NPAN], alog, LD)
    p.dma("g", smallw[:, 0:NDT], bdt, LD)
    lv = p.val(LD)
    p.op("s", lambda s: s.activation(apan[:, :], misc2[:, 0:NPAN], AF.Exp),
         inc=SS, wait=((LD, lv), p.w(VV)))
    p.op("v", lambda v: v.tensor_scalar(apan[:, :], apan[:, :], -1.0, None, ALU.mult),
         inc=VV, wait=(p.w(SS),))
    p.op("v", lambda v: v.memset(hstate[:, :], 0.0), inc=VV)
    p.dma("g", idxa[:, 0:16], xbgidx, LD)
    lvxb = p.val(LD)

    NTH = max(1, Q // 1024)
    for dtile in range(NDT):
        for tq in range(4):
            t0q = tq * Q
            wgx = p.idma(xbsl[:, 0:2048], idxa[:, dtile * 4 + tq: dtile * 4 + tq + 1],
                         xb_ag[:, :], LD, wait=(w_xb, (LD, lvxb), p.w(VV)),
                         bound=4 * DI - 1)
            for c2 in range(Q // 512):
                p.op("t", lambda t, t0q=t0q, c2=c2: t.matmul(
                    pB[:, 0:512], s16[0:16, :], bc16[0:16, t0q + 512 * c2: t0q + 512 * (c2 + 1)],
                    start=True, stop=True), wait=(p.w(VV), p.w(SS)))
                p.op("t", lambda t, t0q=t0q, c2=c2: t.matmul(
                    pB[:, 512:1024], s16[32:48, :], bc16[32:48, t0q + 512 * c2: t0q + 512 * (c2 + 1)],
                    start=True, stop=True), inc=TT)
                p.op("v", lambda v, c2=c2: v.tensor_copy(brepc[:, TS(c2, 512)], pB[:, 0:512]),
                     wait=(p.w(TT),))
                p.op("v", lambda v, c2=c2: v.tensor_copy(crepc[:, TS(c2, 512)], pB[:, 512:1024]),
                     inc=VV)
            for tcc in range(Q // 512):
                tg = t0q + tcc * 512
                p.op("t", lambda t, dtile=dtile, tg=tg: t.matmul(
                    pTf[:, 0:512], wdt_sb[64:65, TS(dtile)],
                    bc16[64:65, tg:tg + 512], start=True, stop=True),
                    inc=TT, wait=(p.w(SS), p.w(VV)))
                p.op("v", lambda v, dtile=dtile: v.tensor_scalar(
                    dtf[:, 0:512], pTf[:, 0:512], smallw[:, dtile:dtile + 1], None,
                    ALU.add), inc=VV, wait=(p.w(TT), p.w(VV)))
                p.op("s", lambda s: s.activation(miscf[:, 0:512], dtf[:, 0:512], AF.Abs),
                     wait=(p.w(VV),))
                p.op("s", lambda s: s.activation(miscf[:, 0:512], miscf[:, 0:512],
                                                 AF.Exp, scale=-1.0))
                p.op("s", lambda s: s.activation(miscf[:, 0:512], miscf[:, 0:512],
                                                 AF.Ln, bias=ceps[:, 1:2]), inc=SS)
                p.op("v", lambda v: v.tensor_relu(misc2[:, 0:512], dtf[:, 0:512]),
                     wait=(p.w(SS),))
                p.op("v", lambda v: v.tensor_add(dtf[:, 0:512], misc2[:, 0:512],
                                                 miscf[:, 0:512]))
                p.op("v", lambda v, tcc=tcc: v.tensor_copy(
                    dtb16[:, TS(tcc, 512)], dtf[:, 0:512]))
                p.op("v", lambda v, tcc=tcc, wgx=wgx: v.tensor_tensor(
                    ub16[:, TS(tcc, 512)], dtf[:, 0:512],
                    xbsl[:, TS(tcc, 512)], ALU.mult), inc=VV,
                    wait=((LD, wgx),))
            for j in range(16):
                pan = dtile * 16 + j
                for th in range(NTH):
                    tl = th * 1024
                    for k2 in range(2):
                        p.op("t", lambda t, j=j, tl=tl, k2=k2: t.matmul(
                            pB[:, TS(k2, 512)], selp[:, j, :],
                            dtb16[:, tl + 512 * k2: tl + 512 * (k2 + 1)],
                            start=True, stop=True),
                            inc=(TT if k2 == 1 else None),
                            wait=((p.w(SS), p.w(VV)) if k2 == 0 else ()))
                    p.op("s", lambda s, pan=pan: s.activation(
                        a_sc[:, 0:1024], pB[:, 0:1024], AF.Exp,
                        scale=apan[:, pan:pan + 1]), inc=SS, wait=(p.w(TT),))
                    for k2 in range(2):
                        p.op("t", lambda t, j=j, tl=tl, k2=k2: t.matmul(
                            pB[:, TS(k2, 512)], selp[:, j, :],
                            ub16[:, tl + 512 * k2: tl + 512 * (k2 + 1)],
                            start=True, stop=True),
                            inc=(TT if k2 == 1 else None),
                            wait=((p.w(SS),) if k2 == 0 else ()))
                    p.op("v", lambda v, tl=tl: v.tensor_tensor(
                        bx_sc[:, 0:1024], pB[:, 0:1024], brepc[:, tl:tl + 1024],
                        ALU.mult), inc=VV, wait=(p.w(TT),))
                    p.op("v", lambda v, pan=pan: v.tensor_tensor_scan(
                        h_sc[:, 0:1024], a_sc[:, 0:1024], bx_sc[:, 0:1024],
                        hstate[:, pan:pan + 1], ALU.mult, ALU.add), inc=VV,
                        wait=(p.w(SS),))
                    p.op("v", lambda v, pan=pan: v.tensor_copy(hstate[:, pan:pan + 1],
                                                               h_sc[:, 1023:1024]),
                         wait=(p.w(VV),))
                    p.op("v", lambda v, tl=tl: v.tensor_tensor(
                        yt_sc[:, 0:1024], h_sc[:, 0:1024], crepc[:, tl:tl + 1024],
                        ALU.mult), inc=VV)
                    for k2 in range(2):
                        p.op("t", lambda t, j=j, th=th, k2=k2: t.matmul(
                            pA[:, th * 1024 + 512 * k2: th * 1024 + 512 * (k2 + 1)],
                            selyp[:, j, :], yt_sc[:, TS(k2, 512)],
                            start=(j == 0), stop=(j == 15)),
                            inc=(TT if (j == 15 and k2 == 1) else None),
                            wait=((p.w(VV), p.w(SS)) if k2 == 0 else ()))
            ydp2 = p.val(YD)
            p.op("s", lambda s, ydp2=ydp2: s.activation(ysl[:, 0:Q], pA[:, 0:Q], AF.Copy),
                 inc=SS, wait=(p.w(TT), (YD, ydp2)))
            sv = p.val(SS)
            p.dma("y", ys_send[TS(dtile), t0q:t0q + Q], ysl[:, 0:Q],
                  YD, wait=((SS, sv),))
    p.op("g", lambda g: g.nop(), wait=(p.w(YD),))
    p.barrier("tvsgy")
    sc3b.close()
    sc3w.close()
    w_ys = p.coll(ys_send[:, :], ys_ag[:, :])

    sc3c = ExitStack()
    yprime = sbs(sc3c, [128, DI // 128, Q], BF16)
    ysl2 = sbs(sc3c, [128, 2048])
    zt = sbs(sc3c, [128, 2048])
    wcp = sbs(sc3c, [128, 16, 1024], BF16)
    p.dma("g", idxa[:, 0:16], ygidx, LD)
    p.dma("g", smallw[:, 0:16], dpar, LD)
    lv = p.val(LD)
    flatys = ys_ag.rearrange("r (m one) -> (r m) one", one=1)
    for dtile in range(DI // 128):
        gy = LD
        wys = p.idma(ysl2[:, 0:Q], idxa[:, dtile:dtile + 1], flatys[:, :], gy,
                     wait=(w_ys, (LD, lv), p.w(VV)), bound=4 * DSL * T - 1)
        lz = p.dma("g", zt[:, 0:Q], z_spill[TS(dtile), :], LD, wait=(p.w(SS), p.w(VV)))
        lx2 = p.dma("g", miscf[:, 0:Q], xb_send[TS(dtile), :], LD)
        p.op("s", lambda s: s.activation(zt[:, 0:Q], zt[:, 0:Q], AF.Silu),
             inc=SS, wait=((LD, p.val(LD)),))
        p.op("v", lambda v, dtile=dtile: v.scalar_tensor_tensor(
            out=ysl2[:, 0:Q], in0=miscf[:, 0:Q], scalar=smallw[:, dtile:dtile + 1],
            in1=ysl2[:, 0:Q], op0=ALU.mult, op1=ALU.add),
            wait=((LD, wys), p.w(SS)))
        p.op("v", lambda v, dtile=dtile: v.tensor_tensor(
            yprime[:, dtile, :], ysl2[:, 0:Q], zt[:, 0:Q], ALU.mult), inc=VV)
    lw = p.dma("g", wcp[:, :, :], cpw.rearrange("(a p) m -> p a m", p=128), LD,
               wait=(p.w(TT),))
    for i in range(NT):
        for oc in range(2):
            for kc in range(16):
                p.op("t", lambda t, i=i, oc=oc, kc=kc: t.matmul(
                    pA[:, TS(oc % 2, 512)], yprime[:, kc, TS(i)],
                    wcp[:, kc, TS(oc, 512)], start=(kc == 0), stop=(kc == 15)),
                    inc=(TT if kc == 15 else None),
                    wait=(((LD, lw), p.w(VV)) if kc == 0 else ()))
            p.op("v", lambda v, i=i, oc=oc: v.tensor_tensor(
                x_sb[:, i, TS(oc, 512)], x_sb[:, i, TS(oc, 512)],
                pA[:, TS(oc % 2, 512)], ALU.add), inc=VV, wait=(p.w(TT),))
    p.barrier("tvsg")
    sc3c.close()

    if DEBUG_DUMPS:
        xd3 = nc.dram_tensor("xdump3", [Q, D], F32, kind="ExternalOutput").ap()
        p.barrier("tvsgy")
        for i in range(NT):
            p.dma("y", xd3[TS(i), :], x_sb[:, i, :], YD)
        p.op("y", lambda y: y.nop(), wait=(p.w(YD),))
        p.barrier("tvsgy")

    emit_ffn()

    p.barrier("tvsgy")
    for i in range(NT):
        p.dma("y", xout[TS(i), :], x_sb[:, i, :], YD)
    p.op("y", lambda y: y.nop(), wait=(p.w(YD),))

    with es:
        with nc.Block() as block:
            p.emit(block)
    return nc



def _host_inputs(T, inputs):
    Q = T // 4
    NT = Q // 128
    QT = T // 128
    CAP = T * 2 // NE
    DSL = DI // 4
    NPAN = DSL * DS // 128
    NDT = DSL // 128
    NROW = 4 * 3 * H * HD

    x = np.asarray(inputs["L0_attn_a_x"], np.float32)
    f32 = lambda a: np.ascontiguousarray(np.asarray(a, np.float32))
    wq = f32(inputs["L0_attn_a_Wq"]); wk = f32(inputs["L0_attn_a_Wk"])
    wv = f32(inputs["L0_attn_a_Wv"]); wo = f32(inputs["L0_attn_a_Wo"])
    bo = f32(inputs["L0_attn_a_bo"]).reshape(1, D)
    choice = f32(inputs["L0_ffn_choice_w"])
    w1 = f32(inputs["L0_ffn_w1"]); w2 = f32(inputs["L0_ffn_w2"])
    inpw = f32(inputs["L0_attn_b_in_proj_w"])
    convw = f32(inputs["L0_attn_b_conv_w"]).reshape(DI, 4)
    convb = f32(inputs["L0_attn_b_conv_b"])
    xpw = f32(inputs["L0_attn_b_x_proj_w"])
    dtw = f32(inputs["L0_attn_b_dt_proj_w"]).reshape(DI)
    dtb = f32(inputs["L0_attn_b_dt_proj_b"])
    alog = f32(inputs["L0_attn_b_A_log"])
    dpar = f32(inputs["L0_attn_b_D_param"])
    cpw = f32(inputs["L0_attn_b_c_proj_w"])

    ndt16 = DI // 128
    convw_l = np.zeros((128, 4 * ndt16), np.float32)
    convb_l = np.zeros((128, ndt16), np.float32)
    dpar_l = np.zeros((128, ndt16), np.float32)
    for dt in range(ndt16):
        for k in range(4):
            convw_l[:, k * ndt16 + dt] = convw[TS(dt), k]
        convb_l[:, dt] = convb[TS(dt)]
        dpar_l[:, dt] = dpar[TS(dt)]

    identb = np.eye(128, dtype=BF)
    identf = np.eye(128, dtype=np.float32)
    onesb = np.ones((128, 128), BF)
    r = np.arange(128)
    ltrib = (r[:, None] <= r[None, :]).astype(BF)
    masku = (r[None, :] > r[:, None]).astype(np.uint8)
    maskl = (r[None, :] < r[:, None]).astype(np.uint8)
    s16 = np.zeros((48, 128), BF)
    for p_ in range(128):
        s16[p_ % 16, p_] = 1
        s16[32 + p_ % 16, p_] = 1
    sel8 = np.zeros((128, 16 * 128), BF)
    selY = np.zeros((128, 16 * 128), BF)
    for j in range(16):
        for p_ in range(128):
            sel8[p_ % 8 * 0 + (j * 8 + p_ // 16) if False else (j * 8 + p_ // 16), j * 128 + p_] = 1
            selY[p_, j * 128 + (j * 8 + p_ // 16)] = 1
    sel8 = np.zeros((128, 16 * 128), BF)
    for j in range(16):
        for p_ in range(128):
            sel8[j * 8 + p_ // 16, j * 128 + p_] = 1
    eqsB = np.zeros((32, 8), np.float32)
    eqbc = np.zeros((8, 32), np.float32)
    eqtri = np.zeros((32, 32), np.float32)
    for e in range(8):
        for q in range(4):
            eqsB[e * 4 + q, e] = 1
            eqbc[e, e * 4 + q] = 1
            for q2 in range(q):
                eqtri[e * 4 + q2, e * 4 + q] = 1

    in_maps = []
    for core in range(8):
        b = core // 4
        myq = core % 4
        xq = np.ascontiguousarray(x[b, myq * Q:(myq + 1) * Q, :])
        qkvg = np.zeros((64, 48), np.int32)
        for hh in range(4):
            hglob = myq * 4 + hh
            for s in range(3):
                for q in range(4):
                    qkvg[:, hh * 12 + s * 4 + q] = (q * 3 * H * HD + s * H * HD
                                                    + hglob * HD + np.arange(64))
        aog = np.zeros((128, 8), np.int32)
        for rblk in range(8):
            aog[:, rblk] = (rblk * 128 + np.arange(128)) * T + myq * Q
        eorder = [2 * myq, 2 * myq + 1] + [e for e in range(8) if e // 2 != myq]
        pg = np.zeros((32, 1), np.int32)
        for j in range(8):
            for q in range(4):
                pg[j * 4 + q, 0] = q * NE + eorder[j]
        selmy = np.zeros((32, 8), np.float32)
        for j in range(8):
            selmy[j * 4 + myq, j] = 1
        ebase = np.zeros((1, 8), np.float32)
        for j in range(8):
            e = eorder[j]
            src, le = e // 2, e % 2
            ebase[0, j] = (src * 2 + le) * CAP
        mygate = np.zeros((2, 128, NE), np.float32)
        for le in range(2):
            mygate[le, :, 2 * myq + le] = 1.0
        ds0 = myq * DSL
        wdtT = dtw[ds0:ds0 + DSL].reshape(1, DSL)
        bdt_l = np.zeros((128, NDT), np.float32)
        alog_l = np.zeros((128, NPAN), np.float32)
        for dt in range(NDT):
            bdt_l[:, dt] = dtb[ds0 + dt * 128: ds0 + (dt + 1) * 128]
            for j in range(16):
                for p_ in range(128):
                    alog_l[p_, dt * 16 + j] = alog[ds0 + dt * 128 + j * 8 + p_ // 16,
                                                   p_ % 16]
        bcg = np.zeros((16, 12), np.int32)
        for q in range(4):
            bcg[:, q] = q * 33 + np.arange(16)
            bcg[:, 4 + q] = q * 33 + 16 + np.arange(16)
            bcg[q, 8] = q * 33 + 32
        xbg = np.zeros((128, 16), np.int32)
        for dt in range(NDT):
            for q in range(4):
                xbg[:, dt * 4 + q] = q * DI + ds0 + dt * 128 + np.arange(128)
        yg = np.zeros((128, 16), np.int32)
        for dt in range(DI // 128):
            yg[:, dt] = (dt * 128 + np.arange(128)) * T + myq * Q
        halog_ = np.full((128, 16), 1 << 28, np.int32)
        if myq > 0:
            for dt in range(DI // 128):
                halog_[:, dt] = (myq - 1) * DI + dt * 128 + np.arange(128)
        w1my = np.stack([w1[2 * myq], w1[2 * myq + 1]])
        w2my = np.stack([w2[2 * myq], w2[2 * myq + 1]])

        in_maps.append({
            "xin": xq,
            "wqT": np.ascontiguousarray(wq.T), "wkT": np.ascontiguousarray(wk.T),
            "wvT": np.ascontiguousarray(wv.T), "woT": np.ascontiguousarray(wo.T),
            "bo": bo, "choiceT": np.ascontiguousarray(choice.T),
            "w1my": w1my, "w2my": w2my,
            "inpw": inpw, "convw": convw_l, "convb": convb_l, "xpw": xpw,
            "wdtT": wdtT, "bdt": bdt_l, "alog": alog_l, "dpar": dpar_l,
            "cpw": cpw,
            "identb": identb, "identf": identf, "onesb": onesb, "ltrib": ltrib,
            "masku": masku, "maskl": maskl, "sel16": s16, "sel8": sel8,
            "selY": selY, "eqsB": eqsB, "eqbc": eqbc, "eqtri": eqtri,
            "selmy": selmy, "mygate": mygate,
            "qkvgidx": qkvg, "aogidx": aog, "pgidx": pg, "bcg": bcg,
            "xbgidx": xbg, "ygidx": yg, "halog": halog_, "ebase": ebase,
        })
    return in_maps


def kernel(**inputs):
    T = 8192
    nc = build_program(T)
    in_maps = _host_inputs(T, inputs)
    res = run_bass_kernel_spmd(nc, in_maps, core_ids=list(range(8)))
    Q = T // 4
    out = np.zeros((2, T, D), np.float32)
    for core in range(8):
        b, myq = core // 4, core % 4
        out[b, myq * Q:(myq + 1) * Q, :] = res.results[core]["xout"]
    return out
